# version 05
# baseline (speedup 1.0000x reference)
import os
os.environ.setdefault("GOMEMLIMIT", "20GiB")
os.environ.setdefault("GOGC", "30")

import numpy as np
import ml_dtypes
from contextlib import ExitStack

import concourse.bass as bass
import concourse.mybir as mybir
from concourse.bass_utils import run_bass_kernel_spmd

F32 = mybir.dt.float32
BF16 = mybir.dt.bfloat16
I32 = mybir.dt.int32
I8 = mybir.dt.int8
AF = mybir.ActivationFunctionType
ALU = mybir.AluOpType
BF = ml_dtypes.bfloat16

NTOK = 4096
D = 2048
DT = D // 128
T = 512
NT = NTOK // T
EPS_LN = 1e-5
EPS_RMS = float(np.finfo(np.float32).eps)
THR = float(np.float32(0.5 + 2.0 ** -10))
ENGINES = ("sync", "vector", "scalar", "tensor", "gpsimd")


def pret(wT):
    K, M = wT.shape
    KC, MT = K // 128, M // 128
    return np.ascontiguousarray(
        wT.reshape(KC, 128, MT, 128).transpose(2, 1, 0, 3).reshape(MT, 128, KC * 128))


def tile_stream(x):
    return np.ascontiguousarray(x.reshape(NTOK, DT, 128).transpose(1, 2, 0))


def untile_stream(s):
    return np.ascontiguousarray(np.asarray(s).transpose(2, 0, 1).reshape(NTOK, D))


CF_W = 520
CB_W = 264


def make_consts():
    cf = np.zeros((128, CF_W), np.float32)
    cb = np.zeros((128, CB_W), BF)
    cf[:, 0] = 1.0
    cf[0, 1:129] = 1.0
    cf[:, 129:257] = np.eye(128, dtype=np.float32)
    cf[:, 257:385] = np.tril(np.ones((128, 128), np.float32))
    cf[:, 385:513] = np.triu(np.ones((128, 128), np.float32))
    cb[:64, 0] = np.ones(64, BF)
    cb[64:, 1] = np.ones(64, BF)
    cb[0, 2:66] = np.ones(64, BF)
    cb[1, 66:130] = np.ones(64, BF)
    cb[:, 130:258] = np.eye(128, dtype=np.float32).astype(BF)
    return cf, cb


class Sem:
    def __init__(self, nc, stack, name):
        self.h = stack.enter_context(nc.semaphore(name))
        self.n = 0

    def inc(self, instr, k=1):
        instr.then_inc(self.h, k)
        self.n += k
        return self.n


class DS:

    def __init__(self, nc, stack, name, depth=2):
        self.sems = [Sem(nc, stack, f"{name}{i}") for i in range(depth)]
        self.depth = depth
        self.i = 0

    def issue(self, instr, k=16):
        s = self.sems[self.i % self.depth]
        self.i += 1
        v = s.inc(instr, k)
        return (s.h, v)

    def wait(self, eng, tok):
        eng.wait_ge(tok[0], tok[1])

    def wait_all(self, eng):
        for s in self.sems:
            eng.wait_ge(s.h, s.n)


FENCES = True

_NOFENCE = {"wait_ge", "sem_inc", "nop", "engine_nop", "drain", "wait_op",
            "dma_start", "dma_start_transpose", "indirect_dma_start", "collective_compute"}


class OrderedEng:

    def __init__(self, eng, sem):
        self._eng = eng
        self._sem = sem
        self._pending = None

    def __getattr__(self, name):
        eng = self._eng
        meth = getattr(eng, name)
        if name in _NOFENCE or name.startswith("reg") or not callable(meth):
            return meth

        def call(*a, **k):
            if FENCES and self._pending is not None:
                eng.wait_ge(self._sem.h, self._pending)
            self._pending = None
            instr = meth(*a, **k)
            self._sem.inc(instr, 1)
            self._pending = self._sem.n
            return instr
        return call


class KB:
    def __init__(self, nc, stack):
        self.nc = nc
        self.stack = stack
        self.bar = Sem(nc, stack, "bar")
        self.nbar = 0
        self.s = {n: Sem(nc, stack, n) for n in
                  ("ld0", "ld1", "ld2", "st0", "st1", "mm0", "mm1",
                   "ev0", "ev1", "ev2", "ev3", "gp0")}
        self.c = {}
        self.uid = 0
        self.ord = {n: Sem(nc, stack, f"ord_{n}") for n in ("vector", "scalar", "gpsimd")}
        self.v = OrderedEng(nc.vector, self.ord["vector"])
        self.sc = OrderedEng(nc.scalar, self.ord["scalar"])
        self.g = OrderedEng(nc.gpsimd, self.ord["gpsimd"])
        self._ds = {}

    def ds(self, name, depth=2):
        if name not in self._ds:
            self._ds[name] = DS(self.nc, self.stack, name, depth)
        return self._ds[name]

    def pump(self, n=1):
        g = getattr(self, "bg_gen", None)
        if g is None:
            return
        for _ in range(n):
            try:
                next(g)
            except StopIteration:
                self.bg_gen = None
                return

    def pump_all(self):
        while getattr(self, "bg_gen", None) is not None:
            self.pump(64)

    def sb(self, scope, name, shape, dtype):
        self.uid += 1
        return scope.enter_context(self.nc.sbuf_tensor(f"{name}_{self.uid}", shape, dtype))

    def ps(self, scope, name, shape):
        self.uid += 1
        return scope.enter_context(self.nc.psum_tensor(f"{name}_{self.uid}", shape, F32))

    def barrier(self):
        self.nbar += 1
        nc = self.nc
        wrapped = [("vector", self.v), ("scalar", self.sc)]
        if not getattr(self, "gpsimd_bg", False):
            wrapped.append(("gpsimd", self.g))
        tgt = self.bar.n + len(wrapped) + 2
        for name, w in wrapped:
            eng = getattr(nc, name)
            o = self.ord[name]
            if w._pending is not None:
                eng.wait_ge(o.h, w._pending)
                w._pending = None
            eng.wait_ge(self.bar.h, self.bar.n)
            eng.sem_inc(self.bar.h, 1)
        nc.tensor.wait_ge(self.s["mm0"].h, self.s["mm0"].n)
        nc.tensor.wait_ge(self.s["mm1"].h, self.s["mm1"].n)
        nc.tensor.wait_ge(self.bar.h, self.bar.n)
        nc.tensor.sem_inc(self.bar.h, 1)
        nc.sync.wait_ge(self.bar.h, self.bar.n)
        nc.sync.sem_inc(self.bar.h, 1)
        for name, _ in wrapped:
            getattr(nc, name).wait_ge(self.bar.h, tgt)
        nc.tensor.wait_ge(self.bar.h, tgt)
        nc.sync.wait_ge(self.bar.h, tgt)
        self.bar.n = tgt


def emit_consts(kb, cf_ap, cb_ap, lut_ap=None):
    nc, s = kb.nc, kb.s
    st = kb.stack
    vo, so = kb.ord["vector"], kb.ord["scalar"]
    cfs = st.enter_context(nc.sbuf_tensor("c_f32", [128, CF_W], F32))
    cbs = st.enter_context(nc.sbuf_tensor("c_bf", [128, CB_W], BF16))
    s["ld0"].inc(nc.sync.dma_start(out=cfs[:], in_=cf_ap[:, :]), 16)
    s["ld0"].inc(nc.sync.dma_start(out=cbs[:], in_=cb_ap[:, :]), 16)
    kb.c = {
        "ones_col": cfs[:, 0:1], "ones_row": cfs[0:1, 1:129],
        "id_f32": cfs[:, 129:257], "tril": cfs[:, 257:385], "triu": cfs[:, 385:513],
        "bd": cbs[:, 0:2], "bdT": cbs[0:2, 2:130], "id_bf": cbs[:, 130:258],
    }
    if lut_ap is not None:
        luts = st.enter_context(nc.sbuf_tensor("c_lut", [128, 48], F32))
        lutb = st.enter_context(nc.sbuf_tensor("c_lutb", [128, 16], BF16))
        psL_scope = ExitStack()
        psL = psL_scope.enter_context(nc.psum_tensor("c_psL", [128, 16], F32))
        v = s["ld0"].inc(nc.sync.dma_start(out=luts[0:1, 32:48], in_=lut_ap[:, :]), 16)
        nc.tensor.wait_ge(s["ld0"].h, s["ld0"].n)
        im = nc.tensor.matmul(psL[:], kb.c["ones_row"], luts[0:1, 32:48], start=True, stop=True)
        s["mm0"].inc(im, 1)
        kb.v.wait_ge(s["mm0"].h, s["mm0"].n)
        kb.v.tensor_copy(luts[:, 32:48], psL[:])
        kb.v.tensor_copy(lutb[:], luts[:, 32:48])
        kb.v.tensor_copy(luts[:, 0:16], lutb[:])
        kb.v.tensor_tensor(luts[:, 32:48], luts[:, 32:48], luts[:, 0:16], ALU.subtract)
        kb.v.tensor_copy(lutb[:], luts[:, 32:48])
        kb.v.tensor_copy(luts[:, 16:32], lutb[:])
        kb.c["lut"] = luts
        kb.v.wait_ge(s["mm0"].h, s["mm0"].n)
        psL_scope.close()
    else:
        kb.v.wait_ge(s["ld0"].h, s["ld0"].n)
    kb.barrier()


def emit_quant_phase(kb, w_f32_ap, wq_ap, n_blocks, blk_w):
    nc, s, c = kb.nc, kb.s, kb.c
    NS = blk_w // 512
    vo, so = kb.ord["vector"], kb.ord["scalar"]
    with ExitStack() as local:
        wf = kb.sb(local, "q_wf", [128, 2 * blk_w], F32)
        wb = kb.sb(local, "q_wb", [128, blk_w], BF16)
        ab = kb.sb(local, "q_ab", [128, blk_w], BF16)
        sgn = kb.sb(local, "q_sgn", [128, blk_w], F32)
        wq = kb.sb(local, "q_wq", [128, 2 * blk_w], BF16)
        sb = kb.sb(local, "q_sb", [128, 512], F32)
        thr = kb.sb(local, "q_thr", [128, 512], F32)
        tmp = kb.sb(local, "q_tmp", [128, 512], F32)
        tbf = kb.sb(local, "q_tbf", [128, 512], BF16)
        scr = kb.sb(local, "q_scr", [2, 512], BF16)
        ps1 = kb.ps(local, "q_ps1", [2, 512])
        ps2 = kb.ps(local, "q_ps2", [128, 512])

        mm = s["mm0"]
        wld = kb.ds("q_wld")
        wst = kb.ds("q_wst")
        ld_done, v_done, st_done, s_done = [], [], [], []
        for j in range(n_blocks):
            buf = j % 2
            wfb = wf[:, buf * blk_w:(buf + 1) * blk_w]
            wqb = wq[:, buf * blk_w:(buf + 1) * blk_w]
            if j >= 2:
                nc.sync.wait_ge(vo.h, v_done[j - 2])
            ld_done.append(wld.issue(nc.sync.dma_start(out=wfb, in_=w_f32_ap[j])))
            wld.wait(kb.v, ld_done[j])
            if j >= 1:
                kb.v.wait_ge(so.h, s_done[j - 1])
            if j >= 2:
                wst.wait(kb.v, st_done[j - 2])
            kb.v.tensor_copy(wb[:], wfb)
            kb.v.tensor_copy(wfb, wb[:])
            wbf32_done = vo.n
            kb.sc.wait_ge(vo.h, wbf32_done)
            kb.sc.activation(ab[:], wb[:], AF.Abs)
            kb.sc.activation(sgn[:], wb[:], AF.Sign)
            s_done.append(so.n)
            for sl in range(NS):
                ssl = slice(sl * 512, (sl + 1) * 512)
                nc.tensor.wait_ge(so.h, s_done[j])
                nc.tensor.wait_ge(vo.h, vo.n)
                im = nc.tensor.matmul(ps1[:], c["bd"], ab[:, ssl], start=True, stop=True)
                mm.inc(im, 1)
                kb.v.wait_ge(mm.h, mm.n)
                kb.v.tensor_scalar(scr[:], ps1[:], 1.0 / 64, 1e-8, ALU.mult, ALU.max)
                nc.tensor.wait_ge(vo.h, vo.n)
                im2 = nc.tensor.matmul(ps2[:], c["bdT"], scr[:], start=True, stop=True)
                mm.inc(im2, 1)
                kb.v.wait_ge(mm.h, mm.n)
                kb.v.tensor_copy(sb[:], ps2[:])
                kb.v.tensor_scalar(thr[:], sb[:], THR, None, ALU.mult)
                kb.v.tensor_tensor(tmp[:], ab[:, ssl], thr[:], ALU.is_gt)
                kb.v.tensor_tensor(tmp[:], tmp[:], sgn[:, ssl], ALU.mult)
                kb.v.tensor_tensor(tmp[:], tmp[:], sb[:], ALU.mult)
                kb.v.tensor_tensor(tmp[:], tmp[:], wfb[:, ssl], ALU.subtract)
                kb.v.tensor_copy(tbf[:], tmp[:])
                kb.v.tensor_tensor(wqb[:, ssl], wfb[:, ssl], tbf[:], ALU.add)
            v_done.append(vo.n)
            if j >= 2:
                wst.wait(kb.sc, st_done[j - 2])
            kb.sc.wait_ge(vo.h, v_done[j])
            st_done.append(wst.issue(kb.sc.dma_start(out=wq_ap[j], in_=wqb)))
        wst.wait_all(kb.sc)
    kb.barrier()


def make_ghost_bufs(kb, scope, max_w=2752):
    return {
        "idx": kb.sb(scope, "g_idx", [128, max_w], I8),
        "eq": kb.sb(scope, "g_eq", [128, max_w], BF16),
        "hi": kb.sb(scope, "g_hi", [128, max_w], BF16),
        "lo": kb.sb(scope, "g_lo", [128, max_w], BF16),
    }


def ghost_gen(kb, calls, bufs):
    nc, s, c = kb.nc, kb.s, kb.c
    lut = c["lut"]
    vo = kb.ord["vector"]
    idx, eq, hi, lo = bufs["idx"], bufs["eq"], bufs["hi"], bufs["lo"]
    gld = kb.ds("gbg_ld")
    gsh = kb.ds("gbg_sth")
    gsl = kb.ds("gbg_stl")
    prev_v_done = None
    prev_st = None
    for idx_ap, whi_ap, wlo_ap, n_blocks, blk_w in calls:
        for j in range(n_blocks):
            idxb = idx[:, 0:blk_w]
            hib = hi[:, 0:blk_w]
            lob = lo[:, 0:blk_w]
            if prev_v_done is not None:
                nc.sync.wait_ge(vo.h, prev_v_done)
            ld = gld.issue(nc.sync.dma_start(out=idxb, in_=idx_ap[j]))
            gld.wait(kb.v, ld)
            if prev_st is not None:
                gsh.wait(kb.v, prev_st[0])
                gsl.wait(kb.v, prev_st[1])
            kb.v.memset(hib, 0.0)
            kb.v.memset(lob, 0.0)
            yield
            for v in range(16):
                kb.v.tensor_scalar(eq[:, 0:blk_w], idxb, float(v), None, ALU.is_equal)
                kb.v.scalar_tensor_tensor(hib, eq[:, 0:blk_w], lut[:, v:v + 1], hib,
                                          ALU.mult, ALU.add)
                kb.v.scalar_tensor_tensor(lob, eq[:, 0:blk_w], lut[:, 16 + v:17 + v], lob,
                                          ALU.mult, ALU.add)
                yield
            prev_v_done = vo.n
            kb.sc.wait_ge(vo.h, prev_v_done)
            t1 = gsh.issue(kb.sc.dma_start(out=whi_ap[j], in_=hib))
            t2 = gsl.issue(kb.sc.dma_start(out=wlo_ap[j], in_=lob))
            prev_st = (t1, t2)
            yield


def emit_ln(kb, x_sb, sq, hhi, hlo, psA, psB, psC, brow, bfull, n_d=DT, tw=T):
    nc, s, c = kb.nc, kb.s, kb.c
    vo, so = kb.ord["vector"], kb.ord["scalar"]
    dinv = 1.0 / (n_d * 128)
    mm = s["mm0"]
    mm_done = []
    for d in range(n_d):
        xd = x_sb[:, d * tw:(d + 1) * tw]
        if d >= 1:
            kb.sc.wait_ge(mm.h, mm_done[d - 1])
        kb.sc.activation(sq[:], xd, AF.Square)
        sq_done = so.n
        im = nc.tensor.matmul(psA[:], c["ones_col"], xd,
                              start=(d == 0), stop=(d == n_d - 1))
        mm.inc(im, 1)
        nc.tensor.wait_ge(so.h, sq_done)
        im2 = nc.tensor.matmul(psB[:], c["ones_col"], sq[:],
                               start=(d == 0), stop=(d == n_d - 1))
        mm_done.append(mm.inc(im2, 1))
    row_m = brow[0:1, 0:tw]
    row_r = brow[0:1, tw:2 * tw]
    kb.v.wait_ge(mm.h, mm.n)
    kb.v.tensor_scalar(row_m, psA[:], dinv, None, ALU.mult)
    kb.v.tensor_scalar(row_r, psB[:], dinv, None, ALU.mult)
    kb.v.tensor_tensor(psB[:], row_m, row_m, ALU.mult)
    kb.v.tensor_tensor(row_r, row_r, psB[:], ALU.subtract)
    kb.v.tensor_scalar(row_r, row_r, EPS_LN, None, ALU.add)
    kb.sc.wait_ge(vo.h, vo.n)
    kb.sc.activation(row_r, row_r, AF.Sqrt)
    kb.v.wait_ge(so.h, so.n)
    kb.v.reciprocal(row_r, row_r)
    nc.tensor.wait_ge(vo.h, vo.n)
    nc.tensor.matmul(psC[:, 0:tw], c["ones_row"], row_m, start=True, stop=True)
    ib2 = nc.tensor.matmul(psC[:, tw:2 * tw], c["ones_row"], row_r, start=True, stop=True)
    mm.inc(ib2, 2)
    kb.v.wait_ge(mm.h, mm.n)
    kb.v.tensor_copy(bfull[:], psC[:, 0:2 * tw])
    for d in range(n_d):
        xd = x_sb[:, d * tw:(d + 1) * tw]
        hd = hhi[:, d * tw:(d + 1) * tw]
        kb.v.tensor_tensor(xd, xd, bfull[:, 0:tw], ALU.subtract)
        kb.v.tensor_tensor(xd, xd, bfull[:, tw:2 * tw], ALU.mult)
        kb.v.tensor_copy(hd, xd)
        kb.v.tensor_tensor(xd, xd, hd, ALU.subtract)
        kb.v.tensor_copy(hlo[:, d * tw:(d + 1) * tw], xd)
    return kb.ord["vector"].n


def emit_ffn0(kb, s_in, s_out, qgu_ap, qproj_ap, fh_hi_ap, fh_lo_ap):
    nc, s, c = kb.nc, kb.s, kb.c
    GW = 2 * 16 * 128
    vo, so = kb.ord["vector"], kb.ord["scalar"]
    mm = s["mm0"]
    xld = kb.ds("f_xld", 1)
    wld = kb.ds("f_wld")
    sph = kb.ds("f_sph")
    spl = kb.ds("f_spl")
    rbh = kb.ds("f_rbh", 1)
    rbl = kb.ds("f_rbl", 1)
    wpld = kb.ds("f_wpld")
    xdld = kb.ds("f_xdld")
    outst = kb.ds("f_outst")
    slscope = ExitStack()
    kb.r2 = kb.sb(slscope, "f_r2", [128, T], F32)
    for t in range(NT):
        tok = slice(t * T, (t + 1) * T)
        ab_scope = ExitStack()
        kb.hhi = kb.sb(ab_scope, "f_hhi", [128, DT * T], BF16)
        kb.hlo = kb.sb(ab_scope, "f_hlo", [128, DT * T], BF16)
        with ExitStack() as lo_:
            x_sb = kb.sb(lo_, "fA_x", [128, (DT + 1) * T], F32)
            brow = kb.sb(lo_, "fA_brow", [1, 2 * T], F32)
            bfull = kb.sb(lo_, "fA_bfull", [128, 2 * T], F32)
            psA = kb.ps(lo_, "fA_psA", [1, T])
            psB = kb.ps(lo_, "fA_psB", [1, T])
            psC = kb.ps(lo_, "fA_psC", [128, 2 * T])
            v_ld = xld.issue(nc.sync.dma_start(
                out=x_sb[:, 0:DT * T].rearrange("p (d t) -> p d t", d=DT),
                in_=s_in[:, :, tok].rearrange("d p t -> p d t")))
            xld.wait(kb.v, v_ld)
            xld.wait(kb.sc, v_ld)
            xld.wait(nc.tensor, v_ld)
            emit_ln(kb, x_sb, x_sb[:, DT * T:], kb.hhi, kb.hlo, psA, psB, psC, brow, bfull)
            kb.barrier()
        with ExitStack() as lo_:
            wq = kb.sb(lo_, "fB_wq", [128, 2 * GW], BF16)
            hf = kb.sb(lo_, "fB_hf", [128, 2 * T], F32)
            ht = kb.sb(lo_, "fB_ht", [128, 2 * T], F32)
            sqh = kb.sb(lo_, "fB_sqh", [128, 2 * T], F32)
            fhh = kb.sb(lo_, "fB_fhh", [128, 2 * T], BF16)
            fhl = kb.sb(lo_, "fB_fhl", [128, 2 * T], BF16)
            rrow = kb.sb(lo_, "fB_rrow", [1, T], F32)
            psG = kb.ps(lo_, "fB_psG", [128, 2 * T])
            psU = kb.ps(lo_, "fB_psU", [128, 2 * T])
            psH = kb.ps(lo_, "fB_psH", [1, T])
            wq_ld, mm_done, silu_done, hm_done, cast_done, sq_done, smm_done, st_done = \
                [], [], [], [], [], [], [], []
            hhi, hlo = kb.hhi, kb.hlo
            for j in range(64):
                buf = j % 2
                wqb = wq[:, buf * GW:(buf + 1) * GW]
                tsl = slice(buf * T, (buf + 1) * T)
                if j >= 2:
                    nc.sync.wait_ge(mm.h, mm_done[j - 2])
                wq_ld.append(wld.issue(nc.sync.dma_start(out=wqb, in_=qgu_ap[j])))
                wld.wait(nc.tensor, wq_ld[j])
                if j >= 2:
                    nc.tensor.wait_ge(vo.h, hm_done[j - 2])
                    nc.tensor.wait_ge(so.h, silu_done[j - 2])
                for g, ps in ((0, psG), (1, psU)):
                    for p_, hsrc in ((0, hhi), (1, hlo)):
                        for kc in range(16):
                            im = nc.tensor.matmul(
                                ps[:, tsl],
                                wqb[:, g * 16 * 128 + kc * 128:g * 16 * 128 + (kc + 1) * 128],
                                hsrc[:, kc * T:(kc + 1) * T],
                                start=(p_ == 0 and kc == 0), stop=(p_ == 1 and kc == 15))
                mm_done.append(mm.inc(im, 1))
                if j >= 1:
                    jm = j - 1
                    nc.tensor.wait_ge(so.h, sq_done[jm])
                    ism = nc.tensor.matmul(psH[:], c["ones_col"],
                                           sqh[:, jm % 2 * T:(jm % 2 + 1) * T],
                                           start=(jm == 0), stop=False)
                    smm_done.append(mm.inc(ism, 1))
                kb.sc.wait_ge(mm.h, mm_done[j])
                if j >= 2:
                    kb.sc.wait_ge(vo.h, cast_done[j - 2])
                kb.sc.activation(hf[:, tsl], psG[:, tsl], AF.Silu)
                silu_done.append(so.n)
                kb.v.wait_ge(so.h, silu_done[j])
                if j >= 2:
                    kb.v.wait_ge(so.h, sq_done[j - 2])
                    sph.wait(kb.v, st_done[j - 2][0])
                    spl.wait(kb.v, st_done[j - 2][1])
                kb.v.tensor_tensor(ht[:, tsl], hf[:, tsl], psU[:, tsl], ALU.mult)
                hm_done.append(vo.n)
                kb.v.tensor_copy(fhh[:, tsl], ht[:, tsl])
                kb.v.tensor_tensor(hf[:, tsl], ht[:, tsl], fhh[:, tsl], ALU.subtract)
                kb.v.tensor_copy(fhl[:, tsl], hf[:, tsl])
                cast_done.append(vo.n)
                kb.sc.wait_ge(vo.h, hm_done[j])
                if j >= 2:
                    kb.sc.wait_ge(mm.h, smm_done[j - 1])
                kb.sc.activation(sqh[:, tsl], ht[:, tsl], AF.Square)
                sq_done.append(so.n)
                kb.sc.wait_ge(vo.h, cast_done[j])
                t1 = sph.issue(kb.sc.dma_start(out=fh_hi_ap[j], in_=fhh[:, tsl]))
                t2 = spl.issue(kb.sc.dma_start(out=fh_lo_ap[j], in_=fhl[:, tsl]))
                st_done.append((t1, t2))
                kb.pump(2)
            nc.tensor.wait_ge(so.h, sq_done[63])
            ism = nc.tensor.matmul(psH[:], c["ones_col"], sqh[:, T:2 * T],
                                   start=False, stop=True)
            mm.inc(ism, 1)
            kb.v.wait_ge(mm.h, mm.n)
            kb.v.tensor_scalar(rrow[:], psH[:], 1.0 / 8192, EPS_RMS, ALU.mult, ALU.add)
            kb.sc.wait_ge(vo.h, vo.n)
            kb.sc.activation(rrow[:], rrow[:], AF.Sqrt)
            kb.v.wait_ge(so.h, so.n)
            kb.v.reciprocal(rrow[:], rrow[:])
            nc.tensor.wait_ge(vo.h, vo.n)
            ib = nc.tensor.matmul(psG[:, 0:T], c["ones_row"], rrow[:], start=True, stop=True)
            mm.inc(ib, 1)
            kb.v.wait_ge(mm.h, mm.n)
            kb.v.tensor_copy(kb.r2[:], psG[:, 0:T])
            sph.wait_all(kb.sc)
            spl.wait_all(kb.sc)
            kb.barrier()
        ab_scope.close()
        with ExitStack() as lo_:
            fhb = kb.sb(lo_, "fD_fhb", [128, 64 * T], BF16)
            flb = kb.sb(lo_, "fD_flb", [128, 64 * T], BF16)
            wp = kb.sb(lo_, "fD_wp", [128, 2 * 64 * 128], BF16)
            xd = kb.sb(lo_, "fD_xd", [128, 2 * T], F32)
            ot = kb.sb(lo_, "fD_ot", [128, 2 * T], F32)
            psD = kb.ps(lo_, "fD_psD", [128, 2 * T])
            v1 = rbh.issue(nc.sync.dma_start(
                out=fhb[:].rearrange("p (j t) -> p j t", j=64),
                in_=fh_hi_ap[0:64].rearrange("j p t -> p j t")))
            v2 = rbl.issue(nc.sync.dma_start(
                out=flb[:].rearrange("p (j t) -> p j t", j=64),
                in_=fh_lo_ap[0:64].rearrange("j p t -> p j t")))
            wp_ld, dmm_done, out_done, st_dn, xd_ld = [], [], [], [], []
            for d in range(DT):
                buf = d % 2
                wpb = wp[:, buf * 64 * 128:(buf + 1) * 64 * 128]
                if d >= 2:
                    nc.sync.wait_ge(mm.h, dmm_done[d - 2])
                wp_ld.append(wpld.issue(nc.sync.dma_start(out=wpb, in_=qproj_ap[d])))
                wpld.wait(nc.tensor, wp_ld[d])
                if d == 0:
                    rbh.wait(nc.tensor, v1)
                    rbl.wait(nc.tensor, v2)
                if d >= 2:
                    nc.tensor.wait_ge(vo.h, out_done[d - 2])
                first = True
                for hsrc in (fhb, flb):
                    for kc in range(64):
                        im = nc.tensor.matmul(
                            psD[:, buf * T:(buf + 1) * T],
                            wpb[:, kc * 128:(kc + 1) * 128],
                            hsrc[:, kc * T:(kc + 1) * T],
                            start=first, stop=(hsrc is flb and kc == 63))
                        first = False
                dmm_done.append(mm.inc(im, 1))
                if d >= 2:
                    nc.sync.wait_ge(vo.h, out_done[d - 2])
                xd_ld.append(xdld.issue(nc.sync.dma_start(
                    out=xd[:, buf * T:(buf + 1) * T], in_=s_in[d, :, tok])))
                kb.v.wait_ge(mm.h, dmm_done[d])
                xdld.wait(kb.v, xd_ld[d])
                if d >= 2:
                    outst.wait(kb.v, st_dn[d - 2])
                kb.v.tensor_tensor(ot[:, buf * T:(buf + 1) * T],
                                   psD[:, buf * T:(buf + 1) * T], kb.r2[:], ALU.mult)
                kb.v.scalar_tensor_tensor(
                    ot[:, buf * T:(buf + 1) * T], ot[:, buf * T:(buf + 1) * T], 0.5,
                    xd[:, buf * T:(buf + 1) * T], ALU.mult, ALU.add)
                out_done.append(vo.n)
                kb.sc.wait_ge(vo.h, out_done[d])
                st_dn.append(outst.issue(kb.sc.dma_start(
                    out=s_out[d, :, tok], in_=ot[:, buf * T:(buf + 1) * T])))
                kb.pump(6)
            outst.wait_all(kb.sc)
            kb.barrier()
    slscope.close()


def emit_split_phase(kb, w_ap, whi_ap, wlo_ap, n_blocks, blk_w, scale_col=None, scale_w=None):
    nc, s, c = kb.nc, kb.s, kb.c
    vo = kb.ord["vector"]
    with ExitStack() as local:
        wf = kb.sb(local, "s_wf", [128, 2 * blk_w], F32)
        hi = kb.sb(local, "s_hi", [128, 2 * blk_w], BF16)
        lo = kb.sb(local, "s_lo", [128, 2 * blk_w], BF16)
        tmp = kb.sb(local, "s_tmp", [128, blk_w], F32)
        wld = kb.ds("q_wld")
        sth = kb.ds("g_sth")
        stl = kb.ds("g_stl")
        ld_done, v_done, st_done = [], [], []
        for j in range(n_blocks):
            buf = j % 2
            wfb = wf[:, buf * blk_w:(buf + 1) * blk_w]
            hib = hi[:, buf * blk_w:(buf + 1) * blk_w]
            lob = lo[:, buf * blk_w:(buf + 1) * blk_w]
            if j >= 2:
                nc.sync.wait_ge(vo.h, v_done[j - 2])
            ld_done.append(wld.issue(nc.sync.dma_start(out=wfb, in_=w_ap[j])))
            wld.wait(kb.v, ld_done[j])
            if j >= 2:
                sth.wait(kb.v, st_done[j - 2][0])
                stl.wait(kb.v, st_done[j - 2][1])
            if scale_col is not None:
                if scale_w is None:
                    kb.v.tensor_scalar(wfb, wfb, scale_col, None, ALU.mult)
                else:
                    for kc in range(blk_w // scale_w):
                        kb.v.tensor_scalar(wfb[:, kc * scale_w:(kc + 1) * scale_w],
                                           wfb[:, kc * scale_w:(kc + 1) * scale_w],
                                           scale_col[:, kc:kc + 1], None, ALU.mult)
            kb.v.tensor_copy(hib, wfb)
            kb.v.tensor_tensor(tmp[:], wfb, hib, ALU.subtract)
            kb.v.tensor_copy(lob, tmp[:])
            v_done.append(vo.n)
            kb.sc.wait_ge(vo.h, v_done[j])
            if j >= 2:
                sth.wait(kb.sc, st_done[j - 2][0])
                stl.wait(kb.sc, st_done[j - 2][1])
            t1 = sth.issue(kb.sc.dma_start(out=whi_ap[j], in_=hib))
            t2 = stl.issue(kb.sc.dma_start(out=wlo_ap[j], in_=lob))
            st_done.append((t1, t2))
        sth.wait_all(kb.sc)
        stl.wait_all(kb.sc)
    kb.barrier()


GJ = 43


def emit_ffn1(kb, s_in, s_out, gh_ap, gl_ap, uh_ap, ul_ap, dh_ap, dl_ap,
              fh_hi_ap, fh_lo_ap):
    nc, s, c = kb.nc, kb.s, kb.c
    vo, so = kb.ord["vector"], kb.ord["scalar"]
    mm = s["mm0"]
    xld = kb.ds("f_xld", 1)
    wld = kb.ds("g_wld", 4)
    wld2 = kb.ds("g_wld2", 4)
    sph = kb.ds("f_sph")
    spl = kb.ds("f_spl")
    rbh = kb.ds("f_rbh", 1)
    rbl = kb.ds("f_rbl", 1)
    wpld = kb.ds("f_wpld")
    wpld2 = kb.ds("f_wpld2")
    xdld = kb.ds("f_xdld")
    outst = kb.ds("f_outst")
    KW = 16 * 128
    slscope = ExitStack()
    kb.hhi = kb.sb(slscope, "g_hhi", [128, DT * T], BF16)
    kb.hlo = kb.sb(slscope, "g_hlo", [128, DT * T], BF16)
    for t in range(NT):
        tok = slice(t * T, (t + 1) * T)
        with ExitStack() as lo_:
            x_sb = kb.sb(lo_, "gA_x", [128, (DT + 1) * T], F32)
            brow = kb.sb(lo_, "gA_brow", [1, 2 * T], F32)
            bfull = kb.sb(lo_, "gA_bfull", [128, 2 * T], F32)
            psA = kb.ps(lo_, "gA_psA", [1, T])
            psB = kb.ps(lo_, "gA_psB", [1, T])
            psC = kb.ps(lo_, "gA_psC", [128, 2 * T])
            v_ld = xld.issue(nc.sync.dma_start(
                out=x_sb[:, 0:DT * T].rearrange("p (d t) -> p d t", d=DT),
                in_=s_in[:, :, tok].rearrange("d p t -> p d t")))
            xld.wait(kb.v, v_ld)
            xld.wait(kb.sc, v_ld)
            xld.wait(nc.tensor, v_ld)
            emit_ln(kb, x_sb, x_sb[:, DT * T:], kb.hhi, kb.hlo, psA, psB, psC, brow, bfull)
            kb.barrier()
        with ExitStack() as lo_:
            wg = kb.sb(lo_, "gB_wg", [128, 4 * KW], BF16)
            wu = kb.sb(lo_, "gB_wu", [128, 4 * KW], BF16)
            hf = kb.sb(lo_, "gB_hf", [128, 2 * T], F32)
            ht = kb.sb(lo_, "gB_ht", [128, 2 * T], F32)
            fhh = kb.sb(lo_, "gB_fhh", [128, 2 * T], BF16)
            fhl = kb.sb(lo_, "gB_fhl", [128, 2 * T], BF16)
            psG = kb.ps(lo_, "gB_psG", [128, 2 * T])
            psU = kb.ps(lo_, "gB_psU", [128, 2 * T])
            wq_ld, wu_ld, mm_done, silu_done, hm_done, cast_done, st_done =                 [], [], [], [], [], [], []
            hhi, hlo = kb.hhi, kb.hlo
            for j in range(GJ):
                buf = j % 2
                wgb = wg[:, buf * 2 * KW:(buf + 1) * 2 * KW]
                wub = wu[:, buf * 2 * KW:(buf + 1) * 2 * KW]
                tsl = slice(buf * T, (buf + 1) * T)
                if j >= 2:
                    nc.sync.wait_ge(mm.h, mm_done[j - 2])
                i1 = nc.sync.dma_start(out=wgb[:, 0:KW], in_=gh_ap[j])
                i2 = nc.sync.dma_start(out=wgb[:, KW:2 * KW], in_=gl_ap[j])
                wq_ld.append((wld.issue(i1), wld.issue(i2)))
                i3 = nc.sync.dma_start(out=wub[:, 0:KW], in_=uh_ap[j])
                i4 = nc.sync.dma_start(out=wub[:, KW:2 * KW], in_=ul_ap[j])
                wu_ld.append((wld2.issue(i3), wld2.issue(i4)))
                wld.wait(nc.tensor, wq_ld[j][0])
                wld.wait(nc.tensor, wq_ld[j][1])
                wld2.wait(nc.tensor, wu_ld[j][0])
                wld2.wait(nc.tensor, wu_ld[j][1])
                if j >= 2:
                    nc.tensor.wait_ge(vo.h, hm_done[j - 2])
                    nc.tensor.wait_ge(so.h, silu_done[j - 2])
                for ps, wb in ((psG, wgb), (psU, wub)):
                    first = True
                    for ph, pl, hsrc in ((0, 0, hhi), (0, 1, hlo), (1, 0, hhi)):
                        woff = pl * 0 + ph * KW
                        for kc in range(16):
                            im = nc.tensor.matmul(
                                ps[:, tsl],
                                wb[:, woff + kc * 128:woff + (kc + 1) * 128],
                                hsrc[:, kc * T:(kc + 1) * T],
                                start=first, stop=(ph == 1 and kc == 15))
                            first = False
                mm_done.append(mm.inc(im, 1))
                kb.sc.wait_ge(mm.h, mm_done[j])
                if j >= 2:
                    kb.sc.wait_ge(vo.h, cast_done[j - 2])
                kb.sc.activation(hf[:, tsl], psG[:, tsl], AF.Silu)
                silu_done.append(so.n)
                kb.v.wait_ge(so.h, silu_done[j])
                if j >= 2:
                    sph.wait(kb.v, st_done[j - 2][0])
                    spl.wait(kb.v, st_done[j - 2][1])
                kb.v.tensor_tensor(ht[:, tsl], hf[:, tsl], psU[:, tsl], ALU.mult)
                hm_done.append(vo.n)
                kb.v.tensor_copy(fhh[:, tsl], ht[:, tsl])
                kb.v.tensor_tensor(hf[:, tsl], ht[:, tsl], fhh[:, tsl], ALU.subtract)
                kb.v.tensor_copy(fhl[:, tsl], hf[:, tsl])
                cast_done.append(vo.n)
                kb.sc.wait_ge(vo.h, cast_done[j])
                t1 = sph.issue(kb.sc.dma_start(out=fh_hi_ap[j], in_=fhh[:, tsl]))
                t2 = spl.issue(kb.sc.dma_start(out=fh_lo_ap[j], in_=fhl[:, tsl]))
                st_done.append((t1, t2))
                kb.pump(2)
            sph.wait_all(kb.sc)
            spl.wait_all(kb.sc)
            kb.barrier()
        with ExitStack() as lo_:
            fhb = kb.sb(lo_, "gD_fhb", [128, GJ * T], BF16)
            flb = kb.sb(lo_, "gD_flb", [128, GJ * T], BF16)
            wp = kb.sb(lo_, "gD_wp", [128, 2 * 5504], BF16)
            wpl = kb.sb(lo_, "gD_wpl", [128, 2 * 5504], BF16)
            xd = kb.sb(lo_, "gD_xd", [128, 2 * T], F32)
            ot = kb.sb(lo_, "gD_ot", [128, 2 * T], F32)
            psD = kb.ps(lo_, "gD_psD", [128, 2 * T])
            v1 = rbh.issue(nc.sync.dma_start(
                out=fhb[:].rearrange("p (j t) -> p j t", j=GJ),
                in_=fh_hi_ap[0:GJ].rearrange("j p t -> p j t")))
            v2 = rbl.issue(nc.sync.dma_start(
                out=flb[:].rearrange("p (j t) -> p j t", j=GJ),
                in_=fh_lo_ap[0:GJ].rearrange("j p t -> p j t")))
            wp_ld, dmm_done, out_done, st_dn, xd_ld = [], [], [], [], []
            for d in range(DT):
                buf = d % 2
                wpb = wp[:, buf * 5504:(buf + 1) * 5504]
                wplb = wpl[:, buf * 5504:(buf + 1) * 5504]
                if d >= 2:
                    nc.sync.wait_ge(mm.h, dmm_done[d - 2])
                i1 = nc.sync.dma_start(out=wpb, in_=dh_ap[d])
                i2 = nc.sync.dma_start(out=wplb, in_=dl_ap[d])
                wp_ld.append((wpld.issue(i1), wpld2.issue(i2)))
                wpld.wait(nc.tensor, wp_ld[d][0])
                wpld2.wait(nc.tensor, wp_ld[d][1])
                if d == 0:
                    rbh.wait(nc.tensor, v1)
                    rbl.wait(nc.tensor, v2)
                if d >= 2:
                    nc.tensor.wait_ge(vo.h, out_done[d - 2])
                first = True
                for wsrc, hsrc in ((wpb, fhb), (wpb, flb), (wplb, fhb)):
                    for kc in range(GJ):
                        im = nc.tensor.matmul(
                            psD[:, buf * T:(buf + 1) * T],
                            wsrc[:, kc * 128:(kc + 1) * 128],
                            hsrc[:, kc * T:(kc + 1) * T],
                            start=first, stop=(wsrc is wplb and kc == GJ - 1))
                        first = False
                dmm_done.append(mm.inc(im, 1))
                if d >= 2:
                    nc.sync.wait_ge(vo.h, out_done[d - 2])
                xd_ld.append(xdld.issue(nc.sync.dma_start(
                    out=xd[:, buf * T:(buf + 1) * T], in_=s_in[d, :, tok])))
                kb.v.wait_ge(mm.h, dmm_done[d])
                xdld.wait(kb.v, xd_ld[d])
                if d >= 2:
                    outst.wait(kb.v, st_dn[d - 2])
                kb.v.tensor_scalar(ot[:, buf * T:(buf + 1) * T],
                                   psD[:, buf * T:(buf + 1) * T], 0.5, None, ALU.mult)
                kb.v.tensor_tensor(ot[:, buf * T:(buf + 1) * T],
                                   ot[:, buf * T:(buf + 1) * T],
                                   xd[:, buf * T:(buf + 1) * T], ALU.add)
                out_done.append(vo.n)
                kb.sc.wait_ge(vo.h, out_done[d])
                st_dn.append(outst.issue(kb.sc.dma_start(
                    out=s_out[d, :, tok], in_=ot[:, buf * T:(buf + 1) * T])))
                kb.pump(6)
            outst.wait_all(kb.sc)
            kb.barrier()
    slscope.close()


MQ = 128
NCH = NTOK // MQ
NH = 64
HD = 64
NG = 8


def emit_mamba(kb, s_in, s_out, mp_hi, mp_lo, wo_hi, wo_lo, conv_t_ap, mvec_ap,
               gate_scr, x_scr, b_scr, c_scr, gh_scr, gl_scr):
    nc, s, c = kb.nc, kb.s, kb.c
    vo, so = kb.ord["vector"], kb.ord["scalar"]
    mm = s["mm0"]
    xld = kb.ds("f_xld", 1)
    wld = kb.ds("g_wld", 4)
    spg = kb.ds("m_spg")
    spx = kb.ds("m_spx")
    spb = kb.ds("m_spb")
    cld = kb.ds("m_cld", 8)
    wold = kb.ds("g_wld2", 4)
    xdld = kb.ds("f_xdld")
    outst = kb.ds("f_outst")

    with ExitStack() as ms:
        dts = kb.sb(ms, "m_dts", [64, NTOK], F32)
        rstds = kb.sb(ms, "m_rstds", [1, NTOK], F32)
        Ls = kb.sb(ms, "m_Ls", [64, NTOK], F32)
        zer = kb.sb(ms, "m_zer", [64, MQ], F32)
        halo = kb.sb(ms, "m_halo", [128, 48 * 3], F32)
        convw = kb.sb(ms, "m_convw", [128, 48 * 5], F32)
        mvec = kb.sb(ms, "m_mvec", [128, 8], F32)
        Dfull = kb.sb(ms, "m_Dfull", [128, 64], F32)
        Hst = kb.sb(ms, "m_H", [128, NH * HD], F32)
        t1 = wld.issue(nc.sync.dma_start(out=convw[:, 0:240], in_=conv_t_ap[:, :]))
        t2 = wld.issue(nc.sync.dma_start(out=mvec[:, 0:4], in_=mvec_ap[:, :]))
        wld.wait(kb.v, t1)
        wld.wait(kb.v, t2)
        kb.v.memset(zer[:], 0.0)
        kb.v.memset(halo[:], 0.0)
        kb.v.memset(Hst[:], 0.0)
        kb.sc.wait_ge(vo.h, vo.n)
        kb.sc.activation(mvec[:, 4:5], mvec[:, 1:2], AF.Exp)
        kb.v.wait_ge(so.h, so.n)
        kb.v.tensor_scalar(mvec[:, 4:5], mvec[:, 4:5], -1.0, None, ALU.mult)
        with ExitStack() as lo_:
            psE = kb.ps(lo_, "m_psE", [128, 512])
            nc.tensor.wait_ge(vo.h, vo.n)
            it = nc.tensor.transpose(psE[0:1, 0:128], mvec[:, 2:3], c["id_f32"])
            mm.inc(it, 1)
            kb.v.wait_ge(mm.h, mm.n)
            kb.v.tensor_copy(Dfull[0:1, :], psE[0:1, 0:64])
            nc.tensor.wait_ge(vo.h, vo.n)
            ib = nc.tensor.matmul(psE[:, 64:128], c["ones_row"], Dfull[0:1, :],
                                  start=True, stop=True)
            mm.inc(ib, 1)
            kb.v.wait_ge(mm.h, mm.n)
            kb.v.tensor_copy(Dfull[:], psE[:, 64:128])
        kb.barrier()

        m1s = ExitStack()
        hhi = kb.sb(m1s, "m_hhi", [128, DT * T], BF16)
        hlo = kb.sb(m1s, "m_hlo", [128, DT * T], BF16)
        for t in range(NT):
            tok = slice(t * T, (t + 1) * T)
            with ExitStack() as lo_:
                x_sb = kb.sb(lo_, "mA_x", [128, (DT + 1) * T], F32)
                brow = kb.sb(lo_, "mA_brow", [1, 2 * T], F32)
                bfull = kb.sb(lo_, "mA_bfull", [128, 2 * T], F32)
                psA = kb.ps(lo_, "mA_psA", [1, T])
                psB = kb.ps(lo_, "mA_psB", [1, T])
                psC = kb.ps(lo_, "mA_psC", [128, 2 * T])
                v_ld = xld.issue(nc.sync.dma_start(
                    out=x_sb[:, 0:DT * T].rearrange("p (d t) -> p d t", d=DT),
                    in_=s_in[:, :, tok].rearrange("d p t -> p d t")))
                xld.wait(kb.v, v_ld)
                xld.wait(kb.sc, v_ld)
                xld.wait(nc.tensor, v_ld)
                emit_ln(kb, x_sb, x_sb[:, DT * T:], hhi, hlo, psA, psB, psC, brow, bfull)
                kb.barrier()
            with ExitStack() as lo_:
                wsp = kb.sb(lo_, "mP_w", [128, 4 * 2048], BF16)
                cin = kb.sb(lo_, "mP_cin", [128, 2 * 515], F32)
                otile = kb.sb(lo_, "mP_ot", [128, 2 * T], F32)
                dt_raw = kb.sb(lo_, "mP_dt", [64, T], F32)
                scr64 = kb.sb(lo_, "mP_scr", [64, T], F32)
                psP = kb.ps(lo_, "mP_psP", [128, 2 * T])
                w_ld, mmd, evd, std = [], [], [], []
                for j in range(81):
                    buf = j % 2
                    whb = wsp[:, buf * 2 * 2048:buf * 2 * 2048 + 2048]
                    wlb = wsp[:, buf * 2 * 2048 + 2048:(buf + 1) * 2 * 2048]
                    if j >= 2:
                        nc.sync.wait_ge(mm.h, mmd[j - 2])
                    i1 = nc.sync.dma_start(out=whb, in_=mp_hi[j])
                    i2 = nc.sync.dma_start(out=wlb, in_=mp_lo[j])
                    w_ld.append((wld.issue(i1), wld.issue(i2)))
                    wld.wait(nc.tensor, w_ld[j][0])
                    wld.wait(nc.tensor, w_ld[j][1])
                    if j >= 2:
                        nc.tensor.wait_ge(vo.h, evd[j - 2])
                    first = True
                    for wsrc, hsrc in ((whb, hhi), (whb, hlo), (wlb, hhi)):
                        for kc in range(16):
                            im = nc.tensor.matmul(
                                psP[:, buf * T:(buf + 1) * T],
                                wsrc[:, kc * 128:(kc + 1) * 128],
                                hsrc[:, kc * T:(kc + 1) * T],
                                start=first, stop=(wsrc is wlb and kc == 15))
                            first = False
                    mmd.append(mm.inc(im, 1))
                    kb.v.wait_ge(mm.h, mmd[j])
                    if j >= 2 and std[j - 2] is not None:
                        std[j - 2][0].wait(kb.v, std[j - 2][1])
                    ot_ = otile[:, buf * T:(buf + 1) * T]
                    if j < 32:
                        kb.v.tensor_copy(ot_, psP[:, buf * T:(buf + 1) * T])
                        evd.append(vo.n)
                        kb.sc.wait_ge(vo.h, evd[j])
                        std.append((spg, spg.issue(kb.sc.dma_start(
                            out=gate_scr[j, :, tok], in_=ot_))))
                    elif j < 80:
                        ct = j - 32
                        cb = cin[:, buf * 515:(buf + 1) * 515]
                        kb.v.tensor_copy(cb[:, 0:3], halo[:, ct * 3:(ct + 1) * 3])
                        kb.v.tensor_copy(cb[:, 3:515], psP[:, buf * T:(buf + 1) * T])
                        kb.v.tensor_copy(halo[:, ct * 3:(ct + 1) * 3], cb[:, 512:515])
                        kb.v.tensor_scalar(ot_, cb[:, 0:T], convw[:, ct * 4:ct * 4 + 1],
                                           convw[:, 192 + ct:193 + ct], ALU.mult, ALU.add)
                        for k in range(1, 4):
                            kb.v.scalar_tensor_tensor(ot_, cb[:, k:k + T],
                                                      convw[:, ct * 4 + k:ct * 4 + k + 1],
                                                      ot_, ALU.mult, ALU.add)
                        evd.append(vo.n)
                        kb.sc.wait_ge(vo.h, evd[j])
                        kb.sc.activation(ot_, ot_, AF.Silu)
                        sil = so.n
                        kb.sc.wait_ge(so.h, sil)
                        if ct < 32:
                            std.append((spx, spx.issue(kb.sc.dma_start(
                                out=x_scr[ct, :, tok], in_=ot_))))
                        elif ct < 40:
                            std.append((spb, spb.issue(kb.sc.dma_start(
                                out=b_scr[ct - 32, :, tok], in_=ot_))))
                        else:
                            std.append((spb, spb.issue(kb.sc.dma_start(
                                out=c_scr[ct - 40, :, tok], in_=ot_))))
                    else:
                        kb.v.tensor_copy(dt_raw[:], psP[0:64, buf * T:(buf + 1) * T])
                        evd.append(vo.n)
                        std.append(None)
                    kb.pump(2)
                kb.v.tensor_scalar(dt_raw[:], dt_raw[:], mvec[0:64, 0:1], None, ALU.add)
                kb.sc.wait_ge(vo.h, vo.n)
                kb.sc.activation(scr64[:], dt_raw[:], AF.Abs)
                kb.sc.activation(scr64[:], scr64[:], AF.Exp, scale=-1.0)
                kb.sc.activation(scr64[:], scr64[:], AF.Ln, bias=c["ones_col"][0:64, :])
                dsl = slice(t * T, (t + 1) * T)
                kb.sc.activation(dts[0:64, dsl], dt_raw[:], AF.Relu)
                kb.v.wait_ge(so.h, so.n)
                kb.v.tensor_tensor(dts[0:64, dsl], dts[0:64, dsl], scr64[:], ALU.add)
                kb.v.tensor_scalar(scr64[:], dts[0:64, dsl], mvec[0:64, 4:5], None, ALU.mult)
                for cc in range(T // MQ):
                    csl = slice(t * T + cc * MQ, t * T + (cc + 1) * MQ)
                    kb.v.tensor_tensor_scan(Ls[0:64, csl], scr64[:, cc * MQ:(cc + 1) * MQ],
                                            zer[:], 0.0, ALU.add, ALU.add)
                spg.wait_all(kb.sc)
                spx.wait_all(kb.sc)
                spb.wait_all(kb.sc)
                kb.barrier()

        m1s.close()
        with ExitStack() as lo_:
            Xf = kb.sb(lo_, "m2_Xf", [128, 32 * MQ], F32)
            Xt = kb.sb(lo_, "m2_Xt", [128, 32 * MQ], F32)
            Bf = kb.sb(lo_, "m2_Bf", [128, NG * MQ], F32)
            Cf = kb.sb(lo_, "m2_Cf", [128, NG * MQ], F32)
            Bt = kb.sb(lo_, "m2_Bt", [128, NG * MQ], F32)
            gt = kb.sb(lo_, "m2_gt", [128, 32 * MQ], F32)
            CBs = kb.sb(lo_, "m2_CB", [128, NG * MQ], F32)
            LT = kb.sb(lo_, "m2_LT", [128, 4 * 64], F32)
            Lfl = kb.sb(lo_, "m2_Lfl", [1, 16 * MQ], F32)
            eGf = kb.sb(lo_, "m2_eGf", [128, 64], F32)
            eLc = kb.sb(lo_, "m2_eLc", [64, MQ], F32)
            wsc = kb.sb(lo_, "m2_wsc", [64, MQ], F32)
            Es = kb.sb(lo_, "m2_Es", [128, 2 * MQ], F32)
            Ss = kb.sb(lo_, "m2_Ss", [128, 2 * MQ], F32)
            yis = kb.sb(lo_, "m2_yis", [128, 2 * HD], F32)
            xws = kb.sb(lo_, "m2_xws", [128, 2 * HD], F32)
            ytm = kb.sb(lo_, "m2_ytm", [128, NH * HD], F32)
            ghc = kb.sb(lo_, "m2_ghc", [128, 32 * MQ], BF16)
            glc = kb.sb(lo_, "m2_glc", [128, 32 * MQ], BF16)
            gf = kb.sb(lo_, "m2_gf", [128, 2 * MQ], F32)
            sg = kb.sb(lo_, "m2_sg", [128, 2 * MQ], F32)
            sq2 = kb.sb(lo_, "m2_sq2", [128, 2 * MQ], F32)
            ps = [kb.ps(lo_, f"m2_ps{i}", [128, 512]) for i in range(8)]
            gst = kb.ds("f_sph")
            gst2 = kb.ds("f_spl")

            for ch in range(NCH):
                csl = slice(ch * MQ, (ch + 1) * MQ)
                cg = ch % 4
                ltoks = []
                ltoks.append(cld.issue(nc.sync.dma_start(
                    out=Bf[:].rearrange("p (j t) -> p j t", j=NG),
                    in_=b_scr[:, :, csl].rearrange("j p t -> p j t"))))
                ltoks.append(cld.issue(nc.sync.dma_start(
                    out=Cf[:].rearrange("p (j t) -> p j t", j=NG),
                    in_=c_scr[:, :, csl].rearrange("j p t -> p j t"))))
                ltoks.append(cld.issue(nc.sync.dma_start(
                    out=Xf[:].rearrange("p (j t) -> p j t", j=32),
                    in_=x_scr[:, :, csl].rearrange("j p t -> p j t"))))
                ltoks.append(cld.issue(nc.sync.dma_start(
                    out=gt[:].rearrange("p (j t) -> p j t", j=32),
                    in_=gate_scr[:, :, csl].rearrange("j p t -> p j t"))))
                for tk in ltoks:
                    cld.wait(nc.tensor, tk)
                cld.wait(kb.sc, ltoks[3])
                kb.sc.activation(eLc[:], Ls[0:64, csl], AF.Exp)
                gcol = Ls[0:64, csl.stop - 1:csl.stop]
                kb.v.tensor_scalar(wsc[:], Ls[0:64, csl], -1.0, gcol, ALU.mult, ALU.add)
                kb.sc.wait_ge(vo.h, vo.n)
                kb.sc.activation(wsc[:], wsc[:], AF.Exp)
                kb.v.wait_ge(so.h, so.n)
                kb.v.tensor_tensor(wsc[:], wsc[:], dts[0:64, csl], ALU.mult)
                tr_ev = []
                for i in range(32):
                    bk = i % 2
                    eng = kb.v if bk == 0 else kb.sc
                    esm = vo if bk == 0 else so
                    if i >= 2:
                        nc.tensor.wait_ge(tr_ev[i - 2][0], tr_ev[i - 2][1])
                    it = nc.tensor.transpose(ps[bk][:, 0:MQ], Xf[:, i * MQ:(i + 1) * MQ],
                                             c["id_f32"])
                    mm.inc(it, 1)
                    eng.wait_ge(mm.h, mm.n)
                    if bk == 0:
                        eng.tensor_copy(Xt[:, i * MQ:(i + 1) * MQ], ps[bk][:, 0:MQ])
                    else:
                        eng.activation(Xt[:, i * MQ:(i + 1) * MQ], ps[bk][:, 0:MQ], AF.Copy)
                    tr_ev.append((esm.h, esm.n))
                for i in range(8):
                    bk = i % 2
                    eng = kb.v if bk == 0 else kb.sc
                    esm = vo if bk == 0 else so
                    nc.tensor.wait_ge(tr_ev[-2][0], tr_ev[-2][1])
                    it = nc.tensor.transpose(ps[bk][:, 0:MQ], Bf[:, i * MQ:(i + 1) * MQ],
                                             c["id_f32"])
                    mm.inc(it, 1)
                    eng.wait_ge(mm.h, mm.n)
                    if bk == 0:
                        eng.tensor_copy(Bt[:, i * MQ:(i + 1) * MQ], ps[bk][:, 0:MQ])
                    else:
                        eng.activation(Bt[:, i * MQ:(i + 1) * MQ], ps[bk][:, 0:MQ], AF.Copy)
                    tr_ev.append((esm.h, esm.n))
                for i, srcv in enumerate((Ls[0:64, csl], dts[0:64, csl], eLc[:], wsc[:])):
                    bk = i % 2
                    nc.tensor.wait_ge(tr_ev[-2][0], tr_ev[-2][1])
                    it = nc.tensor.transpose(ps[bk][0:MQ, 0:64], srcv,
                                             c["id_f32"][0:64, 0:64])
                    mm.inc(it, 1)
                    kb.v.wait_ge(mm.h, mm.n)
                    kb.v.tensor_copy(LT[:, i * 64:(i + 1) * 64], ps[bk][:, 0:64])
                    tr_ev.append((vo.h, vo.n))
                cb_ev = []
                for g in range(NG):
                    bk = 2 + g % 2
                    if g >= 2:
                        nc.tensor.wait_ge(vo.h, cb_ev[g - 2])
                    else:
                        nc.tensor.wait_ge(tr_ev[-1][0], tr_ev[-1][1])
                        nc.tensor.wait_ge(tr_ev[-2][0], tr_ev[-2][1])
                    im = nc.tensor.matmul(ps[bk][:, 0:MQ], Bf[:, g * MQ:(g + 1) * MQ],
                                          Cf[:, g * MQ:(g + 1) * MQ], start=True, stop=True)
                    mm.inc(im, 1)
                    kb.v.wait_ge(mm.h, mm.n)
                    kb.v.tensor_tensor(CBs[:, g * MQ:(g + 1) * MQ], ps[bk][:, 0:MQ],
                                       c["triu"], ALU.mult)
                    cb_ev.append(vo.n)
                nc.tensor.wait_ge(vo.h, vo.n)
                it = nc.tensor.transpose(ps[4][0:1, 0:64], eLc[:, MQ - 1:MQ],
                                         c["id_f32"][0:64, 0:64])
                mm.inc(it, 1)
                kb.v.wait_ge(mm.h, mm.n)
                kb.v.tensor_copy(eGf[0:1, :], ps[4][0:1, 0:64])
                nc.tensor.wait_ge(vo.h, vo.n)
                ib = nc.tensor.matmul(ps[5][:, 0:64], c["ones_row"], eGf[0:1, :],
                                      start=True, stop=True)
                mm.inc(ib, 1)
                kb.v.wait_ge(mm.h, mm.n)
                kb.v.tensor_copy(eGf[:], ps[5][:, 0:64])
                eT_done = [None] * NH
                psdH_last = [None] * NH
                s_done = [None] * NH
                yiadd_done = [None] * NH
                yits_done = [None] * NH
                h_done = [None] * NH
                lf_tok = None
                for h in range(NH):
                    st_ = h % 2
                    g = h // 8
                    psT = ps[0 + st_]
                    psY = ps[2 + st_]
                    psYI = ps[4 + st_]
                    psdH = ps[6 + st_]
                    Xth = Xt[:, (h // 2) * MQ + (h % 2) * 64:(h // 2) * MQ + (h % 2) * 64 + 64]
                    if h % 16 == 0:
                        if h > 0:
                            nc.sync.wait_ge(mm.h, psT_last)
                        lf_tok = cld.issue(nc.sync.dma_start(
                            out=Lfl[0:1, :].rearrange("o (h q) -> o h q", h=16),
                            in_=Ls[h:h + 16, csl]))
                        cld.wait(nc.tensor, lf_tok)
                    if h >= 2:
                        nc.tensor.wait_ge(vo.h, eT_done[h - 2])
                    else:
                        nc.tensor.wait_ge(vo.h, vo.n)
                    im = nc.tensor.matmul(psT[:, 0:MQ], c["ones_row"],
                                          Lfl[0:1, (h % 16) * MQ:(h % 16 + 1) * MQ],
                                          start=True, stop=True)
                    v_psT = mm.inc(im, 1)
                    psT_last = v_psT
                    kb.v.wait_ge(mm.h, v_psT)
                    kb.v.tensor_scalar(Es[:, st_ * MQ:(st_ + 1) * MQ], psT[:, 0:MQ],
                                       LT[:, h:h + 1], 0.0, ALU.subtract, ALU.min)
                    eT_done[h] = vo.n
                    kb.sc.wait_ge(vo.h, eT_done[h])
                    kb.sc.activation(Es[:, st_ * MQ:(st_ + 1) * MQ],
                                     Es[:, st_ * MQ:(st_ + 1) * MQ], AF.Exp)
                    exp_done = so.n
                    kb.v.wait_ge(so.h, exp_done)
                    kb.v.tensor_tensor(Ss[:, st_ * MQ:(st_ + 1) * MQ],
                                       Es[:, st_ * MQ:(st_ + 1) * MQ],
                                       CBs[:, g * MQ:(g + 1) * MQ], ALU.mult)
                    kb.v.tensor_scalar(Ss[:, st_ * MQ:(st_ + 1) * MQ],
                                       Ss[:, st_ * MQ:(st_ + 1) * MQ],
                                       LT[:, 64 + h:65 + h], None, ALU.mult)
                    s_done[h] = vo.n
                    nc.tensor.wait_ge(vo.h, s_done[h])
                    if h >= 2:
                        nc.tensor.wait_ge(vo.h, yiadd_done[h - 2])
                    im = nc.tensor.matmul(psY[:, 0:HD], Ss[:, st_ * MQ:(st_ + 1) * MQ],
                                          Xth, start=True, stop=True)
                    v_psY = mm.inc(im, 1)
                    if h >= 2:
                        nc.tensor.wait_ge(so.h, yits_done[h - 2])
                    im = nc.tensor.matmul(psYI[:, 0:HD], Cf[:, g * MQ:(g + 1) * MQ],
                                          Hst[:, h * HD:(h + 1) * HD], start=True, stop=True)
                    v_psYI = mm.inc(im, 1)
                    kb.sc.wait_ge(mm.h, v_psYI)
                    if h >= 2:
                        kb.sc.wait_ge(vo.h, yiadd_done[h - 2])
                    kb.sc.activation(yis[:, st_ * HD:(st_ + 1) * HD], psYI[:, 0:HD],
                                     AF.Copy, scale=LT[:, 128 + h:129 + h])
                    if h >= 2:
                        kb.sc.wait_ge(mm.h, psdH_last[h - 2])
                    kb.sc.activation(xws[:, st_ * HD:(st_ + 1) * HD], Xth,
                                     AF.Copy, scale=LT[:, 192 + h:193 + h])
                    yits_done[h] = so.n
                    kb.v.wait_ge(mm.h, v_psY)
                    kb.v.wait_ge(so.h, yits_done[h])
                    kb.v.tensor_tensor(yis[:, st_ * HD:(st_ + 1) * HD],
                                       yis[:, st_ * HD:(st_ + 1) * HD],
                                       psY[:, 0:HD], ALU.add)
                    yiadd_done[h] = vo.n
                    kb.v.scalar_tensor_tensor(ytm[:, h * HD:(h + 1) * HD], Xth,
                                              Dfull[:, h:h + 1],
                                              yis[:, st_ * HD:(st_ + 1) * HD],
                                              ALU.mult, ALU.add)
                    xw_done = so.n
                    nc.tensor.wait_ge(so.h, xw_done)
                    if h >= 2:
                        nc.tensor.wait_ge(vo.h, h_done[h - 2])
                    im = nc.tensor.matmul(psdH[:, 0:HD], Bt[:, g * MQ:(g + 1) * MQ],
                                          xws[:, st_ * HD:(st_ + 1) * HD],
                                          start=True, stop=True)
                    v_psdH = mm.inc(im, 1)
                    psdH_last[h] = v_psdH
                    kb.v.wait_ge(mm.h, v_psdH)
                    kb.v.scalar_tensor_tensor(Hst[:, h * HD:(h + 1) * HD],
                                              Hst[:, h * HD:(h + 1) * HD],
                                              eGf[:, h:h + 1], psdH[:, 0:HD],
                                              ALU.mult, ALU.add)
                    h_done[h] = vo.n
                gz_ev = []
                for i in range(32):
                    bk = i % 2
                    if i >= 2:
                        nc.tensor.wait_ge(vo.h, gz_ev[i - 2])
                    else:
                        nc.tensor.wait_ge(vo.h, h_done[NH - 1])
                    it = nc.tensor.transpose(ps[bk][:, 0:MQ], ytm[:, i * MQ:(i + 1) * MQ],
                                             c["id_f32"])
                    trv = mm.inc(it, 1)
                    kb.sc.activation(sg[:, bk * MQ:(bk + 1) * MQ], gt[:, i * MQ:(i + 1) * MQ],
                                     AF.Silu)
                    sgv = so.n
                    kb.v.wait_ge(mm.h, trv)
                    kb.v.wait_ge(so.h, sgv)
                    kb.v.tensor_tensor(gf[:, bk * MQ:(bk + 1) * MQ], ps[bk][:, 0:MQ],
                                       sg[:, bk * MQ:(bk + 1) * MQ], ALU.mult)
                    kb.sc.wait_ge(vo.h, vo.n)
                    kb.sc.activation(sq2[:, bk * MQ:(bk + 1) * MQ],
                                     gf[:, bk * MQ:(bk + 1) * MQ], AF.Square)
                    sqv = so.n
                    nc.tensor.wait_ge(so.h, sqv)
                    im = nc.tensor.matmul(ps[2][0:1, 0:MQ], c["ones_col"],
                                          sq2[:, bk * MQ:(bk + 1) * MQ],
                                          start=(i == 0), stop=(i == 31))
                    mm.inc(im, 1)
                    kb.v.wait_ge(so.h, sqv)
                    kb.v.tensor_copy(ghc[:, i * MQ:(i + 1) * MQ], gf[:, bk * MQ:(bk + 1) * MQ])
                    kb.v.tensor_tensor(gf[:, bk * MQ:(bk + 1) * MQ],
                                       gf[:, bk * MQ:(bk + 1) * MQ],
                                       ghc[:, i * MQ:(i + 1) * MQ], ALU.subtract)
                    kb.v.tensor_copy(glc[:, i * MQ:(i + 1) * MQ], gf[:, bk * MQ:(bk + 1) * MQ])
                    gz_ev.append(vo.n)
                kb.v.wait_ge(mm.h, mm.n)
                kb.v.tensor_scalar(rstds[0:1, csl], ps[2][0:1, 0:MQ],
                                   1.0 / 4096, EPS_LN, ALU.mult, ALU.add)
                kb.sc.wait_ge(vo.h, vo.n)
                kb.sc.activation(rstds[0:1, csl], rstds[0:1, csl], AF.Sqrt)
                kb.v.wait_ge(so.h, so.n)
                kb.v.reciprocal(rstds[0:1, csl], rstds[0:1, csl])
                kb.sc.wait_ge(vo.h, vo.n)
                gst.issue(kb.sc.dma_start(out=gh_scr[ch], in_=ghc[:]))
                gst2.issue(kb.sc.dma_start(out=gl_scr[ch], in_=glc[:]))
                gst.wait_all(kb.sc)
                gst2.wait_all(kb.sc)
                kb.barrier()

        with ExitStack() as lo_:
            ghi = kb.sb(lo_, "m3_ghi", [128, 32 * T], BF16)
            glo = kb.sb(lo_, "m3_glo", [128, 32 * T], BF16)
            wo = kb.sb(lo_, "m3_wo", [128, 4 * 4096], BF16)
            xd = kb.sb(lo_, "m3_xd", [128, 2 * T], F32)
            ot3 = kb.sb(lo_, "m3_ot", [128, 2 * T], F32)
            rstdb = kb.sb(lo_, "m3_rstdb", [128, T], F32)
            psO = kb.ps(lo_, "m3_psO", [128, 2 * T])
            psR = kb.ps(lo_, "m3_psR", [128, T])
            for grp in range(8):
                tok3 = slice(grp * T, (grp + 1) * T)
                rtk = []
                ghi_v = ghi[:].rearrange("p (j g t) -> p j g t", j=32, g=4)
                glo_v = glo[:].rearrange("p (j g t) -> p j g t", j=32, g=4)
                for cg in range(4):
                    ch = grp * 4 + cg
                    rtk.append(cld.issue(nc.sync.dma_start(
                        out=ghi_v[:, :, cg, :],
                        in_=gh_scr[ch].rearrange("p (j t) -> p j t", j=32))))
                    rtk.append(cld.issue(nc.sync.dma_start(
                        out=glo_v[:, :, cg, :],
                        in_=gl_scr[ch].rearrange("p (j t) -> p j t", j=32))))
                for tk in rtk:
                    cld.wait(nc.tensor, tk)
                ib = nc.tensor.matmul(psR[:], c["ones_row"], rstds[0:1, tok3],
                                      start=True, stop=True)
                mm.inc(ib, 1)
                kb.v.wait_ge(mm.h, mm.n)
                kb.v.tensor_copy(rstdb[:], psR[:])
                wo_ld, dmm, ev3, st3, xl3 = [], [], [], [], []
                for d in range(DT):
                    buf = d % 2
                    whb = wo[:, buf * 2 * 4096:buf * 2 * 4096 + 4096]
                    wlb = wo[:, buf * 2 * 4096 + 4096:(buf + 1) * 2 * 4096]
                    if d >= 2:
                        nc.sync.wait_ge(mm.h, dmm[d - 2])
                    i1 = nc.sync.dma_start(out=whb, in_=wo_hi[d])
                    i2 = nc.sync.dma_start(out=wlb, in_=wo_lo[d])
                    wo_ld.append((wold.issue(i1), wold.issue(i2)))
                    wold.wait(nc.tensor, wo_ld[d][0])
                    wold.wait(nc.tensor, wo_ld[d][1])
                    if d >= 2:
                        nc.tensor.wait_ge(vo.h, ev3[d - 2])
                    first = True
                    for wsrc, hsrc in ((whb, ghi), (whb, glo), (wlb, ghi)):
                        for kc in range(32):
                            im = nc.tensor.matmul(
                                psO[:, buf * T:(buf + 1) * T],
                                wsrc[:, kc * 128:(kc + 1) * 128],
                                hsrc[:, kc * T:(kc + 1) * T],
                                start=first, stop=(wsrc is wlb and kc == 31))
                            first = False
                    dmm.append(mm.inc(im, 1))
                    if d >= 2:
                        nc.sync.wait_ge(vo.h, ev3[d - 2])
                    xl3.append(xdld.issue(nc.sync.dma_start(
                        out=xd[:, buf * T:(buf + 1) * T], in_=s_in[d, :, tok3])))
                    kb.v.wait_ge(mm.h, dmm[d])
                    xdld.wait(kb.v, xl3[d])
                    if d >= 2:
                        outst.wait(kb.v, st3[d - 2])
                    kb.v.tensor_tensor(ot3[:, buf * T:(buf + 1) * T],
                                       psO[:, buf * T:(buf + 1) * T], rstdb[:], ALU.mult)
                    kb.v.tensor_tensor(ot3[:, buf * T:(buf + 1) * T],
                                       ot3[:, buf * T:(buf + 1) * T],
                                       xd[:, buf * T:(buf + 1) * T], ALU.add)
                    ev3.append(vo.n)
                    kb.sc.wait_ge(vo.h, ev3[d])
                    st3.append(outst.issue(kb.sc.dma_start(
                        out=s_out[d, :, tok3], in_=ot3[:, buf * T:(buf + 1) * T])))
                outst.wait_all(kb.sc)
                kb.barrier()


ACH = 64
NAC = NTOK // ACH
AH = 16


def emit_attn(kb, s_in, s_out, wq_hi, wq_lo, wk_hi, wk_lo, wv_hi, wv_lo,
              wo_hi, wo_lo, fm_hi, fm_lo, avec_ap, qf_scr, kf_scr, vtm_scr):
    nc, s, c = kb.nc, kb.s, kb.c
    vo, so = kb.ord["vector"], kb.ord["scalar"]
    mm = s["mm0"]
    xld = kb.ds("f_xld", 1)
    wld = kb.ds("g_wld", 4)
    sps = [kb.ds("m_spg"), kb.ds("m_spx"), kb.ds("m_spb")]
    cld = kb.ds("m_cld", 8)
    wold = kb.ds("g_wld2", 4)
    xdld = kb.ds("f_xdld")
    outst = kb.ds("f_outst")
    SCALE = float(np.float32(128.0 ** -0.5))

    with ExitStack() as ms:
        avec = kb.sb(ms, "a_avec", [128, 4], F32)
        fmh = kb.sb(ms, "a_fmh", [128, 512], BF16)
        fml = kb.sb(ms, "a_fml", [128, 512], BF16)
        KV = kb.sb(ms, "a_KV", [128, AH * 128], F32)
        t1 = wld.issue(nc.sync.dma_start(out=avec[:], in_=avec_ap[:, :]))
        t2 = wld.issue(nc.sync.dma_start(out=fmh[:], in_=fm_hi[0]))
        t3 = wld.issue(nc.sync.dma_start(out=fml[:], in_=fm_lo[0]))
        for tk in (t1, t2, t3):
            wld.wait(kb.v, tk)
        kb.v.memset(KV[:], 0.0)
        kb.barrier()

        a1s = ExitStack()
        hhi = kb.sb(a1s, "a_hhi", [128, DT * T], BF16)
        hlo = kb.sb(a1s, "a_hlo", [128, DT * T], BF16)
        for t in range(NT):
            tok = slice(t * T, (t + 1) * T)
            with ExitStack() as lo_:
                x_sb = kb.sb(lo_, "aA_x", [128, (DT + 1) * T], F32)
                brow = kb.sb(lo_, "aA_brow", [1, 2 * T], F32)
                bfull = kb.sb(lo_, "aA_bfull", [128, 2 * T], F32)
                psA = kb.ps(lo_, "aA_psA", [1, T])
                psB = kb.ps(lo_, "aA_psB", [1, T])
                psC = kb.ps(lo_, "aA_psC", [128, 2 * T])
                v_ld = xld.issue(nc.sync.dma_start(
                    out=x_sb[:, 0:DT * T].rearrange("p (d t) -> p d t", d=DT),
                    in_=s_in[:, :, tok].rearrange("d p t -> p d t")))
                xld.wait(kb.v, v_ld)
                xld.wait(kb.sc, v_ld)
                xld.wait(nc.tensor, v_ld)
                emit_ln(kb, x_sb, x_sb[:, DT * T:], hhi, hlo, psA, psB, psC, brow, bfull)
                kb.barrier()
            with ExitStack() as lo_:
                wsp = kb.sb(lo_, "aP_w", [128, 4 * 2048], BF16)
                qhi = kb.sb(lo_, "aP_qhi", [128, 16 * T], BF16)
                qlo = kb.sb(lo_, "aP_qlo", [128, 16 * T], BF16)
                khi = kb.sb(lo_, "aP_khi", [128, 16 * T], BF16)
                klo = kb.sb(lo_, "aP_klo", [128, 16 * T], BF16)
                vfm = kb.sb(lo_, "aP_vfm", [128, 16 * T], F32)
                tmp = kb.sb(lo_, "aP_tmp", [128, 2 * T], F32)
                qf = kb.sb(lo_, "aP_qf", [128, 2 * T], F32)
                vt = kb.sb(lo_, "aP_vt", [64, 2 * 128], F32)
                psP = kb.ps(lo_, "aP_psP", [128, 2 * T])
                psM = kb.ps(lo_, "aP_psM", [128, 2 * T])
                evd = {}
                mmd = {}
                w_ld = []
                wsrc_hi = {"q": wq_hi, "k": wk_hi, "v": wv_hi}
                wsrc_lo = {"q": wq_lo, "k": wk_lo, "v": wv_lo}
                jj = 0
                for mat in ("q", "k", "v"):
                    for d in range(16):
                        buf = jj % 2
                        whb = wsp[:, buf * 2 * 2048:buf * 2 * 2048 + 2048]
                        wlb = wsp[:, buf * 2 * 2048 + 2048:(buf + 1) * 2 * 2048]
                        if jj >= 2:
                            nc.sync.wait_ge(mm.h, mmd[jj - 2])
                        i1 = nc.sync.dma_start(out=whb, in_=wsrc_hi[mat][d])
                        i2 = nc.sync.dma_start(out=wlb, in_=wsrc_lo[mat][d])
                        w_ld.append((wld.issue(i1), wld.issue(i2)))
                        wld.wait(nc.tensor, w_ld[jj][0])
                        wld.wait(nc.tensor, w_ld[jj][1])
                        if jj >= 2:
                            nc.tensor.wait_ge(vo.h, evd[jj - 2])
                        first = True
                        for wsrc, hsrc in ((whb, hhi), (whb, hlo), (wlb, hhi)):
                            for kc in range(16):
                                im = nc.tensor.matmul(
                                    psP[:, buf * T:(buf + 1) * T],
                                    wsrc[:, kc * 128:(kc + 1) * 128],
                                    hsrc[:, kc * T:(kc + 1) * T],
                                    start=first, stop=(wsrc is wlb and kc == 15))
                                first = False
                        mmd[jj] = mm.inc(im, 1)
                        kb.v.wait_ge(mm.h, mmd[jj])
                        if mat == "q":
                            dst_hi, dst_lo = qhi, qlo
                        elif mat == "k":
                            dst_hi, dst_lo = khi, klo
                        else:
                            dst_hi, dst_lo = None, None
                        dsl = slice(d * T, (d + 1) * T)
                        if dst_hi is not None:
                            kb.v.tensor_copy(dst_hi[:, dsl], psP[:, buf * T:(buf + 1) * T])
                            kb.v.tensor_tensor(tmp[:, buf * T:(buf + 1) * T],
                                               psP[:, buf * T:(buf + 1) * T],
                                               dst_hi[:, dsl], ALU.subtract)
                            kb.v.tensor_copy(dst_lo[:, dsl], tmp[:, buf * T:(buf + 1) * T])
                        else:
                            kb.v.tensor_copy(vfm[:, dsl], psP[:, buf * T:(buf + 1) * T])
                        evd[jj] = vo.n
                        jj += 1
                kb.barrier()
                fm_ev = {}
                sp_tok = {}
                for i in range(2 * AH):
                    h, which = i // 2, i % 2
                    src_hi = qhi if which == 0 else khi
                    src_lo = qlo if which == 0 else klo
                    scr = qf_scr if which == 0 else kf_scr
                    f1 = slice((2 * which) * 128, (2 * which) * 128 + 128)
                    f2 = slice((2 * which + 1) * 128, (2 * which + 1) * 128 + 128)
                    b1 = avec[:, 2 * which:2 * which + 1]
                    b2 = avec[:, 2 * which + 1:2 * which + 2]
                    hb = i % 2
                    mb = slice(hb * T, (hb + 1) * T)
                    hsl = slice(h * T, (h + 1) * T)
                    if i >= 2:
                        nc.tensor.wait_ge(vo.h, fm_ev[i - 2])
                    first = True
                    for wsrc, hsrc in ((fmh, src_hi), (fmh, src_lo), (fml, src_hi)):
                        im = nc.tensor.matmul(psM[:, mb], wsrc[:, f1], hsrc[:, hsl],
                                              start=first, stop=(wsrc is fml))
                        first = False
                    first = True
                    for wsrc, hsrc in ((fmh, src_hi), (fmh, src_lo), (fml, src_hi)):
                        im2 = nc.tensor.matmul(psP[:, mb], wsrc[:, f2], hsrc[:, hsl],
                                               start=first, stop=(wsrc is fml))
                        first = False
                    v_mm = mm.inc(im2, 1)
                    kb.v.wait_ge(mm.h, v_mm)
                    if i >= 2:
                        sp_tok[i - 2][0].wait(kb.v, sp_tok[i - 2][1])
                    kb.v.tensor_scalar(tmp[:, mb], psM[:, mb], b1, None, ALU.add)
                    kb.v.tensor_scalar(qf[:, mb], psP[:, mb], b2, None, ALU.add)
                    kb.v.tensor_tensor(qf[:, mb], qf[:, mb], tmp[:, mb], ALU.mult)
                    if which == 0:
                        kb.v.tensor_scalar(qf[:, mb], qf[:, mb], SCALE, None, ALU.mult)
                    fm_ev[i] = vo.n
                    kb.sc.wait_ge(vo.h, fm_ev[i])
                    ds_ = sps[i % 3]
                    sp_tok[i] = (ds_, ds_.issue(kb.sc.dma_start(
                        out=scr[h][:, tok], in_=qf[:, mb])))
                kb.barrier()
                vt_ev = {}
                vsp = {}
                for cc in range(T // ACH):
                    ch = t * (T // ACH) + cc
                    for d in range(16):
                        i = cc * 16 + d
                        bk = i % 2
                        if i >= 2:
                            nc.tensor.wait_ge(vo.h, vt_ev[i - 2])
                        it = nc.tensor.transpose(
                            psP[:, bk * T:bk * T + 64].rearrange("p q -> p q") if False
                            else psP[0:64, bk * T:bk * T + 128],
                            vfm[:, d * T + cc * ACH:d * T + (cc + 1) * ACH],
                            c["id_f32"])
                        mm.inc(it, 1)
                        kb.v.wait_ge(mm.h, mm.n)
                        if i >= 2:
                            vsp[i - 2][0].wait(kb.v, vsp[i - 2][1])
                        kb.v.tensor_copy(vt[0:64, (i % 2) * 128:(i % 2) * 128 + 128],
                                         psP[0:64, bk * T:bk * T + 128])
                        vt_ev[i] = vo.n
                        kb.sc.wait_ge(vo.h, vt_ev[i])
                        ds_ = sps[i % 3]
                        vsp[i] = (ds_, ds_.issue(kb.sc.dma_start(
                            out=vtm_scr[ch, :, d * 128:(d + 1) * 128],
                            in_=vt[0:64, (i % 2) * 128:(i % 2) * 128 + 128])))
                for sp in sps:
                    sp.wait_all(kb.sc)
                kb.barrier()
        a1s.close()

        with ExitStack() as lo_:
            qfc = kb.sb(lo_, "a2_qfc", [128, AH * ACH], F32)
            kfc = kb.sb(lo_, "a2_kfc", [128, AH * ACH], F32)
            vtc = kb.sb(lo_, "a2_vtc", [64, AH * 128], F32)
            kft = kb.sb(lo_, "a2_kft", [64, 2 * 128], F32)
            Ssb = kb.sb(lo_, "a2_Ssb", [64, 2 * ACH], F32)
            otm = kb.sb(lo_, "a2_otm", [64, AH * 128], F32)
            osq = kb.sb(lo_, "a2_osq", [64, AH * 128], F32)
            rst = kb.sb(lo_, "a2_rst", [64, AH], F32)
            onm = kb.sb(lo_, "a2_onm", [64, 2 * 128], F32)
            ofm = kb.sb(lo_, "a2_ofm", [128, AH * T], F32)
            ps = [kb.ps(lo_, f"a2_ps{i}", [128, 512]) for i in range(8)]
            ohi = kb.sb(lo_, "a2_ohi", [128, AH * T], BF16)
            olo = kb.sb(lo_, "a2_olo", [128, AH * T], BF16)
            for ch in range(NAC):
                cg = ch % 8
                grp = ch // 8
                csl = slice(ch * ACH, (ch + 1) * ACH)
                ltoks = []
                ltoks.append(cld.issue(nc.sync.dma_start(
                    out=qfc[:].rearrange("p (h t) -> p h t", h=AH),
                    in_=qf_scr[:, :, csl].rearrange("h p t -> p h t"))))
                ltoks.append(cld.issue(nc.sync.dma_start(
                    out=kfc[:].rearrange("p (h t) -> p h t", h=AH),
                    in_=kf_scr[:, :, csl].rearrange("h p t -> p h t"))))
                ltoks.append(cld.issue(nc.sync.dma_start(
                    out=vtc[0:64, :], in_=vtm_scr[ch])))
                for tk in ltoks:
                    cld.wait(nc.tensor, tk)
                cld.wait(kb.v, ltoks[2])
                kv_ev = [None] * AH
                oc_ev = [None] * AH
                sm_ev = [None] * AH
                for h in range(AH):
                    hb = h % 2
                    qfh = qfc[:, h * ACH:(h + 1) * ACH]
                    kfh = kfc[:, h * ACH:(h + 1) * ACH]
                    vth = vtc[0:64, h * 128:(h + 1) * 128]
                    psS = ps[0 + hb]
                    psY = ps[2 + hb]
                    psKV = ps[4 + hb]
                    psT2 = ps[6 + hb]
                    if h >= 2:
                        nc.tensor.wait_ge(vo.h, sm_ev[h - 2])
                    im = nc.tensor.matmul(psS[0:64, 0:ACH], kfh, qfh, start=True, stop=True)
                    v_s = mm.inc(im, 1)
                    kb.v.wait_ge(mm.h, v_s)
                    kb.v.tensor_tensor(Ssb[0:64, hb * ACH:(hb + 1) * ACH], psS[0:64, 0:ACH],
                                       c["triu"][0:64, 0:64], ALU.mult)
                    sm_ev[h] = vo.n
                    if h >= 2:
                        nc.tensor.wait_ge(so.h, oc_ev[h - 2])
                    im = nc.tensor.matmul(psY[0:64, 0:128], qfh, KV[:, h * 128:(h + 1) * 128],
                                          start=True, stop=False)
                    nc.tensor.wait_ge(vo.h, sm_ev[h])
                    im = nc.tensor.matmul(psY[0:64, 0:128], Ssb[0:64, hb * ACH:(hb + 1) * ACH],
                                          vth, start=False, stop=True)
                    v_y = mm.inc(im, 1)
                    it = nc.tensor.transpose(psT2[0:64, 0:128], kfh, c["id_f32"])
                    v_t = mm.inc(it, 1)
                    kb.sc.wait_ge(mm.h, v_y)
                    kb.sc.activation(otm[0:64, h * 128:(h + 1) * 128], psY[0:64, 0:128],
                                     AF.Copy)
                    oc_ev[h] = so.n
                    kb.v.wait_ge(mm.h, v_t)
                    if h >= 2:
                        nc.tensor.wait_ge(vo.h, kv_ev[h - 2])
                    kb.v.tensor_copy(kft[0:64, hb * 128:(hb + 1) * 128], psT2[0:64, 0:128])
                    kt_ev = vo.n
                    nc.tensor.wait_ge(vo.h, kt_ev)
                    im = nc.tensor.matmul(psKV[:, 0:128], kft[0:64, hb * 128:(hb + 1) * 128],
                                          vth, start=True, stop=True)
                    v_kv = mm.inc(im, 1)
                    kb.v.wait_ge(mm.h, v_kv)
                    kb.v.tensor_tensor(KV[:, h * 128:(h + 1) * 128],
                                       KV[:, h * 128:(h + 1) * 128],
                                       psKV[:, 0:128], ALU.add)
                    kv_ev[h] = vo.n
                kb.sc.wait_ge(vo.h, vo.n)
                kb.sc.activation(osq[0:64, :], otm[0:64, :], AF.Square)
                kb.v.wait_ge(so.h, so.n)
                kb.v.tensor_reduce(rst[0:64, :],
                                   osq[0:64, :].rearrange("p (h d) -> p h d", h=AH),
                                   mybir.AxisListType.X, ALU.add)
                kb.v.tensor_scalar(rst[0:64, :], rst[0:64, :], 1.0 / 128, EPS_LN,
                                   ALU.mult, ALU.add)
                kb.sc.wait_ge(vo.h, vo.n)
                kb.sc.activation(rst[0:64, :], rst[0:64, :], AF.Sqrt)
                kb.v.wait_ge(so.h, so.n)
                kb.v.reciprocal(rst[0:64, :], rst[0:64, :])
                tr_ev = {}
                for h in range(AH):
                    hb = h % 2
                    kb.v.tensor_scalar(onm[0:64, hb * 128:(hb + 1) * 128],
                                       otm[0:64, h * 128:(h + 1) * 128],
                                       rst[0:64, h:h + 1], None, ALU.mult)
                    on_ev = vo.n
                    nc.tensor.wait_ge(vo.h, on_ev)
                    if h >= 2:
                        nc.tensor.wait_ge(vo.h, tr_ev[h - 2])
                    it = nc.tensor.transpose(ps[0 + hb][:, 0:64],
                                             onm[0:64, hb * 128:(hb + 1) * 128],
                                             c["id_f32"][0:64, 0:64])
                    v_t = mm.inc(it, 1)
                    kb.v.wait_ge(mm.h, v_t)
                    kb.v.tensor_copy(ofm[:, h * T + cg * ACH:h * T + (cg + 1) * ACH],
                                     ps[0 + hb][:, 0:64])
                    tr_ev[h] = vo.n
                kb.barrier()
                if cg == 7:
                    tok3 = slice(grp * T, (grp + 1) * T)
                    for d in range(AH):
                        dsl = slice(d * T, (d + 1) * T)
                        kb.v.tensor_copy(ohi[:, dsl], ofm[:, dsl])
                        kb.v.tensor_tensor(ofm[:, dsl], ofm[:, dsl], ohi[:, dsl], ALU.subtract)
                        kb.v.tensor_copy(olo[:, dsl], ofm[:, dsl])
                    cast_v = vo.n
                    with ExitStack() as l3:
                        wo = kb.sb(l3, "a3_wo", [128, 4 * 2048], BF16)
                        xd = kb.sb(l3, "a3_xd", [128, 2 * T], F32)
                        ot3 = kb.sb(l3, "a3_ot", [128, 2 * T], F32)
                        wo_ld, dmm, ev3, st3, xl3 = [], [], [], [], []
                        for d in range(DT):
                            buf = d % 2
                            whb = wo[:, buf * 2 * 2048:buf * 2 * 2048 + 2048]
                            wlb = wo[:, buf * 2 * 2048 + 2048:(buf + 1) * 2 * 2048]
                            if d >= 2:
                                nc.sync.wait_ge(mm.h, dmm[d - 2])
                            i1 = nc.sync.dma_start(out=whb, in_=wo_hi[d])
                            i2 = nc.sync.dma_start(out=wlb, in_=wo_lo[d])
                            wo_ld.append((wold.issue(i1), wold.issue(i2)))
                            wold.wait(nc.tensor, wo_ld[d][0])
                            wold.wait(nc.tensor, wo_ld[d][1])
                            if d == 0:
                                nc.tensor.wait_ge(vo.h, cast_v)
                            if d >= 2:
                                nc.tensor.wait_ge(vo.h, ev3[d - 2])
                            first = True
                            for wsrc, hsrc in ((whb, ohi), (whb, olo), (wlb, ohi)):
                                for kc in range(16):
                                    im = nc.tensor.matmul(
                                        ps[4 + buf][:, 0:T],
                                        wsrc[:, kc * 128:(kc + 1) * 128],
                                        hsrc[:, kc * T:(kc + 1) * T],
                                        start=first, stop=(wsrc is wlb and kc == 15))
                                    first = False
                            dmm.append(mm.inc(im, 1))
                            if d >= 2:
                                nc.sync.wait_ge(vo.h, ev3[d - 2])
                            xl3.append(xdld.issue(nc.sync.dma_start(
                                out=xd[:, buf * T:(buf + 1) * T], in_=s_in[d, :, tok3])))
                            kb.v.wait_ge(mm.h, dmm[d])
                            xdld.wait(kb.v, xl3[d])
                            if d >= 2:
                                outst.wait(kb.v, st3[d - 2])
                            kb.v.tensor_tensor(ot3[:, buf * T:(buf + 1) * T],
                                               ps[4 + buf][:, 0:T],
                                               xd[:, buf * T:(buf + 1) * T], ALU.add)
                            ev3.append(vo.n)
                            kb.sc.wait_ge(vo.h, ev3[d])
                            st3.append(outst.issue(kb.sc.dma_start(
                                out=s_out[d, :, tok3],
                                in_=ot3[:, buf * T:(buf + 1) * T])))
                        outst.wait_all(kb.sc)
                    kb.barrier()


def build_program(parts):
    nc = bass.Bass("TRN2", target_bir_lowering=False)
    stack = ExitStack()
    aps = {}

    def din(name, shape, dtype):
        aps[name] = nc.dram_tensor(name, list(shape), dtype, kind="ExternalInput").ap()
        return aps[name]

    def dout(name, shape, dtype):
        aps[name] = nc.dram_tensor(name, list(shape), dtype, kind="ExternalOutput").ap()
        return aps[name]

    def dint(name, shape, dtype):
        aps[name] = nc.dram_tensor(name, list(shape), dtype, kind="Internal").ap()
        return aps[name]

    cf_ap = din("c_f32_in", (128, CF_W), F32)
    cb_ap = din("c_bf_in", (128, CB_W), BF16)

    with stack:
        with nc.Block():
            kb = KB(nc, stack)
            need_lut = any(p.startswith("ghost") or p.startswith("ffn1") or p == "full"
                           for p in parts)
            lut_ap = din("lut_in", (1, 16), F32) if need_lut else None
            emit_consts(kb, cf_ap, cb_ap, lut_ap)

            for p in parts:
                if p == "quant":
                    wgu = din("wgu_t", (64, 128, 2 * 16 * 128), F32)
                    wpr = din("wproj_t", (16, 128, 64 * 128), F32)
                    qgu = dout("qgu", (64, 128, 2 * 16 * 128), BF16)
                    qpr = dout("qproj", (16, 128, 64 * 128), BF16)
                    emit_quant_phase(kb, wgu, qgu, 64, 2 * 16 * 128)
                    emit_quant_phase(kb, wpr, qpr, 16, 64 * 128)
                elif p == "full":
                    wgu = din("wgu_t", (64, 128, 2 * 16 * 128), F32)
                    wpr = din("wproj_t", (16, 128, 64 * 128), F32)
                    mpr = din("mproj_t", (81, 128, 2048), F32)
                    wor = din("wout_t", (16, 128, 4096), F32)
                    nwc = din("nw_t", (128, 32), F32)
                    cvt = din("conv_t", (128, 240), F32)
                    mvc = din("mvec", (128, 4), F32)
                    gidx = din("gidx_t", (GJ, 128, 16 * 128), I8)
                    uidx = din("uidx_t", (GJ, 128, 16 * 128), I8)
                    didx = din("didx_t", (16, 128, 5504), I8)
                    aw_in = {nm: din(nm + "_t", (16, 128, 2048), F32)
                             for nm in ("awq", "awk", "awv", "awo")}
                    nwa = din("anw_t", (128, 1), F32)
                    fm_in = din("afm_t", (1, 128, 512), F32)
                    ave = din("avec", (128, 4), F32)
                    sin = din("x_in", (DT, 128, NTOK), F32)
                    sout = dout("x_out", (DT, 128, NTOK), F32)
                    qgu = dint("qgu", (64, 128, 2 * 16 * 128), BF16)
                    qpr = dint("qproj", (16, 128, 64 * 128), BF16)
                    mph = dint("mp_hi", (81, 128, 2048), BF16)
                    mpl = dint("mp_lo", (81, 128, 2048), BF16)
                    woh = dint("wo_hi", (16, 128, 4096), BF16)
                    wol = dint("wo_lo", (16, 128, 4096), BF16)
                    gh = dint("gh", (GJ, 128, 16 * 128), BF16)
                    gl = dint("gl", (GJ, 128, 16 * 128), BF16)
                    uh = dint("uh", (GJ, 128, 16 * 128), BF16)
                    ul = dint("ul", (GJ, 128, 16 * 128), BF16)
                    dh = dint("dh", (16, 128, 5504), BF16)
                    dl = dint("dl", (16, 128, 5504), BF16)
                    aw = {}
                    for nm in ("awq", "awk", "awv", "awo"):
                        aw[nm + "_hi"] = dint(nm + "_hi", (16, 128, 2048), BF16)
                        aw[nm + "_lo"] = dint(nm + "_lo", (16, 128, 2048), BF16)
                    fmhi = dint("afm_hi", (1, 128, 512), BF16)
                    fmlo = dint("afm_lo", (1, 128, 512), BF16)
                    fhh = dint("fh_hi", (64, 128, T), BF16)
                    fhl = dint("fh_lo", (64, 128, T), BF16)
                    fh1h = dint("fh1_hi", (GJ, 128, T), BF16)
                    fh1l = dint("fh1_lo", (GJ, 128, T), BF16)
                    gsc = dint("m_gate_scr", (32, 128, NTOK), F32)
                    xsc = dint("m_x_scr", (32, 128, NTOK), F32)
                    bsc = dint("m_b_scr", (8, 128, NTOK), F32)
                    csc = dint("m_c_scr", (8, 128, NTOK), F32)
                    ghs = dint("m_gh_scr", (NCH, 128, 32 * MQ), BF16)
                    gls = dint("m_gl_scr", (NCH, 128, 32 * MQ), BF16)
                    qfs = dint("a_qf_scr", (AH, 128, NTOK), F32)
                    kfs = dint("a_kf_scr", (AH, 128, NTOK), F32)
                    vts = dint("a_vtm_scr", (NAC, 64, 2048), F32)
                    sA = dint("s_a", (DT, 128, NTOK), F32)
                    sB = dint("s_b", (DT, 128, NTOK), F32)
                    nw_sb = kb.sb(stack, "w_nw", [128, 32], F32)
                    nws = kb.sb(stack, "a_nw", [128, 1], F32)
                    tn1 = kb.ds("m_cld", 8).issue(nc.sync.dma_start(out=nw_sb[:], in_=nwc[:, :]))
                    tn2 = kb.ds("m_cld", 8).issue(nc.sync.dma_start(out=nws[:], in_=nwa[:, :]))
                    kb.ds("m_cld", 8).wait(kb.v, tn1)
                    kb.ds("m_cld", 8).wait(kb.v, tn2)
                    kb.barrier()
                    emit_quant_phase(kb, wgu, qgu, 64, 2 * 16 * 128)
                    emit_quant_phase(kb, wpr, qpr, 16, 64 * 128)
                    emit_split_phase(kb, mpr, mph, mpl, 81, 2048)
                    emit_split_phase(kb, wor, woh, wol, 16, 4096, scale_col=nw_sb, scale_w=128)
                    for nm in ("awq", "awk", "awv"):
                        emit_split_phase(kb, aw_in[nm], aw[nm + "_hi"], aw[nm + "_lo"], 16, 2048)
                    emit_split_phase(kb, aw_in["awo"], aw["awo_hi"], aw["awo_lo"], 16, 2048,
                                     scale_col=nws[:, 0:1])
                    gsc_scope = ExitStack()
                    gbufs = make_ghost_bufs(kb, gsc_scope)
                    kb.bg_gen = ghost_gen(kb, [
                        (gidx, gh, gl, GJ, 16 * 128),
                        (uidx, uh, ul, GJ, 16 * 128),
                        (didx[:, :, 0:2752], dh[:, :, 0:2752], dl[:, :, 0:2752], 16, 2752),
                        (didx[:, :, 2752:5504], dh[:, :, 2752:5504], dl[:, :, 2752:5504],
                         16, 2752),
                    ], gbufs)
                    emit_split_phase(kb, fm_in, fmhi, fmlo, 1, 512)
                    emit_ffn0(kb, sin, sA, qgu, qpr, fhh, fhl)
                    emit_mamba(kb, sA, sB, mph, mpl, woh, wol, cvt, mvc,
                               gsc, xsc, bsc, csc, ghs, gls)
                    emit_ffn0(kb, sB, sA, qgu, qpr, fhh, fhl)
                    kb.pump_all()
                    for dsn in ("gbg_sth", "gbg_stl"):
                        kb.ds(dsn).wait_all(nc.sync)
                    kb.barrier()
                    emit_ffn1(kb, sA, sB, gh, gl, uh, ul, dh, dl, fh1h, fh1l)
                    emit_attn(kb, sB, sA, aw["awq_hi"], aw["awq_lo"], aw["awk_hi"],
                              aw["awk_lo"], aw["awv_hi"], aw["awv_lo"], aw["awo_hi"],
                              aw["awo_lo"], fmhi, fmlo, ave, qfs, kfs, vts)
                    emit_ffn1(kb, sA, sout, gh, gl, uh, ul, dh, dl, fh1h, fh1l)
                    gsc_scope.close()
                elif p == "msplit":
                    mpr = din("mproj_t", (81, 128, 2048), F32)
                    mph = dout("mp_hi", (81, 128, 2048), BF16)
                    mpl = dout("mp_lo", (81, 128, 2048), BF16)
                    wor = din("wout_t", (16, 128, 4096), F32)
                    nwc = din("nw_t", (128, 32), F32)
                    woh = dout("wo_hi", (16, 128, 4096), BF16)
                    wol = dout("wo_lo", (16, 128, 4096), BF16)
                    nw_sb = kb.sb(stack, "w_nw", [128, 32], F32)
                    tnw = kb.ds("m_cld", 8).issue(nc.sync.dma_start(out=nw_sb[:], in_=nwc[:, :]))
                    kb.ds("m_cld", 8).wait(kb.v, tnw)
                    kb.barrier()
                    emit_split_phase(kb, mpr, mph, mpl, 81, 2048)
                    emit_split_phase(kb, wor, woh, wol, 16, 4096, scale_col=nw_sb, scale_w=128)
                elif p == "mamba":
                    sin = din("x_in", (DT, 128, NTOK), F32)
                    sout = dout("x_out", (DT, 128, NTOK), F32)
                    mph = din("mp_hi_in", (81, 128, 2048), BF16)
                    mpl = din("mp_lo_in", (81, 128, 2048), BF16)
                    woh = din("wo_hi_in", (16, 128, 4096), BF16)
                    wol = din("wo_lo_in", (16, 128, 4096), BF16)
                    cvt = din("conv_t", (128, 240), F32)
                    mvc = din("mvec", (128, 4), F32)
                    gsc = dint("m_gate_scr", (32, 128, NTOK), F32)
                    xsc = dint("m_x_scr", (32, 128, NTOK), F32)
                    bsc = dint("m_b_scr", (8, 128, NTOK), F32)
                    csc = dint("m_c_scr", (8, 128, NTOK), F32)
                    ghs = dint("m_gh_scr", (NCH, 128, 32 * MQ), BF16)
                    gls = dint("m_gl_scr", (NCH, 128, 32 * MQ), BF16)
                    emit_mamba(kb, sin, sout, mph, mpl, woh, wol, cvt, mvc,
                               gsc, xsc, bsc, csc, ghs, gls)
                elif p == "asplit":
                    names = ("awq", "awk", "awv", "awo")
                    for nm in names:
                        w_in = din(nm + "_t", (16, 128, 2048), F32)
                        whi = dout(nm + "_hi", (16, 128, 2048), BF16)
                        wlo = dout(nm + "_lo", (16, 128, 2048), BF16)
                        if nm == "awo":
                            nwa = din("anw_t", (128, 1), F32)
                            nws = kb.sb(stack, "a_nw", [128, 1], F32)
                            tn = kb.ds("m_cld", 8).issue(nc.sync.dma_start(out=nws[:], in_=nwa[:, :]))
                            kb.ds("m_cld", 8).wait(kb.v, tn)
                            kb.barrier()
                            emit_split_phase(kb, w_in, whi, wlo, 16, 2048, scale_col=nws[:, 0:1])
                        else:
                            emit_split_phase(kb, w_in, whi, wlo, 16, 2048)
                    fm_in = din("afm_t", (1, 128, 512), F32)
                    fmh = dout("afm_hi", (1, 128, 512), BF16)
                    fml = dout("afm_lo", (1, 128, 512), BF16)
                    emit_split_phase(kb, fm_in, fmh, fml, 1, 512)
                elif p == "attn":
                    sin = din("x_in", (DT, 128, NTOK), F32)
                    sout = dout("x_out", (DT, 128, NTOK), F32)
                    wqs = [din(f"{nm}_{sfx}_in", (16, 128, 2048), BF16)
                           for nm in ("awq", "awk", "awv", "awo") for sfx in ("hi", "lo")]
                    fmhi = din("afm_hi_in", (1, 128, 512), BF16)
                    fmlo = din("afm_lo_in", (1, 128, 512), BF16)
                    ave = din("avec", (128, 4), F32)
                    qfs = dint("a_qf_scr", (AH, 128, NTOK), F32)
                    kfs = dint("a_kf_scr", (AH, 128, NTOK), F32)
                    vts = dint("a_vtm_scr", (NAC, 64, 2048), F32)
                    emit_attn(kb, sin, sout, wqs[0], wqs[1], wqs[2], wqs[3],
                              wqs[4], wqs[5], wqs[6], wqs[7], fmhi, fmlo, ave,
                              qfs, kfs, vts)
                elif p == "ghostmat":
                    gidx = din("gidx_t", (GJ, 128, 16 * 128), I8)
                    uidx = din("uidx_t", (GJ, 128, 16 * 128), I8)
                    didx = din("didx_t", (16, 128, 5504), I8)
                    gh = dout("gh", (GJ, 128, 16 * 128), BF16)
                    gl = dout("gl", (GJ, 128, 16 * 128), BF16)
                    uh = dout("uh", (GJ, 128, 16 * 128), BF16)
                    ul = dout("ul", (GJ, 128, 16 * 128), BF16)
                    dh = dout("dh", (16, 128, 5504), BF16)
                    dl = dout("dl", (16, 128, 5504), BF16)
                    gsc_scope = ExitStack()
                    gbufs = make_ghost_bufs(kb, gsc_scope)
                    kb.gpsimd_bg = True
                    emit_ghost_phase(kb, gidx, gh, gl, GJ, 16 * 128, gbufs)
                    emit_ghost_phase(kb, uidx, uh, ul, GJ, 16 * 128, gbufs)
                    emit_ghost_phase(kb, didx[:, :, 0:2752], dh[:, :, 0:2752],
                                     dl[:, :, 0:2752], 16, 2752, gbufs)
                    emit_ghost_phase(kb, didx[:, :, 2752:5504], dh[:, :, 2752:5504],
                                     dl[:, :, 2752:5504], 16, 2752, gbufs)
                    kb.gpsimd_bg = False
                    for dsn in ("gbg_sth", "gbg_stl"):
                        kb.ds(dsn).wait_all(nc.sync)
                    kb.barrier()
                    gsc_scope.close()
                elif p == "ffn1":
                    sin = din("x_in", (DT, 128, NTOK), F32)
                    sout = dout("x_out", (DT, 128, NTOK), F32)
                    gh = din("gh_in", (GJ, 128, 16 * 128), BF16)
                    gl = din("gl_in", (GJ, 128, 16 * 128), BF16)
                    uh = din("uh_in", (GJ, 128, 16 * 128), BF16)
                    ul = din("ul_in", (GJ, 128, 16 * 128), BF16)
                    dh = din("dh_in", (16, 128, 5504), BF16)
                    dl = din("dl_in", (16, 128, 5504), BF16)
                    fhh = dint("fh1_hi", (GJ, 128, T), BF16)
                    fhl = dint("fh1_lo", (GJ, 128, T), BF16)
                    emit_ffn1(kb, sin, sout, gh, gl, uh, ul, dh, dl, fhh, fhl)
                elif p == "ffn0":
                    sin = din("x_in", (DT, 128, NTOK), F32)
                    sout = dout("x_out", (DT, 128, NTOK), F32)
                    qgu = din("qgu_in", (64, 128, 2 * 16 * 128), BF16)
                    qpr = din("qproj_in", (16, 128, 64 * 128), BF16)
                    fhh = dint("fh_hi", (64, 128, T), BF16)
                    fhl = dint("fh_lo", (64, 128, T), BF16)
                    emit_ffn0(kb, sin, sout, qgu, qpr, fhh, fhl)
                else:
                    raise ValueError(p)
    return nc, aps


def host_prep(inputs):
    z = inputs
    h = {}
    wguT = np.ascontiguousarray(z['L0_ffn_w_gate_up'].T)
    h['wgu_t'] = np.concatenate([pret(wguT[:, :8192]), pret(wguT[:, 8192:])], axis=2)
    h['wproj_t'] = pret(np.ascontiguousarray(z['L0_ffn_w_proj'].T))
    Wdt_p = np.zeros((2048, 128), np.float32)
    Wdt_p[:, :64] = z['L0_attn_Wdt']
    Wcat = np.concatenate([z['L0_attn_Wz'], z['L0_attn_Wx'], z['L0_attn_WB'],
                           z['L0_attn_WC'], Wdt_p], axis=1)
    h['mproj_t'] = pret(np.ascontiguousarray(Wcat))
    h['wout_t'] = pret(np.ascontiguousarray(z['L0_attn_Wout']))
    h['nw_t'] = np.ascontiguousarray(z['L0_attn_norm_w'].reshape(32, 128).T)
    conv_t = np.zeros((128, 240), np.float32)
    cw = z['L0_attn_conv_w'].reshape(48, 128, 4)
    cbv = z['L0_attn_conv_b'].reshape(48, 128)
    for ct in range(48):
        conv_t[:, ct * 4:(ct + 1) * 4] = cw[ct]
        conv_t[:, 192 + ct] = cbv[ct]
    h['conv_t'] = conv_t
    mvec = np.zeros((128, 4), np.float32)
    mvec[:64, 0] = z['L0_attn_dt_bias']
    mvec[:64, 1] = z['L0_attn_A_log']
    mvec[:64, 2] = z['L0_attn_D']
    h['mvec'] = mvec
    h['gidx_t'] = pret(np.ascontiguousarray(z['L1_ffn_gate_idx'].T)).astype(np.int8)
    h['uidx_t'] = pret(np.ascontiguousarray(z['L1_ffn_up_idx'].T)).astype(np.int8)
    h['didx_t'] = pret(np.ascontiguousarray(z['L1_ffn_down_idx'].T)).astype(np.int8)
    h['lut_in'] = np.ascontiguousarray(z['L1_ffn_lut'].reshape(1, 16))
    h['awq_t'] = pret(np.ascontiguousarray(z['L1_attn_wq'].T))
    h['awk_t'] = pret(np.ascontiguousarray(z['L1_attn_wk'].T))
    h['awv_t'] = pret(np.ascontiguousarray(z['L1_attn_wv'].T))
    h['awo_t'] = pret(np.ascontiguousarray(z['L1_attn_wo'].T))
    h['anw_t'] = np.ascontiguousarray(z['L1_attn_norm_w'].reshape(128, 1))
    h['afm_t'] = np.concatenate([np.ascontiguousarray(z['L1_attn_fq1_w'].T),
                                 np.ascontiguousarray(z['L1_attn_fq2_w'].T),
                                 np.ascontiguousarray(z['L1_attn_fk1_w'].T),
                                 np.ascontiguousarray(z['L1_attn_fk2_w'].T)], axis=1)[None]
    avec = np.zeros((128, 4), np.float32)
    avec[:, 0] = z['L1_attn_fq1_b']
    avec[:, 1] = z['L1_attn_fq2_b']
    avec[:, 2] = z['L1_attn_fk1_b']
    avec[:, 3] = z['L1_attn_fk2_b']
    h['avec'] = avec
    cf, cb = make_consts()
    h['c_f32_in'] = cf
    h['c_bf_in'] = cb
    return h


def kernel(**inputs):
    h = host_prep(inputs)
    x = inputs['L0_attn_hidden_states'].astype(np.float32, copy=False)
    nc, aps = build_program(['full'])
    in_maps = []
    for core in range(8):
        m = dict(h)
        m['x_in'] = tile_stream(np.ascontiguousarray(x[core // 2]))
        in_maps.append(m)
    res = run_bass_kernel_spmd(nc, in_maps, core_ids=list(range(8)))
    out = np.stack([untile_stream(np.asarray(res.results[2 * s]['x_out']).reshape(DT, 128, NTOK))
                    for s in range(4)], axis=0)
    return out.astype(np.float32, copy=False)


# version 06 (final)
# speedup vs baseline: 1.2155x; 1.2155x over previous
import os
os.environ.setdefault("GOMEMLIMIT", "20GiB")
os.environ.setdefault("GOGC", "30")

import numpy as np
import ml_dtypes
from contextlib import ExitStack

import concourse.bass as bass
import concourse.mybir as mybir
from concourse.bass_utils import run_bass_kernel_spmd

F32 = mybir.dt.float32
BF16 = mybir.dt.bfloat16
I32 = mybir.dt.int32
I8 = mybir.dt.int8
AF = mybir.ActivationFunctionType
ALU = mybir.AluOpType
BF = ml_dtypes.bfloat16

NTOK = 4096
D = 2048
DT = D // 128
T = 512
NT = NTOK // T
EPS_LN = 1e-5
EPS_RMS = float(np.finfo(np.float32).eps)
THR = float(np.float32(0.5 + 2.0 ** -10))
ENGINES = ("sync", "vector", "scalar", "tensor", "gpsimd")


def pret(wT):
    K, M = wT.shape
    KC, MT = K // 128, M // 128
    return np.ascontiguousarray(
        wT.reshape(KC, 128, MT, 128).transpose(2, 1, 0, 3).reshape(MT, 128, KC * 128))


def tile_stream(x):
    return np.ascontiguousarray(x.reshape(NTOK, DT, 128).transpose(1, 2, 0))


def untile_stream(s):
    return np.ascontiguousarray(np.asarray(s).transpose(2, 0, 1).reshape(NTOK, D))


CF_W = 520
CB_W = 264


def make_consts():
    cf = np.zeros((128, CF_W), np.float32)
    cb = np.zeros((128, CB_W), BF)
    cf[:, 0] = 1.0
    cf[0, 1:129] = 1.0
    cf[:, 129:257] = np.eye(128, dtype=np.float32)
    cf[:, 257:385] = np.tril(np.ones((128, 128), np.float32))
    cf[:, 385:513] = np.triu(np.ones((128, 128), np.float32))
    cb[:64, 0] = np.ones(64, BF)
    cb[64:, 1] = np.ones(64, BF)
    cb[0, 2:66] = np.ones(64, BF)
    cb[1, 66:130] = np.ones(64, BF)
    cb[:, 130:258] = np.eye(128, dtype=np.float32).astype(BF)
    return cf, cb


class Sem:
    def __init__(self, nc, stack, name):
        self.h = stack.enter_context(nc.semaphore(name))
        self.n = 0

    def inc(self, instr, k=1):
        instr.then_inc(self.h, k)
        self.n += k
        return self.n


class DS:

    def __init__(self, nc, stack, name, depth=2):
        self.sems = [Sem(nc, stack, f"{name}{i}") for i in range(depth)]
        self.depth = depth
        self.i = 0

    def issue(self, instr, k=16):
        s = self.sems[self.i % self.depth]
        self.i += 1
        v = s.inc(instr, k)
        return (s.h, v)

    def wait(self, eng, tok):
        eng.wait_ge(tok[0], tok[1])

    def wait_all(self, eng):
        for s in self.sems:
            eng.wait_ge(s.h, s.n)


FENCES = True

_NOFENCE = {"wait_ge", "sem_inc", "nop", "engine_nop", "drain", "wait_op",
            "dma_start", "dma_start_transpose", "indirect_dma_start", "collective_compute"}


class OrderedEng:

    def __init__(self, eng, sem):
        self._eng = eng
        self._sem = sem
        self._pending = None

    def __getattr__(self, name):
        eng = self._eng
        meth = getattr(eng, name)
        if name in _NOFENCE or name.startswith("reg") or not callable(meth):
            return meth

        def call(*a, **k):
            if FENCES and self._pending is not None:
                eng.wait_ge(self._sem.h, self._pending)
            self._pending = None
            instr = meth(*a, **k)
            self._sem.inc(instr, 1)
            self._pending = self._sem.n
            return instr
        return call


class KB:
    def __init__(self, nc, stack):
        self.nc = nc
        self.stack = stack
        self.bar = Sem(nc, stack, "bar")
        self.nbar = 0
        self.s = {n: Sem(nc, stack, n) for n in
                  ("ld0", "ld1", "ld2", "st0", "st1", "mm0", "mm1",
                   "ev0", "ev1", "ev2", "ev3", "gp0")}
        self.c = {}
        self.uid = 0
        self.ord = {n: Sem(nc, stack, f"ord_{n}") for n in ("vector", "scalar", "gpsimd")}
        self.v = OrderedEng(nc.vector, self.ord["vector"])
        self.sc = OrderedEng(nc.scalar, self.ord["scalar"])
        self.g = OrderedEng(nc.gpsimd, self.ord["gpsimd"])
        self._ds = {}

    def ds(self, name, depth=2):
        if name not in self._ds:
            self._ds[name] = DS(self.nc, self.stack, name, depth)
        return self._ds[name]

    def pump(self, n=1):
        g = getattr(self, "bg_gen", None)
        if g is None:
            return
        for _ in range(n):
            try:
                next(g)
            except StopIteration:
                self.bg_gen = None
                return

    def pump_all(self):
        while getattr(self, "bg_gen", None) is not None:
            self.pump(64)

    def sb(self, scope, name, shape, dtype):
        self.uid += 1
        return scope.enter_context(self.nc.sbuf_tensor(f"{name}_{self.uid}", shape, dtype))

    def ps(self, scope, name, shape):
        self.uid += 1
        return scope.enter_context(self.nc.psum_tensor(f"{name}_{self.uid}", shape, F32))

    def barrier(self):
        self.nbar += 1
        nc = self.nc
        wrapped = [("vector", self.v), ("scalar", self.sc)]
        if not getattr(self, "gpsimd_bg", False):
            wrapped.append(("gpsimd", self.g))
        tgt = self.bar.n + len(wrapped) + 2
        for name, w in wrapped:
            eng = getattr(nc, name)
            o = self.ord[name]
            if w._pending is not None:
                eng.wait_ge(o.h, w._pending)
                w._pending = None
            eng.wait_ge(self.bar.h, self.bar.n)
            eng.sem_inc(self.bar.h, 1)
        nc.tensor.wait_ge(self.s["mm0"].h, self.s["mm0"].n)
        nc.tensor.wait_ge(self.s["mm1"].h, self.s["mm1"].n)
        nc.tensor.wait_ge(self.bar.h, self.bar.n)
        nc.tensor.sem_inc(self.bar.h, 1)
        nc.sync.wait_ge(self.bar.h, self.bar.n)
        nc.sync.sem_inc(self.bar.h, 1)
        for name, _ in wrapped:
            getattr(nc, name).wait_ge(self.bar.h, tgt)
        nc.tensor.wait_ge(self.bar.h, tgt)
        nc.sync.wait_ge(self.bar.h, tgt)
        self.bar.n = tgt


def emit_consts(kb, cf_ap, cb_ap, lut_ap=None):
    nc, s = kb.nc, kb.s
    st = kb.stack
    vo, so = kb.ord["vector"], kb.ord["scalar"]
    cfs = st.enter_context(nc.sbuf_tensor("c_f32", [128, CF_W], F32))
    cbs = st.enter_context(nc.sbuf_tensor("c_bf", [128, CB_W], BF16))
    s["ld0"].inc(nc.sync.dma_start(out=cfs[:], in_=cf_ap[:, :]), 16)
    s["ld0"].inc(nc.sync.dma_start(out=cbs[:], in_=cb_ap[:, :]), 16)
    kb.c = {
        "ones_col": cfs[:, 0:1], "ones_row": cfs[0:1, 1:129],
        "id_f32": cfs[:, 129:257], "tril": cfs[:, 257:385], "triu": cfs[:, 385:513],
        "bd": cbs[:, 0:2], "bdT": cbs[0:2, 2:130], "id_bf": cbs[:, 130:258],
    }
    if lut_ap is not None:
        luts = st.enter_context(nc.sbuf_tensor("c_lut", [128, 48], F32))
        lutb = st.enter_context(nc.sbuf_tensor("c_lutb", [128, 16], BF16))
        psL_scope = ExitStack()
        psL = psL_scope.enter_context(nc.psum_tensor("c_psL", [128, 16], F32))
        v = s["ld0"].inc(nc.sync.dma_start(out=luts[0:1, 32:48], in_=lut_ap[:, :]), 16)
        nc.tensor.wait_ge(s["ld0"].h, s["ld0"].n)
        im = nc.tensor.matmul(psL[:], kb.c["ones_row"], luts[0:1, 32:48], start=True, stop=True)
        s["mm0"].inc(im, 1)
        kb.v.wait_ge(s["mm0"].h, s["mm0"].n)
        kb.v.tensor_copy(luts[:, 32:48], psL[:])
        kb.v.tensor_copy(lutb[:], luts[:, 32:48])
        kb.v.tensor_copy(luts[:, 0:16], lutb[:])
        kb.v.tensor_tensor(luts[:, 32:48], luts[:, 32:48], luts[:, 0:16], ALU.subtract)
        kb.v.tensor_copy(lutb[:], luts[:, 32:48])
        kb.v.tensor_copy(luts[:, 16:32], lutb[:])
        kb.c["lut"] = luts
        kb.v.wait_ge(s["mm0"].h, s["mm0"].n)
        psL_scope.close()
    else:
        kb.v.wait_ge(s["ld0"].h, s["ld0"].n)
    kb.barrier()


def emit_quant_phase(kb, w_f32_ap, wq_ap, n_blocks, blk_w):
    nc, s, c = kb.nc, kb.s, kb.c
    NS = blk_w // 512
    vo, so = kb.ord["vector"], kb.ord["scalar"]
    with ExitStack() as local:
        wf = kb.sb(local, "q_wf", [128, 2 * blk_w], F32)
        wb = kb.sb(local, "q_wb", [128, blk_w], BF16)
        ab = kb.sb(local, "q_ab", [128, blk_w], BF16)
        sgn = kb.sb(local, "q_sgn", [128, blk_w], F32)
        wq = kb.sb(local, "q_wq", [128, 2 * blk_w], BF16)
        sb = kb.sb(local, "q_sb", [128, 512], F32)
        thr = kb.sb(local, "q_thr", [128, 512], F32)
        tmp = kb.sb(local, "q_tmp", [128, 512], F32)
        tbf = kb.sb(local, "q_tbf", [128, 512], BF16)
        scr = kb.sb(local, "q_scr", [2, 512], BF16)
        ps1 = kb.ps(local, "q_ps1", [2, 512])
        ps2 = kb.ps(local, "q_ps2", [128, 512])

        mm = s["mm0"]
        wld = kb.ds("q_wld")
        wst = kb.ds("q_wst")
        ld_done, v_done, st_done, s_done = [], [], [], []
        for j in range(n_blocks):
            buf = j % 2
            wfb = wf[:, buf * blk_w:(buf + 1) * blk_w]
            wqb = wq[:, buf * blk_w:(buf + 1) * blk_w]
            if j >= 2:
                nc.sync.wait_ge(vo.h, v_done[j - 2])
            ld_done.append(wld.issue(nc.sync.dma_start(out=wfb, in_=w_f32_ap[j])))
            wld.wait(kb.v, ld_done[j])
            if j >= 1:
                kb.v.wait_ge(so.h, s_done[j - 1])
            if j >= 2:
                wst.wait(kb.v, st_done[j - 2])
            kb.v.tensor_copy(wb[:], wfb)
            kb.v.tensor_copy(wfb, wb[:])
            wbf32_done = vo.n
            kb.sc.wait_ge(vo.h, wbf32_done)
            kb.sc.activation(ab[:], wb[:], AF.Abs)
            kb.sc.activation(sgn[:], wb[:], AF.Sign)
            s_done.append(so.n)
            for sl in range(NS):
                ssl = slice(sl * 512, (sl + 1) * 512)
                nc.tensor.wait_ge(so.h, s_done[j])
                nc.tensor.wait_ge(vo.h, vo.n)
                im = nc.tensor.matmul(ps1[:], c["bd"], ab[:, ssl], start=True, stop=True)
                mm.inc(im, 1)
                kb.v.wait_ge(mm.h, mm.n)
                kb.v.tensor_scalar(scr[:], ps1[:], 1.0 / 64, 1e-8, ALU.mult, ALU.max)
                nc.tensor.wait_ge(vo.h, vo.n)
                im2 = nc.tensor.matmul(ps2[:], c["bdT"], scr[:], start=True, stop=True)
                mm.inc(im2, 1)
                kb.v.wait_ge(mm.h, mm.n)
                kb.v.tensor_copy(sb[:], ps2[:])
                kb.v.tensor_scalar(thr[:], sb[:], THR, None, ALU.mult)
                kb.v.tensor_tensor(tmp[:], ab[:, ssl], thr[:], ALU.is_gt)
                kb.v.tensor_tensor(tmp[:], tmp[:], sgn[:, ssl], ALU.mult)
                kb.v.tensor_tensor(tmp[:], tmp[:], sb[:], ALU.mult)
                kb.v.tensor_tensor(tmp[:], tmp[:], wfb[:, ssl], ALU.subtract)
                kb.v.tensor_copy(tbf[:], tmp[:])
                kb.v.tensor_tensor(wqb[:, ssl], wfb[:, ssl], tbf[:], ALU.add)
            v_done.append(vo.n)
            if j >= 2:
                wst.wait(kb.sc, st_done[j - 2])
            kb.sc.wait_ge(vo.h, v_done[j])
            st_done.append(wst.issue(kb.sc.dma_start(out=wq_ap[j], in_=wqb)))
        wst.wait_all(kb.sc)
    kb.barrier()


def make_ghost_bufs(kb, scope, max_w=2752):
    return {
        "idx": kb.sb(scope, "g_idx", [128, max_w], I8),
        "eq": kb.sb(scope, "g_eq", [128, max_w], BF16),
        "hi": kb.sb(scope, "g_hi", [128, max_w], BF16),
        "lo": kb.sb(scope, "g_lo", [128, max_w], BF16),
    }


def make_split_bufs(kb, scope):
    return {
        "wf": kb.sb(scope, "sg_wf", [128, 4096], F32),
        "hi": kb.sb(scope, "sg_hi", [128, 4096], BF16),
        "lo": kb.sb(scope, "sg_lo", [128, 4096], BF16),
    }


def split_gen(kb, calls, bufs):
    nc, s, c = kb.nc, kb.s, kb.c
    vo = kb.ord["vector"]
    wf, hi, lo = bufs["wf"], bufs["hi"], bufs["lo"]
    wld = kb.ds("sbg_ld")
    sth = kb.ds("sbg_sth")
    stl = kb.ds("sbg_stl")
    prev = None
    for w_ap, whi_ap, wlo_ap, n_blocks, blk_w, scale_col, scale_w in calls:
        for j in range(n_blocks):
            wfb = wf[:, 0:blk_w]
            hib = hi[:, 0:blk_w]
            lob = lo[:, 0:blk_w]
            if prev is not None:
                nc.sync.wait_ge(vo.h, prev[0])
            ld = wld.issue(nc.sync.dma_start(out=wfb, in_=w_ap[j]))
            wld.wait(kb.v, ld)
            if prev is not None:
                sth.wait(kb.v, prev[1])
                stl.wait(kb.v, prev[2])
            if scale_col is not None:
                if scale_w is None:
                    kb.v.tensor_scalar(wfb, wfb, scale_col, None, ALU.mult)
                else:
                    for kc in range(blk_w // scale_w):
                        kb.v.tensor_scalar(wfb[:, kc * scale_w:(kc + 1) * scale_w],
                                           wfb[:, kc * scale_w:(kc + 1) * scale_w],
                                           scale_col[:, kc:kc + 1], None, ALU.mult)
                yield
            kb.v.tensor_copy(hib, wfb)
            kb.v.tensor_tensor(wfb, wfb, hib, ALU.subtract)
            kb.v.tensor_copy(lob, wfb)
            vdone = vo.n
            kb.sc.wait_ge(vo.h, vdone)
            t1 = sth.issue(kb.sc.dma_start(out=whi_ap[j], in_=hib))
            t2 = stl.issue(kb.sc.dma_start(out=wlo_ap[j], in_=lob))
            prev = (vdone, t1, t2)
            yield


def ghost_gen(kb, calls, bufs):
    nc, s, c = kb.nc, kb.s, kb.c
    lut = c["lut"]
    vo = kb.ord["vector"]
    idx, eq, hi, lo = bufs["idx"], bufs["eq"], bufs["hi"], bufs["lo"]
    gld = kb.ds("gbg_ld")
    gsh = kb.ds("gbg_sth")
    gsl = kb.ds("gbg_stl")
    prev_v_done = None
    prev_st = None
    for idx_ap, whi_ap, wlo_ap, n_blocks, blk_w in calls:
        for j in range(n_blocks):
            idxb = idx[:, 0:blk_w]
            hib = hi[:, 0:blk_w]
            lob = lo[:, 0:blk_w]
            if prev_v_done is not None:
                nc.sync.wait_ge(vo.h, prev_v_done)
            ld = gld.issue(nc.sync.dma_start(out=idxb, in_=idx_ap[j]))
            gld.wait(kb.v, ld)
            if prev_st is not None:
                gsh.wait(kb.v, prev_st[0])
                gsl.wait(kb.v, prev_st[1])
            kb.v.memset(hib, 0.0)
            kb.v.memset(lob, 0.0)
            yield
            for v in range(16):
                kb.v.tensor_scalar(eq[:, 0:blk_w], idxb, float(v), None, ALU.is_equal)
                kb.v.scalar_tensor_tensor(hib, eq[:, 0:blk_w], lut[:, v:v + 1], hib,
                                          ALU.mult, ALU.add)
                kb.v.scalar_tensor_tensor(lob, eq[:, 0:blk_w], lut[:, 16 + v:17 + v], lob,
                                          ALU.mult, ALU.add)
                yield
            prev_v_done = vo.n
            kb.sc.wait_ge(vo.h, prev_v_done)
            t1 = gsh.issue(kb.sc.dma_start(out=whi_ap[j], in_=hib))
            t2 = gsl.issue(kb.sc.dma_start(out=wlo_ap[j], in_=lob))
            prev_st = (t1, t2)
            yield


def emit_ln(kb, x_sb, sq, hhi, hlo, psA, psB, psC, brow, bfull, n_d=DT, tw=T):
    nc, s, c = kb.nc, kb.s, kb.c
    vo, so = kb.ord["vector"], kb.ord["scalar"]
    dinv = 1.0 / (n_d * 128)
    mm = s["mm0"]
    mm_done = []
    for d in range(n_d):
        xd = x_sb[:, d * tw:(d + 1) * tw]
        if d >= 1:
            kb.sc.wait_ge(mm.h, mm_done[d - 1])
        kb.sc.activation(sq[:], xd, AF.Square)
        sq_done = so.n
        im = nc.tensor.matmul(psA[:], c["ones_col"], xd,
                              start=(d == 0), stop=(d == n_d - 1))
        mm.inc(im, 1)
        nc.tensor.wait_ge(so.h, sq_done)
        im2 = nc.tensor.matmul(psB[:], c["ones_col"], sq[:],
                               start=(d == 0), stop=(d == n_d - 1))
        mm_done.append(mm.inc(im2, 1))
    row_m = brow[0:1, 0:tw]
    row_r = brow[0:1, tw:2 * tw]
    kb.v.wait_ge(mm.h, mm.n)
    kb.v.tensor_scalar(row_m, psA[:], dinv, None, ALU.mult)
    kb.v.tensor_scalar(row_r, psB[:], dinv, None, ALU.mult)
    kb.v.tensor_tensor(psB[:], row_m, row_m, ALU.mult)
    kb.v.tensor_tensor(row_r, row_r, psB[:], ALU.subtract)
    kb.v.tensor_scalar(row_r, row_r, EPS_LN, None, ALU.add)
    kb.sc.wait_ge(vo.h, vo.n)
    kb.sc.activation(row_r, row_r, AF.Sqrt)
    kb.v.wait_ge(so.h, so.n)
    kb.v.reciprocal(row_r, row_r)
    nc.tensor.wait_ge(vo.h, vo.n)
    nc.tensor.matmul(psC[:, 0:tw], c["ones_row"], row_m, start=True, stop=True)
    ib2 = nc.tensor.matmul(psC[:, tw:2 * tw], c["ones_row"], row_r, start=True, stop=True)
    mm.inc(ib2, 2)
    kb.v.wait_ge(mm.h, mm.n)
    kb.v.tensor_copy(bfull[:], psC[:, 0:2 * tw])
    for d in range(n_d):
        xd = x_sb[:, d * tw:(d + 1) * tw]
        hd = hhi[:, d * tw:(d + 1) * tw]
        kb.v.tensor_tensor(xd, xd, bfull[:, 0:tw], ALU.subtract)
        kb.v.tensor_tensor(xd, xd, bfull[:, tw:2 * tw], ALU.mult)
        kb.v.tensor_copy(hd, xd)
        kb.v.tensor_tensor(xd, xd, hd, ALU.subtract)
        kb.v.tensor_copy(hlo[:, d * tw:(d + 1) * tw], xd)
    return kb.ord["vector"].n


def emit_ffn0(kb, s_in, s_out, qgu_ap, qproj_ap, fh_hi_ap, fh_lo_ap):
    nc, s, c = kb.nc, kb.s, kb.c
    GW = 2 * 16 * 128
    vo, so = kb.ord["vector"], kb.ord["scalar"]
    mm = s["mm0"]
    xld = kb.ds("f_xld", 1)
    wld = kb.ds("f_wld")
    sph = kb.ds("f_sph")
    spl = kb.ds("f_spl")
    rbh = kb.ds("f_rbh", 1)
    rbl = kb.ds("f_rbl", 1)
    wpld = kb.ds("f_wpld")
    xdld = kb.ds("f_xdld")
    outst = kb.ds("f_outst")
    slscope = ExitStack()
    kb.r2 = kb.sb(slscope, "f_r2", [128, T], F32)
    for t in range(NT):
        tok = slice(t * T, (t + 1) * T)
        ab_scope = ExitStack()
        kb.hhi = kb.sb(ab_scope, "f_hhi", [128, DT * T], BF16)
        kb.hlo = kb.sb(ab_scope, "f_hlo", [128, DT * T], BF16)
        with ExitStack() as lo_:
            x_sb = kb.sb(lo_, "fA_x", [128, (DT + 1) * T], F32)
            brow = kb.sb(lo_, "fA_brow", [1, 2 * T], F32)
            bfull = kb.sb(lo_, "fA_bfull", [128, 2 * T], F32)
            psA = kb.ps(lo_, "fA_psA", [1, T])
            psB = kb.ps(lo_, "fA_psB", [1, T])
            psC = kb.ps(lo_, "fA_psC", [128, 2 * T])
            v_ld = xld.issue(nc.sync.dma_start(
                out=x_sb[:, 0:DT * T].rearrange("p (d t) -> p d t", d=DT),
                in_=s_in[:, :, tok].rearrange("d p t -> p d t")))
            xld.wait(kb.v, v_ld)
            xld.wait(kb.sc, v_ld)
            xld.wait(nc.tensor, v_ld)
            emit_ln(kb, x_sb, x_sb[:, DT * T:], kb.hhi, kb.hlo, psA, psB, psC, brow, bfull)
            kb.barrier()
        with ExitStack() as lo_:
            wq = kb.sb(lo_, "fB_wq", [128, 2 * GW], BF16)
            hf = kb.sb(lo_, "fB_hf", [128, 2 * T], F32)
            ht = kb.sb(lo_, "fB_ht", [128, 2 * T], F32)
            sqh = kb.sb(lo_, "fB_sqh", [128, 2 * T], F32)
            fhh = kb.sb(lo_, "fB_fhh", [128, 2 * T], BF16)
            fhl = kb.sb(lo_, "fB_fhl", [128, 2 * T], BF16)
            rrow = kb.sb(lo_, "fB_rrow", [1, T], F32)
            psG = kb.ps(lo_, "fB_psG", [128, 2 * T])
            psU = kb.ps(lo_, "fB_psU", [128, 2 * T])
            psH = kb.ps(lo_, "fB_psH", [1, T])
            wq_ld, mm_done, silu_done, hm_done, cast_done, sq_done, smm_done, st_done = \
                [], [], [], [], [], [], [], []
            hhi, hlo = kb.hhi, kb.hlo
            for j in range(64):
                buf = j % 2
                wqb = wq[:, buf * GW:(buf + 1) * GW]
                tsl = slice(buf * T, (buf + 1) * T)
                if j >= 2:
                    nc.sync.wait_ge(mm.h, mm_done[j - 2])
                wq_ld.append(wld.issue(nc.sync.dma_start(out=wqb, in_=qgu_ap[j])))
                wld.wait(nc.tensor, wq_ld[j])
                if j >= 2:
                    nc.tensor.wait_ge(vo.h, hm_done[j - 2])
                    nc.tensor.wait_ge(so.h, silu_done[j - 2])
                for g, ps in ((0, psG), (1, psU)):
                    for p_, hsrc in ((0, hhi), (1, hlo)):
                        for kc in range(16):
                            im = nc.tensor.matmul(
                                ps[:, tsl],
                                wqb[:, g * 16 * 128 + kc * 128:g * 16 * 128 + (kc + 1) * 128],
                                hsrc[:, kc * T:(kc + 1) * T],
                                start=(p_ == 0 and kc == 0), stop=(p_ == 1 and kc == 15))
                mm_done.append(mm.inc(im, 1))
                if j >= 1:
                    jm = j - 1
                    nc.tensor.wait_ge(so.h, sq_done[jm])
                    ism = nc.tensor.matmul(psH[:], c["ones_col"],
                                           sqh[:, jm % 2 * T:(jm % 2 + 1) * T],
                                           start=(jm == 0), stop=False)
                    smm_done.append(mm.inc(ism, 1))
                kb.sc.wait_ge(mm.h, mm_done[j])
                if j >= 2:
                    kb.sc.wait_ge(vo.h, cast_done[j - 2])
                kb.sc.activation(hf[:, tsl], psG[:, tsl], AF.Silu)
                silu_done.append(so.n)
                kb.v.wait_ge(so.h, silu_done[j])
                if j >= 2:
                    kb.v.wait_ge(so.h, sq_done[j - 2])
                    sph.wait(kb.v, st_done[j - 2][0])
                    spl.wait(kb.v, st_done[j - 2][1])
                kb.v.tensor_tensor(ht[:, tsl], hf[:, tsl], psU[:, tsl], ALU.mult)
                hm_done.append(vo.n)
                kb.v.tensor_copy(fhh[:, tsl], ht[:, tsl])
                kb.v.tensor_tensor(hf[:, tsl], ht[:, tsl], fhh[:, tsl], ALU.subtract)
                kb.v.tensor_copy(fhl[:, tsl], hf[:, tsl])
                cast_done.append(vo.n)
                kb.sc.wait_ge(vo.h, hm_done[j])
                if j >= 2:
                    kb.sc.wait_ge(mm.h, smm_done[j - 1])
                kb.sc.activation(sqh[:, tsl], ht[:, tsl], AF.Square)
                sq_done.append(so.n)
                kb.sc.wait_ge(vo.h, cast_done[j])
                t1 = sph.issue(kb.sc.dma_start(out=fh_hi_ap[j], in_=fhh[:, tsl]))
                t2 = spl.issue(kb.sc.dma_start(out=fh_lo_ap[j], in_=fhl[:, tsl]))
                st_done.append((t1, t2))
                kb.pump(2)
            nc.tensor.wait_ge(so.h, sq_done[63])
            ism = nc.tensor.matmul(psH[:], c["ones_col"], sqh[:, T:2 * T],
                                   start=False, stop=True)
            mm.inc(ism, 1)
            kb.v.wait_ge(mm.h, mm.n)
            kb.v.tensor_scalar(rrow[:], psH[:], 1.0 / 8192, EPS_RMS, ALU.mult, ALU.add)
            kb.sc.wait_ge(vo.h, vo.n)
            kb.sc.activation(rrow[:], rrow[:], AF.Sqrt)
            kb.v.wait_ge(so.h, so.n)
            kb.v.reciprocal(rrow[:], rrow[:])
            nc.tensor.wait_ge(vo.h, vo.n)
            ib = nc.tensor.matmul(psG[:, 0:T], c["ones_row"], rrow[:], start=True, stop=True)
            mm.inc(ib, 1)
            kb.v.wait_ge(mm.h, mm.n)
            kb.v.tensor_copy(kb.r2[:], psG[:, 0:T])
            sph.wait_all(kb.sc)
            spl.wait_all(kb.sc)
            kb.barrier()
        ab_scope.close()
        with ExitStack() as lo_:
            fhb = kb.sb(lo_, "fD_fhb", [128, 64 * T], BF16)
            flb = kb.sb(lo_, "fD_flb", [128, 64 * T], BF16)
            wp = kb.sb(lo_, "fD_wp", [128, 2 * 64 * 128], BF16)
            xd = kb.sb(lo_, "fD_xd", [128, 2 * T], F32)
            ot = kb.sb(lo_, "fD_ot", [128, 2 * T], F32)
            psD = kb.ps(lo_, "fD_psD", [128, 2 * T])
            v1 = rbh.issue(nc.sync.dma_start(
                out=fhb[:].rearrange("p (j t) -> p j t", j=64),
                in_=fh_hi_ap[0:64].rearrange("j p t -> p j t")))
            v2 = rbl.issue(nc.sync.dma_start(
                out=flb[:].rearrange("p (j t) -> p j t", j=64),
                in_=fh_lo_ap[0:64].rearrange("j p t -> p j t")))
            wp_ld, dmm_done, out_done, st_dn, xd_ld = [], [], [], [], []
            for d in range(DT):
                buf = d % 2
                wpb = wp[:, buf * 64 * 128:(buf + 1) * 64 * 128]
                if d >= 2:
                    nc.sync.wait_ge(mm.h, dmm_done[d - 2])
                wp_ld.append(wpld.issue(nc.sync.dma_start(out=wpb, in_=qproj_ap[d])))
                wpld.wait(nc.tensor, wp_ld[d])
                if d == 0:
                    rbh.wait(nc.tensor, v1)
                    rbl.wait(nc.tensor, v2)
                if d >= 2:
                    nc.tensor.wait_ge(vo.h, out_done[d - 2])
                first = True
                for hsrc in (fhb, flb):
                    for kc in range(64):
                        im = nc.tensor.matmul(
                            psD[:, buf * T:(buf + 1) * T],
                            wpb[:, kc * 128:(kc + 1) * 128],
                            hsrc[:, kc * T:(kc + 1) * T],
                            start=first, stop=(hsrc is flb and kc == 63))
                        first = False
                dmm_done.append(mm.inc(im, 1))
                if d >= 2:
                    nc.sync.wait_ge(vo.h, out_done[d - 2])
                xd_ld.append(xdld.issue(nc.sync.dma_start(
                    out=xd[:, buf * T:(buf + 1) * T], in_=s_in[d, :, tok])))
                kb.v.wait_ge(mm.h, dmm_done[d])
                xdld.wait(kb.v, xd_ld[d])
                if d >= 2:
                    outst.wait(kb.v, st_dn[d - 2])
                kb.v.tensor_tensor(ot[:, buf * T:(buf + 1) * T],
                                   psD[:, buf * T:(buf + 1) * T], kb.r2[:], ALU.mult)
                kb.v.scalar_tensor_tensor(
                    ot[:, buf * T:(buf + 1) * T], ot[:, buf * T:(buf + 1) * T], 0.5,
                    xd[:, buf * T:(buf + 1) * T], ALU.mult, ALU.add)
                out_done.append(vo.n)
                kb.sc.wait_ge(vo.h, out_done[d])
                st_dn.append(outst.issue(kb.sc.dma_start(
                    out=s_out[d, :, tok], in_=ot[:, buf * T:(buf + 1) * T])))
                kb.pump(6)
            outst.wait_all(kb.sc)
            kb.barrier()
    slscope.close()


def emit_split_phase(kb, w_ap, whi_ap, wlo_ap, n_blocks, blk_w, scale_col=None, scale_w=None):
    nc, s, c = kb.nc, kb.s, kb.c
    vo = kb.ord["vector"]
    with ExitStack() as local:
        wf = kb.sb(local, "s_wf", [128, 2 * blk_w], F32)
        hi = kb.sb(local, "s_hi", [128, 2 * blk_w], BF16)
        lo = kb.sb(local, "s_lo", [128, 2 * blk_w], BF16)
        tmp = kb.sb(local, "s_tmp", [128, blk_w], F32)
        wld = kb.ds("q_wld")
        sth = kb.ds("g_sth")
        stl = kb.ds("g_stl")
        ld_done, v_done, st_done = [], [], []
        for j in range(n_blocks):
            buf = j % 2
            wfb = wf[:, buf * blk_w:(buf + 1) * blk_w]
            hib = hi[:, buf * blk_w:(buf + 1) * blk_w]
            lob = lo[:, buf * blk_w:(buf + 1) * blk_w]
            if j >= 2:
                nc.sync.wait_ge(vo.h, v_done[j - 2])
            ld_done.append(wld.issue(nc.sync.dma_start(out=wfb, in_=w_ap[j])))
            wld.wait(kb.v, ld_done[j])
            if j >= 2:
                sth.wait(kb.v, st_done[j - 2][0])
                stl.wait(kb.v, st_done[j - 2][1])
            if scale_col is not None:
                if scale_w is None:
                    kb.v.tensor_scalar(wfb, wfb, scale_col, None, ALU.mult)
                else:
                    for kc in range(blk_w // scale_w):
                        kb.v.tensor_scalar(wfb[:, kc * scale_w:(kc + 1) * scale_w],
                                           wfb[:, kc * scale_w:(kc + 1) * scale_w],
                                           scale_col[:, kc:kc + 1], None, ALU.mult)
            kb.v.tensor_copy(hib, wfb)
            kb.v.tensor_tensor(tmp[:], wfb, hib, ALU.subtract)
            kb.v.tensor_copy(lob, tmp[:])
            v_done.append(vo.n)
            kb.sc.wait_ge(vo.h, v_done[j])
            if j >= 2:
                sth.wait(kb.sc, st_done[j - 2][0])
                stl.wait(kb.sc, st_done[j - 2][1])
            t1 = sth.issue(kb.sc.dma_start(out=whi_ap[j], in_=hib))
            t2 = stl.issue(kb.sc.dma_start(out=wlo_ap[j], in_=lob))
            st_done.append((t1, t2))
        sth.wait_all(kb.sc)
        stl.wait_all(kb.sc)
    kb.barrier()


GJ = 43


def emit_ffn1(kb, s_in, s_out, gh_ap, gl_ap, uh_ap, ul_ap, dh_ap, dl_ap,
              fh_hi_ap, fh_lo_ap):
    nc, s, c = kb.nc, kb.s, kb.c
    vo, so = kb.ord["vector"], kb.ord["scalar"]
    mm = s["mm0"]
    xld = kb.ds("f_xld", 1)
    wld = kb.ds("g_wld", 4)
    wld2 = kb.ds("g_wld2", 4)
    sph = kb.ds("f_sph")
    spl = kb.ds("f_spl")
    rbh = kb.ds("f_rbh", 1)
    rbl = kb.ds("f_rbl", 1)
    wpld = kb.ds("f_wpld")
    wpld2 = kb.ds("f_wpld2")
    xdld = kb.ds("f_xdld")
    outst = kb.ds("f_outst")
    KW = 16 * 128
    slscope = ExitStack()
    kb.hhi = kb.sb(slscope, "g_hhi", [128, DT * T], BF16)
    kb.hlo = kb.sb(slscope, "g_hlo", [128, DT * T], BF16)
    for t in range(NT):
        tok = slice(t * T, (t + 1) * T)
        with ExitStack() as lo_:
            x_sb = kb.sb(lo_, "gA_x", [128, (DT + 1) * T], F32)
            brow = kb.sb(lo_, "gA_brow", [1, 2 * T], F32)
            bfull = kb.sb(lo_, "gA_bfull", [128, 2 * T], F32)
            psA = kb.ps(lo_, "gA_psA", [1, T])
            psB = kb.ps(lo_, "gA_psB", [1, T])
            psC = kb.ps(lo_, "gA_psC", [128, 2 * T])
            v_ld = xld.issue(nc.sync.dma_start(
                out=x_sb[:, 0:DT * T].rearrange("p (d t) -> p d t", d=DT),
                in_=s_in[:, :, tok].rearrange("d p t -> p d t")))
            xld.wait(kb.v, v_ld)
            xld.wait(kb.sc, v_ld)
            xld.wait(nc.tensor, v_ld)
            emit_ln(kb, x_sb, x_sb[:, DT * T:], kb.hhi, kb.hlo, psA, psB, psC, brow, bfull)
            kb.barrier()
        with ExitStack() as lo_:
            wg = kb.sb(lo_, "gB_wg", [128, 4 * KW], BF16)
            wu = kb.sb(lo_, "gB_wu", [128, 4 * KW], BF16)
            hf = kb.sb(lo_, "gB_hf", [128, 2 * T], F32)
            ht = kb.sb(lo_, "gB_ht", [128, 2 * T], F32)
            fhh = kb.sb(lo_, "gB_fhh", [128, 2 * T], BF16)
            fhl = kb.sb(lo_, "gB_fhl", [128, 2 * T], BF16)
            psG = kb.ps(lo_, "gB_psG", [128, 2 * T])
            psU = kb.ps(lo_, "gB_psU", [128, 2 * T])
            wq_ld, wu_ld, mm_done, silu_done, hm_done, cast_done, st_done =                 [], [], [], [], [], [], []
            hhi, hlo = kb.hhi, kb.hlo
            for j in range(GJ):
                buf = j % 2
                wgb = wg[:, buf * 2 * KW:(buf + 1) * 2 * KW]
                wub = wu[:, buf * 2 * KW:(buf + 1) * 2 * KW]
                tsl = slice(buf * T, (buf + 1) * T)
                if j >= 2:
                    nc.sync.wait_ge(mm.h, mm_done[j - 2])
                i1 = nc.sync.dma_start(out=wgb[:, 0:KW], in_=gh_ap[j])
                i2 = nc.sync.dma_start(out=wgb[:, KW:2 * KW], in_=gl_ap[j])
                wq_ld.append((wld.issue(i1), wld.issue(i2)))
                i3 = nc.sync.dma_start(out=wub[:, 0:KW], in_=uh_ap[j])
                i4 = nc.sync.dma_start(out=wub[:, KW:2 * KW], in_=ul_ap[j])
                wu_ld.append((wld2.issue(i3), wld2.issue(i4)))
                wld.wait(nc.tensor, wq_ld[j][0])
                wld.wait(nc.tensor, wq_ld[j][1])
                wld2.wait(nc.tensor, wu_ld[j][0])
                wld2.wait(nc.tensor, wu_ld[j][1])
                if j >= 2:
                    nc.tensor.wait_ge(vo.h, hm_done[j - 2])
                    nc.tensor.wait_ge(so.h, silu_done[j - 2])
                for ps, wb in ((psG, wgb), (psU, wub)):
                    first = True
                    for ph, pl, hsrc in ((0, 0, hhi), (0, 1, hlo), (1, 0, hhi)):
                        woff = pl * 0 + ph * KW
                        for kc in range(16):
                            im = nc.tensor.matmul(
                                ps[:, tsl],
                                wb[:, woff + kc * 128:woff + (kc + 1) * 128],
                                hsrc[:, kc * T:(kc + 1) * T],
                                start=first, stop=(ph == 1 and kc == 15))
                            first = False
                mm_done.append(mm.inc(im, 1))
                kb.sc.wait_ge(mm.h, mm_done[j])
                if j >= 2:
                    kb.sc.wait_ge(vo.h, cast_done[j - 2])
                kb.sc.activation(hf[:, tsl], psG[:, tsl], AF.Silu)
                silu_done.append(so.n)
                kb.v.wait_ge(so.h, silu_done[j])
                if j >= 2:
                    sph.wait(kb.v, st_done[j - 2][0])
                    spl.wait(kb.v, st_done[j - 2][1])
                kb.v.tensor_tensor(ht[:, tsl], hf[:, tsl], psU[:, tsl], ALU.mult)
                hm_done.append(vo.n)
                kb.v.tensor_copy(fhh[:, tsl], ht[:, tsl])
                kb.v.tensor_tensor(hf[:, tsl], ht[:, tsl], fhh[:, tsl], ALU.subtract)
                kb.v.tensor_copy(fhl[:, tsl], hf[:, tsl])
                cast_done.append(vo.n)
                kb.sc.wait_ge(vo.h, cast_done[j])
                t1 = sph.issue(kb.sc.dma_start(out=fh_hi_ap[j], in_=fhh[:, tsl]))
                t2 = spl.issue(kb.sc.dma_start(out=fh_lo_ap[j], in_=fhl[:, tsl]))
                st_done.append((t1, t2))
                kb.pump(2)
            sph.wait_all(kb.sc)
            spl.wait_all(kb.sc)
            kb.barrier()
        with ExitStack() as lo_:
            fhb = kb.sb(lo_, "gD_fhb", [128, GJ * T], BF16)
            flb = kb.sb(lo_, "gD_flb", [128, GJ * T], BF16)
            wp = kb.sb(lo_, "gD_wp", [128, 2 * 5504], BF16)
            wpl = kb.sb(lo_, "gD_wpl", [128, 2 * 5504], BF16)
            xd = kb.sb(lo_, "gD_xd", [128, 2 * T], F32)
            ot = kb.sb(lo_, "gD_ot", [128, 2 * T], F32)
            psD = kb.ps(lo_, "gD_psD", [128, 2 * T])
            v1 = rbh.issue(nc.sync.dma_start(
                out=fhb[:].rearrange("p (j t) -> p j t", j=GJ),
                in_=fh_hi_ap[0:GJ].rearrange("j p t -> p j t")))
            v2 = rbl.issue(nc.sync.dma_start(
                out=flb[:].rearrange("p (j t) -> p j t", j=GJ),
                in_=fh_lo_ap[0:GJ].rearrange("j p t -> p j t")))
            wp_ld, dmm_done, out_done, st_dn, xd_ld = [], [], [], [], []
            for d in range(DT):
                buf = d % 2
                wpb = wp[:, buf * 5504:(buf + 1) * 5504]
                wplb = wpl[:, buf * 5504:(buf + 1) * 5504]
                if d >= 2:
                    nc.sync.wait_ge(mm.h, dmm_done[d - 2])
                i1 = nc.sync.dma_start(out=wpb, in_=dh_ap[d])
                i2 = nc.sync.dma_start(out=wplb, in_=dl_ap[d])
                wp_ld.append((wpld.issue(i1), wpld2.issue(i2)))
                wpld.wait(nc.tensor, wp_ld[d][0])
                wpld2.wait(nc.tensor, wp_ld[d][1])
                if d == 0:
                    rbh.wait(nc.tensor, v1)
                    rbl.wait(nc.tensor, v2)
                if d >= 2:
                    nc.tensor.wait_ge(vo.h, out_done[d - 2])
                first = True
                for wsrc, hsrc in ((wpb, fhb), (wpb, flb), (wplb, fhb)):
                    for kc in range(GJ):
                        im = nc.tensor.matmul(
                            psD[:, buf * T:(buf + 1) * T],
                            wsrc[:, kc * 128:(kc + 1) * 128],
                            hsrc[:, kc * T:(kc + 1) * T],
                            start=first, stop=(wsrc is wplb and kc == GJ - 1))
                        first = False
                dmm_done.append(mm.inc(im, 1))
                if d >= 2:
                    nc.sync.wait_ge(vo.h, out_done[d - 2])
                xd_ld.append(xdld.issue(nc.sync.dma_start(
                    out=xd[:, buf * T:(buf + 1) * T], in_=s_in[d, :, tok])))
                kb.v.wait_ge(mm.h, dmm_done[d])
                xdld.wait(kb.v, xd_ld[d])
                if d >= 2:
                    outst.wait(kb.v, st_dn[d - 2])
                kb.v.tensor_scalar(ot[:, buf * T:(buf + 1) * T],
                                   psD[:, buf * T:(buf + 1) * T], 0.5, None, ALU.mult)
                kb.v.tensor_tensor(ot[:, buf * T:(buf + 1) * T],
                                   ot[:, buf * T:(buf + 1) * T],
                                   xd[:, buf * T:(buf + 1) * T], ALU.add)
                out_done.append(vo.n)
                kb.sc.wait_ge(vo.h, out_done[d])
                st_dn.append(outst.issue(kb.sc.dma_start(
                    out=s_out[d, :, tok], in_=ot[:, buf * T:(buf + 1) * T])))
                kb.pump(6)
            outst.wait_all(kb.sc)
            kb.barrier()
    slscope.close()


MQ = 128
NCH = NTOK // MQ
NH = 64
HD = 64
NG = 8


def emit_mamba(kb, s_in, s_out, mp_hi, mp_lo, wo_hi, wo_lo, conv_t_ap, mvec_ap,
               gate_scr, x_scr, b_scr, c_scr, gh_scr, gl_scr):
    nc, s, c = kb.nc, kb.s, kb.c
    vo, so = kb.ord["vector"], kb.ord["scalar"]
    mm = s["mm0"]
    xld = kb.ds("f_xld", 1)
    wld = kb.ds("g_wld", 4)
    spg = kb.ds("m_spg")
    spx = kb.ds("m_spx")
    spb = kb.ds("m_spb")
    cld = kb.ds("m_cld", 8)
    wold = kb.ds("g_wld2", 4)
    xdld = kb.ds("f_xdld")
    outst = kb.ds("f_outst")

    with ExitStack() as ms:
        dts = kb.sb(ms, "m_dts", [64, NTOK], F32)
        rstds = kb.sb(ms, "m_rstds", [1, NTOK], F32)
        Ls = kb.sb(ms, "m_Ls", [64, NTOK], F32)
        zer = kb.sb(ms, "m_zer", [64, MQ], F32)
        halo = kb.sb(ms, "m_halo", [128, 48 * 3], F32)
        convw = kb.sb(ms, "m_convw", [128, 48 * 5], F32)
        mvec = kb.sb(ms, "m_mvec", [128, 8], F32)
        Dfull = kb.sb(ms, "m_Dfull", [128, 64], F32)
        Hst = kb.sb(ms, "m_H", [128, NH * HD], F32)
        t1 = wld.issue(nc.sync.dma_start(out=convw[:, 0:240], in_=conv_t_ap[:, :]))
        t2 = wld.issue(nc.sync.dma_start(out=mvec[:, 0:4], in_=mvec_ap[:, :]))
        wld.wait(kb.v, t1)
        wld.wait(kb.v, t2)
        kb.v.memset(zer[:], 0.0)
        kb.v.memset(halo[:], 0.0)
        kb.v.memset(Hst[:], 0.0)
        kb.sc.wait_ge(vo.h, vo.n)
        kb.sc.activation(mvec[:, 4:5], mvec[:, 1:2], AF.Exp)
        kb.v.wait_ge(so.h, so.n)
        kb.v.tensor_scalar(mvec[:, 4:5], mvec[:, 4:5], -1.0, None, ALU.mult)
        with ExitStack() as lo_:
            psE = kb.ps(lo_, "m_psE", [128, 512])
            nc.tensor.wait_ge(vo.h, vo.n)
            it = nc.tensor.transpose(psE[0:1, 0:128], mvec[:, 2:3], c["id_f32"])
            mm.inc(it, 1)
            kb.v.wait_ge(mm.h, mm.n)
            kb.v.tensor_copy(Dfull[0:1, :], psE[0:1, 0:64])
            nc.tensor.wait_ge(vo.h, vo.n)
            ib = nc.tensor.matmul(psE[:, 64:128], c["ones_row"], Dfull[0:1, :],
                                  start=True, stop=True)
            mm.inc(ib, 1)
            kb.v.wait_ge(mm.h, mm.n)
            kb.v.tensor_copy(Dfull[:], psE[:, 64:128])
        kb.barrier()

        m1s = ExitStack()
        hhi = kb.sb(m1s, "m_hhi", [128, DT * T], BF16)
        hlo = kb.sb(m1s, "m_hlo", [128, DT * T], BF16)
        for t in range(NT):
            tok = slice(t * T, (t + 1) * T)
            with ExitStack() as lo_:
                x_sb = kb.sb(lo_, "mA_x", [128, (DT + 1) * T], F32)
                brow = kb.sb(lo_, "mA_brow", [1, 2 * T], F32)
                bfull = kb.sb(lo_, "mA_bfull", [128, 2 * T], F32)
                psA = kb.ps(lo_, "mA_psA", [1, T])
                psB = kb.ps(lo_, "mA_psB", [1, T])
                psC = kb.ps(lo_, "mA_psC", [128, 2 * T])
                v_ld = xld.issue(nc.sync.dma_start(
                    out=x_sb[:, 0:DT * T].rearrange("p (d t) -> p d t", d=DT),
                    in_=s_in[:, :, tok].rearrange("d p t -> p d t")))
                xld.wait(kb.v, v_ld)
                xld.wait(kb.sc, v_ld)
                xld.wait(nc.tensor, v_ld)
                emit_ln(kb, x_sb, x_sb[:, DT * T:], hhi, hlo, psA, psB, psC, brow, bfull)
                kb.barrier()
            with ExitStack() as lo_:
                wsp = kb.sb(lo_, "mP_w", [128, 4 * 2048], BF16)
                cin = kb.sb(lo_, "mP_cin", [128, 2 * 515], F32)
                otile = kb.sb(lo_, "mP_ot", [128, 2 * T], F32)
                dt_raw = kb.sb(lo_, "mP_dt", [64, T], F32)
                scr64 = kb.sb(lo_, "mP_scr", [64, T], F32)
                psP = kb.ps(lo_, "mP_psP", [128, 2 * T])
                w_ld, mmd, evd, std = [], [], [], []
                for j in range(81):
                    buf = j % 2
                    whb = wsp[:, buf * 2 * 2048:buf * 2 * 2048 + 2048]
                    wlb = wsp[:, buf * 2 * 2048 + 2048:(buf + 1) * 2 * 2048]
                    if j >= 2:
                        nc.sync.wait_ge(mm.h, mmd[j - 2])
                    i1 = nc.sync.dma_start(out=whb, in_=mp_hi[j])
                    i2 = nc.sync.dma_start(out=wlb, in_=mp_lo[j])
                    w_ld.append((wld.issue(i1), wld.issue(i2)))
                    wld.wait(nc.tensor, w_ld[j][0])
                    wld.wait(nc.tensor, w_ld[j][1])
                    if j >= 2:
                        nc.tensor.wait_ge(vo.h, evd[j - 2])
                    first = True
                    for wsrc, hsrc in ((whb, hhi), (whb, hlo), (wlb, hhi)):
                        for kc in range(16):
                            im = nc.tensor.matmul(
                                psP[:, buf * T:(buf + 1) * T],
                                wsrc[:, kc * 128:(kc + 1) * 128],
                                hsrc[:, kc * T:(kc + 1) * T],
                                start=first, stop=(wsrc is wlb and kc == 15))
                            first = False
                    mmd.append(mm.inc(im, 1))
                    kb.v.wait_ge(mm.h, mmd[j])
                    if j >= 2 and std[j - 2] is not None:
                        std[j - 2][0].wait(kb.v, std[j - 2][1])
                    ot_ = otile[:, buf * T:(buf + 1) * T]
                    if j < 32:
                        kb.v.tensor_copy(ot_, psP[:, buf * T:(buf + 1) * T])
                        evd.append(vo.n)
                        kb.sc.wait_ge(vo.h, evd[j])
                        std.append((spg, spg.issue(kb.sc.dma_start(
                            out=gate_scr[j, :, tok], in_=ot_))))
                    elif j < 80:
                        ct = j - 32
                        cb = cin[:, buf * 515:(buf + 1) * 515]
                        kb.v.tensor_copy(cb[:, 0:3], halo[:, ct * 3:(ct + 1) * 3])
                        kb.v.tensor_copy(cb[:, 3:515], psP[:, buf * T:(buf + 1) * T])
                        kb.v.tensor_copy(halo[:, ct * 3:(ct + 1) * 3], cb[:, 512:515])
                        kb.v.tensor_scalar(ot_, cb[:, 0:T], convw[:, ct * 4:ct * 4 + 1],
                                           convw[:, 192 + ct:193 + ct], ALU.mult, ALU.add)
                        for k in range(1, 4):
                            kb.v.scalar_tensor_tensor(ot_, cb[:, k:k + T],
                                                      convw[:, ct * 4 + k:ct * 4 + k + 1],
                                                      ot_, ALU.mult, ALU.add)
                        evd.append(vo.n)
                        kb.sc.wait_ge(vo.h, evd[j])
                        kb.sc.activation(ot_, ot_, AF.Silu)
                        sil = so.n
                        kb.sc.wait_ge(so.h, sil)
                        if ct < 32:
                            std.append((spx, spx.issue(kb.sc.dma_start(
                                out=x_scr[ct, :, tok], in_=ot_))))
                        elif ct < 40:
                            std.append((spb, spb.issue(kb.sc.dma_start(
                                out=b_scr[ct - 32, :, tok], in_=ot_))))
                        else:
                            std.append((spb, spb.issue(kb.sc.dma_start(
                                out=c_scr[ct - 40, :, tok], in_=ot_))))
                    else:
                        kb.v.tensor_copy(dt_raw[:], psP[0:64, buf * T:(buf + 1) * T])
                        evd.append(vo.n)
                        std.append(None)
                    kb.pump(2)
                kb.v.tensor_scalar(dt_raw[:], dt_raw[:], mvec[0:64, 0:1], None, ALU.add)
                kb.sc.wait_ge(vo.h, vo.n)
                kb.sc.activation(scr64[:], dt_raw[:], AF.Abs)
                kb.sc.activation(scr64[:], scr64[:], AF.Exp, scale=-1.0)
                kb.sc.activation(scr64[:], scr64[:], AF.Ln, bias=c["ones_col"][0:64, :])
                dsl = slice(t * T, (t + 1) * T)
                kb.sc.activation(dts[0:64, dsl], dt_raw[:], AF.Relu)
                kb.v.wait_ge(so.h, so.n)
                kb.v.tensor_tensor(dts[0:64, dsl], dts[0:64, dsl], scr64[:], ALU.add)
                kb.v.tensor_scalar(scr64[:], dts[0:64, dsl], mvec[0:64, 4:5], None, ALU.mult)
                for cc in range(T // MQ):
                    csl = slice(t * T + cc * MQ, t * T + (cc + 1) * MQ)
                    kb.v.tensor_tensor_scan(Ls[0:64, csl], scr64[:, cc * MQ:(cc + 1) * MQ],
                                            zer[:], 0.0, ALU.add, ALU.add)
                spg.wait_all(kb.sc)
                spx.wait_all(kb.sc)
                spb.wait_all(kb.sc)
                kb.barrier()

        m1s.close()
        with ExitStack() as lo_:
            Xf = kb.sb(lo_, "m2_Xf", [128, 32 * MQ], F32)
            Xt = kb.sb(lo_, "m2_Xt", [128, 32 * MQ], F32)
            Bf = kb.sb(lo_, "m2_Bf", [128, NG * MQ], F32)
            Cf = kb.sb(lo_, "m2_Cf", [128, NG * MQ], F32)
            Bt = kb.sb(lo_, "m2_Bt", [128, NG * MQ], F32)
            gt = kb.sb(lo_, "m2_gt", [128, 32 * MQ], F32)
            CBs = kb.sb(lo_, "m2_CB", [128, NG * MQ], F32)
            LT = kb.sb(lo_, "m2_LT", [128, 4 * 64], F32)
            Lfl = kb.sb(lo_, "m2_Lfl", [1, 16 * MQ], F32)
            eGf = kb.sb(lo_, "m2_eGf", [128, 64], F32)
            eLc = kb.sb(lo_, "m2_eLc", [64, MQ], F32)
            wsc = kb.sb(lo_, "m2_wsc", [64, MQ], F32)
            Es = kb.sb(lo_, "m2_Es", [128, 2 * MQ], F32)
            Ss = kb.sb(lo_, "m2_Ss", [128, 2 * MQ], F32)
            yis = kb.sb(lo_, "m2_yis", [128, 2 * HD], F32)
            xws = kb.sb(lo_, "m2_xws", [128, 2 * HD], F32)
            ytm = kb.sb(lo_, "m2_ytm", [128, NH * HD], F32)
            ghc = kb.sb(lo_, "m2_ghc", [128, 32 * MQ], BF16)
            glc = kb.sb(lo_, "m2_glc", [128, 32 * MQ], BF16)
            gf = kb.sb(lo_, "m2_gf", [128, 2 * MQ], F32)
            sg = kb.sb(lo_, "m2_sg", [128, 2 * MQ], F32)
            sq2 = kb.sb(lo_, "m2_sq2", [128, 2 * MQ], F32)
            ps = [kb.ps(lo_, f"m2_ps{i}", [128, 512]) for i in range(8)]
            gst = kb.ds("f_sph")
            gst2 = kb.ds("f_spl")

            for ch in range(NCH):
                csl = slice(ch * MQ, (ch + 1) * MQ)
                cg = ch % 4
                ltoks = []
                ltoks.append(cld.issue(nc.sync.dma_start(
                    out=Bf[:].rearrange("p (j t) -> p j t", j=NG),
                    in_=b_scr[:, :, csl].rearrange("j p t -> p j t"))))
                ltoks.append(cld.issue(nc.sync.dma_start(
                    out=Cf[:].rearrange("p (j t) -> p j t", j=NG),
                    in_=c_scr[:, :, csl].rearrange("j p t -> p j t"))))
                ltoks.append(cld.issue(nc.sync.dma_start(
                    out=Xf[:].rearrange("p (j t) -> p j t", j=32),
                    in_=x_scr[:, :, csl].rearrange("j p t -> p j t"))))
                ltoks.append(cld.issue(nc.sync.dma_start(
                    out=gt[:].rearrange("p (j t) -> p j t", j=32),
                    in_=gate_scr[:, :, csl].rearrange("j p t -> p j t"))))
                for tk in ltoks:
                    cld.wait(nc.tensor, tk)
                cld.wait(kb.sc, ltoks[3])
                kb.sc.activation(eLc[:], Ls[0:64, csl], AF.Exp)
                gcol = Ls[0:64, csl.stop - 1:csl.stop]
                kb.v.tensor_scalar(wsc[:], Ls[0:64, csl], -1.0, gcol, ALU.mult, ALU.add)
                kb.sc.wait_ge(vo.h, vo.n)
                kb.sc.activation(wsc[:], wsc[:], AF.Exp)
                kb.v.wait_ge(so.h, so.n)
                kb.v.tensor_tensor(wsc[:], wsc[:], dts[0:64, csl], ALU.mult)
                tr_ev = []
                for i in range(32):
                    bk = i % 2
                    eng = kb.v if bk == 0 else kb.sc
                    esm = vo if bk == 0 else so
                    if i >= 2:
                        nc.tensor.wait_ge(tr_ev[i - 2][0], tr_ev[i - 2][1])
                    it = nc.tensor.transpose(ps[bk][:, 0:MQ], Xf[:, i * MQ:(i + 1) * MQ],
                                             c["id_f32"])
                    mm.inc(it, 1)
                    eng.wait_ge(mm.h, mm.n)
                    if bk == 0:
                        eng.tensor_copy(Xt[:, i * MQ:(i + 1) * MQ], ps[bk][:, 0:MQ])
                    else:
                        eng.activation(Xt[:, i * MQ:(i + 1) * MQ], ps[bk][:, 0:MQ], AF.Copy)
                    tr_ev.append((esm.h, esm.n))
                for i in range(8):
                    bk = i % 2
                    eng = kb.v if bk == 0 else kb.sc
                    esm = vo if bk == 0 else so
                    nc.tensor.wait_ge(tr_ev[-2][0], tr_ev[-2][1])
                    it = nc.tensor.transpose(ps[bk][:, 0:MQ], Bf[:, i * MQ:(i + 1) * MQ],
                                             c["id_f32"])
                    mm.inc(it, 1)
                    eng.wait_ge(mm.h, mm.n)
                    if bk == 0:
                        eng.tensor_copy(Bt[:, i * MQ:(i + 1) * MQ], ps[bk][:, 0:MQ])
                    else:
                        eng.activation(Bt[:, i * MQ:(i + 1) * MQ], ps[bk][:, 0:MQ], AF.Copy)
                    tr_ev.append((esm.h, esm.n))
                for i, srcv in enumerate((Ls[0:64, csl], dts[0:64, csl], eLc[:], wsc[:])):
                    bk = i % 2
                    nc.tensor.wait_ge(tr_ev[-2][0], tr_ev[-2][1])
                    it = nc.tensor.transpose(ps[bk][0:MQ, 0:64], srcv,
                                             c["id_f32"][0:64, 0:64])
                    mm.inc(it, 1)
                    kb.v.wait_ge(mm.h, mm.n)
                    kb.v.tensor_copy(LT[:, i * 64:(i + 1) * 64], ps[bk][:, 0:64])
                    tr_ev.append((vo.h, vo.n))
                cb_ev = []
                for g in range(NG):
                    bk = 2 + g % 2
                    if g >= 2:
                        nc.tensor.wait_ge(vo.h, cb_ev[g - 2])
                    else:
                        nc.tensor.wait_ge(tr_ev[-1][0], tr_ev[-1][1])
                        nc.tensor.wait_ge(tr_ev[-2][0], tr_ev[-2][1])
                    im = nc.tensor.matmul(ps[bk][:, 0:MQ], Bf[:, g * MQ:(g + 1) * MQ],
                                          Cf[:, g * MQ:(g + 1) * MQ], start=True, stop=True)
                    mm.inc(im, 1)
                    kb.v.wait_ge(mm.h, mm.n)
                    kb.v.tensor_tensor(CBs[:, g * MQ:(g + 1) * MQ], ps[bk][:, 0:MQ],
                                       c["triu"], ALU.mult)
                    cb_ev.append(vo.n)
                nc.tensor.wait_ge(vo.h, vo.n)
                it = nc.tensor.transpose(ps[4][0:1, 0:64], eLc[:, MQ - 1:MQ],
                                         c["id_f32"][0:64, 0:64])
                mm.inc(it, 1)
                kb.v.wait_ge(mm.h, mm.n)
                kb.v.tensor_copy(eGf[0:1, :], ps[4][0:1, 0:64])
                nc.tensor.wait_ge(vo.h, vo.n)
                ib = nc.tensor.matmul(ps[5][:, 0:64], c["ones_row"], eGf[0:1, :],
                                      start=True, stop=True)
                mm.inc(ib, 1)
                kb.v.wait_ge(mm.h, mm.n)
                kb.v.tensor_copy(eGf[:], ps[5][:, 0:64])
                eT_done = [None] * NH
                psdH_last = [None] * NH
                s_done = [None] * NH
                yiadd_done = [None] * NH
                yits_done = [None] * NH
                h_done = [None] * NH
                lf_tok = None
                exp_done = [None] * NH
                psY_mm = [None] * NH
                psYI_mm = [None] * NH

                def head_front(kb, h):
                    st_ = h % 2
                    psT = ps[0 + st_]
                    if h % 16 == 0:
                        nonlocal lf_tok
                        if h > 0:
                            nc.sync.wait_ge(mm.h, psT_mm[h - 1])
                        lf_tok = cld.issue(nc.sync.dma_start(
                            out=Lfl[0:1, :].rearrange("o (h q) -> o h q", h=16),
                            in_=Ls[h:h + 16, csl]))
                        cld.wait(nc.tensor, lf_tok)
                    if h >= 2:
                        nc.tensor.wait_ge(vo.h, eT_done[h - 2])
                    im = nc.tensor.matmul(psT[:, 0:MQ], c["ones_row"],
                                          Lfl[0:1, (h % 16) * MQ:(h % 16 + 1) * MQ],
                                          start=True, stop=True)
                    psT_mm[h] = mm.inc(im, 1)
                    kb.v.wait_ge(mm.h, psT_mm[h])
                    if h >= 2:
                        kb.v.wait_ge(so.h, exp_done[h - 2])
                    kb.v.tensor_scalar(Es[:, st_ * MQ:(st_ + 1) * MQ], psT[:, 0:MQ],
                                       LT[:, h:h + 1], 0.0, ALU.subtract, ALU.min)
                    eT_done[h] = vo.n
                    kb.sc.wait_ge(vo.h, eT_done[h])
                    kb.sc.activation(Es[:, st_ * MQ:(st_ + 1) * MQ],
                                     Es[:, st_ * MQ:(st_ + 1) * MQ], AF.Exp)
                    exp_done[h] = so.n
                    kb.v.wait_ge(so.h, exp_done[h])
                    if h >= 2:
                        kb.v.wait_ge(mm.h, psY_mm[h - 2])
                    kb.v.tensor_tensor(Ss[:, st_ * MQ:(st_ + 1) * MQ],
                                       Es[:, st_ * MQ:(st_ + 1) * MQ],
                                       CBs[:, (h // 8) * MQ:(h // 8 + 1) * MQ], ALU.mult)
                    kb.v.tensor_scalar(Ss[:, st_ * MQ:(st_ + 1) * MQ],
                                       Ss[:, st_ * MQ:(st_ + 1) * MQ],
                                       LT[:, 64 + h:65 + h], None, ALU.mult)
                    s_done[h] = vo.n

                def head_back(kb, h):
                    st_ = h % 2
                    g = h // 8
                    psY = ps[2 + st_]
                    psYI = ps[4 + st_]
                    psdH = ps[6 + st_]
                    Xth = Xt[:, (h // 2) * MQ + (h % 2) * 64:(h // 2) * MQ + (h % 2) * 64 + 64]
                    nc.tensor.wait_ge(vo.h, s_done[h])
                    if h >= 2:
                        nc.tensor.wait_ge(vo.h, yiadd_done[h - 2])
                    im = nc.tensor.matmul(psY[:, 0:HD], Ss[:, st_ * MQ:(st_ + 1) * MQ],
                                          Xth, start=True, stop=True)
                    psY_mm[h] = mm.inc(im, 1)
                    if h >= 2:
                        nc.tensor.wait_ge(so.h, yits_done[h - 2])
                    im = nc.tensor.matmul(psYI[:, 0:HD], Cf[:, g * MQ:(g + 1) * MQ],
                                          Hst[:, h * HD:(h + 1) * HD], start=True, stop=True)
                    psYI_mm[h] = mm.inc(im, 1)
                    kb.sc.wait_ge(mm.h, psYI_mm[h])
                    if h >= 2:
                        kb.sc.wait_ge(vo.h, yiadd_done[h - 2])
                        kb.sc.wait_ge(mm.h, psdH_last[h - 2])
                    kb.sc.activation(yis[:, st_ * HD:(st_ + 1) * HD], psYI[:, 0:HD],
                                     AF.Copy, scale=LT[:, 128 + h:129 + h])
                    kb.sc.activation(xws[:, st_ * HD:(st_ + 1) * HD], Xth,
                                     AF.Copy, scale=LT[:, 192 + h:193 + h])
                    yits_done[h] = so.n
                    kb.v.wait_ge(mm.h, psY_mm[h])
                    kb.v.wait_ge(so.h, yits_done[h])
                    kb.v.tensor_tensor(yis[:, st_ * HD:(st_ + 1) * HD],
                                       yis[:, st_ * HD:(st_ + 1) * HD],
                                       psY[:, 0:HD], ALU.add)
                    yiadd_done[h] = vo.n
                    kb.v.scalar_tensor_tensor(ytm[:, h * HD:(h + 1) * HD], Xth,
                                              Dfull[:, h:h + 1],
                                              yis[:, st_ * HD:(st_ + 1) * HD],
                                              ALU.mult, ALU.add)
                    nc.tensor.wait_ge(so.h, yits_done[h])
                    if h >= 2:
                        nc.tensor.wait_ge(vo.h, h_done[h - 2])
                    im = nc.tensor.matmul(psdH[:, 0:HD], Bt[:, g * MQ:(g + 1) * MQ],
                                          xws[:, st_ * HD:(st_ + 1) * HD],
                                          start=True, stop=True)
                    psdH_last[h] = mm.inc(im, 1)
                    kb.v.wait_ge(mm.h, psdH_last[h])
                    kb.v.scalar_tensor_tensor(Hst[:, h * HD:(h + 1) * HD],
                                              Hst[:, h * HD:(h + 1) * HD],
                                              eGf[:, h:h + 1], psdH[:, 0:HD],
                                              ALU.mult, ALU.add)
                    h_done[h] = vo.n

                psT_mm = [None] * NH
                for i in range(NH + 2):
                    if i >= 2:
                        head_back(kb, i - 2)
                    if i < NH:
                        head_front(kb, i)
                gz_ev = []
                for i in range(32):
                    bk = i % 2
                    if i >= 2:
                        nc.tensor.wait_ge(vo.h, gz_ev[i - 2])
                    else:
                        nc.tensor.wait_ge(vo.h, h_done[NH - 1])
                    it = nc.tensor.transpose(ps[bk][:, 0:MQ], ytm[:, i * MQ:(i + 1) * MQ],
                                             c["id_f32"])
                    trv = mm.inc(it, 1)
                    kb.sc.activation(sg[:, bk * MQ:(bk + 1) * MQ], gt[:, i * MQ:(i + 1) * MQ],
                                     AF.Silu)
                    sgv = so.n
                    kb.v.wait_ge(mm.h, trv)
                    kb.v.wait_ge(so.h, sgv)
                    kb.v.tensor_tensor(gf[:, bk * MQ:(bk + 1) * MQ], ps[bk][:, 0:MQ],
                                       sg[:, bk * MQ:(bk + 1) * MQ], ALU.mult)
                    kb.sc.wait_ge(vo.h, vo.n)
                    kb.sc.activation(sq2[:, bk * MQ:(bk + 1) * MQ],
                                     gf[:, bk * MQ:(bk + 1) * MQ], AF.Square)
                    sqv = so.n
                    nc.tensor.wait_ge(so.h, sqv)
                    im = nc.tensor.matmul(ps[2][0:1, 0:MQ], c["ones_col"],
                                          sq2[:, bk * MQ:(bk + 1) * MQ],
                                          start=(i == 0), stop=(i == 31))
                    mm.inc(im, 1)
                    kb.v.wait_ge(so.h, sqv)
                    kb.v.tensor_copy(ghc[:, i * MQ:(i + 1) * MQ], gf[:, bk * MQ:(bk + 1) * MQ])
                    kb.v.tensor_tensor(gf[:, bk * MQ:(bk + 1) * MQ],
                                       gf[:, bk * MQ:(bk + 1) * MQ],
                                       ghc[:, i * MQ:(i + 1) * MQ], ALU.subtract)
                    kb.v.tensor_copy(glc[:, i * MQ:(i + 1) * MQ], gf[:, bk * MQ:(bk + 1) * MQ])
                    gz_ev.append(vo.n)
                kb.v.wait_ge(mm.h, mm.n)
                kb.v.tensor_scalar(rstds[0:1, csl], ps[2][0:1, 0:MQ],
                                   1.0 / 4096, EPS_LN, ALU.mult, ALU.add)
                kb.sc.wait_ge(vo.h, vo.n)
                kb.sc.activation(rstds[0:1, csl], rstds[0:1, csl], AF.Sqrt)
                kb.v.wait_ge(so.h, so.n)
                kb.v.reciprocal(rstds[0:1, csl], rstds[0:1, csl])
                kb.sc.wait_ge(vo.h, vo.n)
                gst.issue(kb.sc.dma_start(out=gh_scr[ch], in_=ghc[:]))
                gst2.issue(kb.sc.dma_start(out=gl_scr[ch], in_=glc[:]))
                gst.wait_all(kb.sc)
                gst2.wait_all(kb.sc)
                kb.barrier()

        with ExitStack() as lo_:
            ghi = kb.sb(lo_, "m3_ghi", [128, 32 * T], BF16)
            glo = kb.sb(lo_, "m3_glo", [128, 32 * T], BF16)
            wo = kb.sb(lo_, "m3_wo", [128, 4 * 4096], BF16)
            xd = kb.sb(lo_, "m3_xd", [128, 2 * T], F32)
            ot3 = kb.sb(lo_, "m3_ot", [128, 2 * T], F32)
            rstdb = kb.sb(lo_, "m3_rstdb", [128, T], F32)
            psO = kb.ps(lo_, "m3_psO", [128, 2 * T])
            psR = kb.ps(lo_, "m3_psR", [128, T])
            for grp in range(8):
                tok3 = slice(grp * T, (grp + 1) * T)
                rtk = []
                ghi_v = ghi[:].rearrange("p (j g t) -> p j g t", j=32, g=4)
                glo_v = glo[:].rearrange("p (j g t) -> p j g t", j=32, g=4)
                for cg in range(4):
                    ch = grp * 4 + cg
                    rtk.append(cld.issue(nc.sync.dma_start(
                        out=ghi_v[:, :, cg, :],
                        in_=gh_scr[ch].rearrange("p (j t) -> p j t", j=32))))
                    rtk.append(cld.issue(nc.sync.dma_start(
                        out=glo_v[:, :, cg, :],
                        in_=gl_scr[ch].rearrange("p (j t) -> p j t", j=32))))
                for tk in rtk:
                    cld.wait(nc.tensor, tk)
                ib = nc.tensor.matmul(psR[:], c["ones_row"], rstds[0:1, tok3],
                                      start=True, stop=True)
                mm.inc(ib, 1)
                kb.v.wait_ge(mm.h, mm.n)
                kb.v.tensor_copy(rstdb[:], psR[:])
                wo_ld, dmm, ev3, st3, xl3 = [], [], [], [], []
                for d in range(DT):
                    buf = d % 2
                    whb = wo[:, buf * 2 * 4096:buf * 2 * 4096 + 4096]
                    wlb = wo[:, buf * 2 * 4096 + 4096:(buf + 1) * 2 * 4096]
                    if d >= 2:
                        nc.sync.wait_ge(mm.h, dmm[d - 2])
                    i1 = nc.sync.dma_start(out=whb, in_=wo_hi[d])
                    i2 = nc.sync.dma_start(out=wlb, in_=wo_lo[d])
                    wo_ld.append((wold.issue(i1), wold.issue(i2)))
                    wold.wait(nc.tensor, wo_ld[d][0])
                    wold.wait(nc.tensor, wo_ld[d][1])
                    if d >= 2:
                        nc.tensor.wait_ge(vo.h, ev3[d - 2])
                    first = True
                    for wsrc, hsrc in ((whb, ghi), (whb, glo), (wlb, ghi)):
                        for kc in range(32):
                            im = nc.tensor.matmul(
                                psO[:, buf * T:(buf + 1) * T],
                                wsrc[:, kc * 128:(kc + 1) * 128],
                                hsrc[:, kc * T:(kc + 1) * T],
                                start=first, stop=(wsrc is wlb and kc == 31))
                            first = False
                    dmm.append(mm.inc(im, 1))
                    if d >= 2:
                        nc.sync.wait_ge(vo.h, ev3[d - 2])
                    xl3.append(xdld.issue(nc.sync.dma_start(
                        out=xd[:, buf * T:(buf + 1) * T], in_=s_in[d, :, tok3])))
                    kb.v.wait_ge(mm.h, dmm[d])
                    xdld.wait(kb.v, xl3[d])
                    if d >= 2:
                        outst.wait(kb.v, st3[d - 2])
                    kb.v.tensor_tensor(ot3[:, buf * T:(buf + 1) * T],
                                       psO[:, buf * T:(buf + 1) * T], rstdb[:], ALU.mult)
                    kb.v.tensor_tensor(ot3[:, buf * T:(buf + 1) * T],
                                       ot3[:, buf * T:(buf + 1) * T],
                                       xd[:, buf * T:(buf + 1) * T], ALU.add)
                    ev3.append(vo.n)
                    kb.sc.wait_ge(vo.h, ev3[d])
                    st3.append(outst.issue(kb.sc.dma_start(
                        out=s_out[d, :, tok3], in_=ot3[:, buf * T:(buf + 1) * T])))
                outst.wait_all(kb.sc)
                kb.barrier()


ACH = 64
NAC = NTOK // ACH
AH = 16


def emit_attn(kb, s_in, s_out, wq_hi, wq_lo, wk_hi, wk_lo, wv_hi, wv_lo,
              wo_hi, wo_lo, fm_hi, fm_lo, avec_ap, qf_scr, kf_scr, vtm_scr):
    nc, s, c = kb.nc, kb.s, kb.c
    vo, so = kb.ord["vector"], kb.ord["scalar"]
    mm = s["mm0"]
    xld = kb.ds("f_xld", 1)
    wld = kb.ds("g_wld", 4)
    sps = [kb.ds("m_spg"), kb.ds("m_spx"), kb.ds("m_spb")]
    cld = kb.ds("m_cld", 8)
    wold = kb.ds("g_wld2", 4)
    xdld = kb.ds("f_xdld")
    outst = kb.ds("f_outst")
    SCALE = float(np.float32(128.0 ** -0.5))

    with ExitStack() as ms:
        avec = kb.sb(ms, "a_avec", [128, 4], F32)
        fmh = kb.sb(ms, "a_fmh", [128, 512], BF16)
        fml = kb.sb(ms, "a_fml", [128, 512], BF16)
        KV = kb.sb(ms, "a_KV", [128, AH * 128], F32)
        t1 = wld.issue(nc.sync.dma_start(out=avec[:], in_=avec_ap[:, :]))
        t2 = wld.issue(nc.sync.dma_start(out=fmh[:], in_=fm_hi[0]))
        t3 = wld.issue(nc.sync.dma_start(out=fml[:], in_=fm_lo[0]))
        for tk in (t1, t2, t3):
            wld.wait(kb.v, tk)
        kb.v.memset(KV[:], 0.0)
        kb.barrier()

        a1s = ExitStack()
        hhi = kb.sb(a1s, "a_hhi", [128, DT * T], BF16)
        hlo = kb.sb(a1s, "a_hlo", [128, DT * T], BF16)
        for t in range(NT):
            tok = slice(t * T, (t + 1) * T)
            with ExitStack() as lo_:
                x_sb = kb.sb(lo_, "aA_x", [128, (DT + 1) * T], F32)
                brow = kb.sb(lo_, "aA_brow", [1, 2 * T], F32)
                bfull = kb.sb(lo_, "aA_bfull", [128, 2 * T], F32)
                psA = kb.ps(lo_, "aA_psA", [1, T])
                psB = kb.ps(lo_, "aA_psB", [1, T])
                psC = kb.ps(lo_, "aA_psC", [128, 2 * T])
                v_ld = xld.issue(nc.sync.dma_start(
                    out=x_sb[:, 0:DT * T].rearrange("p (d t) -> p d t", d=DT),
                    in_=s_in[:, :, tok].rearrange("d p t -> p d t")))
                xld.wait(kb.v, v_ld)
                xld.wait(kb.sc, v_ld)
                xld.wait(nc.tensor, v_ld)
                emit_ln(kb, x_sb, x_sb[:, DT * T:], hhi, hlo, psA, psB, psC, brow, bfull)
                kb.barrier()
            with ExitStack() as lo_:
                wsp = kb.sb(lo_, "aP_w", [128, 4 * 2048], BF16)
                qhi = kb.sb(lo_, "aP_qhi", [128, 16 * T], BF16)
                qlo = kb.sb(lo_, "aP_qlo", [128, 16 * T], BF16)
                khi = kb.sb(lo_, "aP_khi", [128, 16 * T], BF16)
                klo = kb.sb(lo_, "aP_klo", [128, 16 * T], BF16)
                vfm = kb.sb(lo_, "aP_vfm", [128, 16 * T], F32)
                tmp = kb.sb(lo_, "aP_tmp", [128, 2 * T], F32)
                qf = kb.sb(lo_, "aP_qf", [128, 2 * T], F32)
                vt = kb.sb(lo_, "aP_vt", [64, 2 * 128], F32)
                psP = kb.ps(lo_, "aP_psP", [128, 2 * T])
                psM = kb.ps(lo_, "aP_psM", [128, 2 * T])
                evd = {}
                mmd = {}
                w_ld = []
                wsrc_hi = {"q": wq_hi, "k": wk_hi, "v": wv_hi}
                wsrc_lo = {"q": wq_lo, "k": wk_lo, "v": wv_lo}
                jj = 0
                for mat in ("q", "k", "v"):
                    for d in range(16):
                        buf = jj % 2
                        whb = wsp[:, buf * 2 * 2048:buf * 2 * 2048 + 2048]
                        wlb = wsp[:, buf * 2 * 2048 + 2048:(buf + 1) * 2 * 2048]
                        if jj >= 2:
                            nc.sync.wait_ge(mm.h, mmd[jj - 2])
                        i1 = nc.sync.dma_start(out=whb, in_=wsrc_hi[mat][d])
                        i2 = nc.sync.dma_start(out=wlb, in_=wsrc_lo[mat][d])
                        w_ld.append((wld.issue(i1), wld.issue(i2)))
                        wld.wait(nc.tensor, w_ld[jj][0])
                        wld.wait(nc.tensor, w_ld[jj][1])
                        if jj >= 2:
                            nc.tensor.wait_ge(vo.h, evd[jj - 2])
                        first = True
                        for wsrc, hsrc in ((whb, hhi), (whb, hlo), (wlb, hhi)):
                            for kc in range(16):
                                im = nc.tensor.matmul(
                                    psP[:, buf * T:(buf + 1) * T],
                                    wsrc[:, kc * 128:(kc + 1) * 128],
                                    hsrc[:, kc * T:(kc + 1) * T],
                                    start=first, stop=(wsrc is wlb and kc == 15))
                                first = False
                        mmd[jj] = mm.inc(im, 1)
                        kb.v.wait_ge(mm.h, mmd[jj])
                        if mat == "q":
                            dst_hi, dst_lo = qhi, qlo
                        elif mat == "k":
                            dst_hi, dst_lo = khi, klo
                        else:
                            dst_hi, dst_lo = None, None
                        dsl = slice(d * T, (d + 1) * T)
                        if dst_hi is not None:
                            kb.v.tensor_copy(dst_hi[:, dsl], psP[:, buf * T:(buf + 1) * T])
                            kb.v.tensor_tensor(tmp[:, buf * T:(buf + 1) * T],
                                               psP[:, buf * T:(buf + 1) * T],
                                               dst_hi[:, dsl], ALU.subtract)
                            kb.v.tensor_copy(dst_lo[:, dsl], tmp[:, buf * T:(buf + 1) * T])
                        else:
                            kb.v.tensor_copy(vfm[:, dsl], psP[:, buf * T:(buf + 1) * T])
                        evd[jj] = vo.n
                        jj += 1
                kb.barrier()
                fm_ev = {}
                sp_tok = {}
                for i in range(2 * AH):
                    h, which = i // 2, i % 2
                    src_hi = qhi if which == 0 else khi
                    src_lo = qlo if which == 0 else klo
                    scr = qf_scr if which == 0 else kf_scr
                    f1 = slice((2 * which) * 128, (2 * which) * 128 + 128)
                    f2 = slice((2 * which + 1) * 128, (2 * which + 1) * 128 + 128)
                    b1 = avec[:, 2 * which:2 * which + 1]
                    b2 = avec[:, 2 * which + 1:2 * which + 2]
                    hb = i % 2
                    mb = slice(hb * T, (hb + 1) * T)
                    hsl = slice(h * T, (h + 1) * T)
                    if i >= 2:
                        nc.tensor.wait_ge(vo.h, fm_ev[i - 2])
                    first = True
                    for wsrc, hsrc in ((fmh, src_hi), (fmh, src_lo), (fml, src_hi)):
                        im = nc.tensor.matmul(psM[:, mb], wsrc[:, f1], hsrc[:, hsl],
                                              start=first, stop=(wsrc is fml))
                        first = False
                    first = True
                    for wsrc, hsrc in ((fmh, src_hi), (fmh, src_lo), (fml, src_hi)):
                        im2 = nc.tensor.matmul(psP[:, mb], wsrc[:, f2], hsrc[:, hsl],
                                               start=first, stop=(wsrc is fml))
                        first = False
                    v_mm = mm.inc(im2, 1)
                    kb.v.wait_ge(mm.h, v_mm)
                    if i >= 2:
                        sp_tok[i - 2][0].wait(kb.v, sp_tok[i - 2][1])
                    kb.v.tensor_scalar(tmp[:, mb], psM[:, mb], b1, None, ALU.add)
                    kb.v.tensor_scalar(qf[:, mb], psP[:, mb], b2, None, ALU.add)
                    kb.v.tensor_tensor(qf[:, mb], qf[:, mb], tmp[:, mb], ALU.mult)
                    if which == 0:
                        kb.v.tensor_scalar(qf[:, mb], qf[:, mb], SCALE, None, ALU.mult)
                    fm_ev[i] = vo.n
                    kb.sc.wait_ge(vo.h, fm_ev[i])
                    ds_ = sps[i % 3]
                    sp_tok[i] = (ds_, ds_.issue(kb.sc.dma_start(
                        out=scr[h][:, tok], in_=qf[:, mb])))
                kb.barrier()
                vt_ev = {}
                vsp = {}
                for cc in range(T // ACH):
                    ch = t * (T // ACH) + cc
                    for d in range(16):
                        i = cc * 16 + d
                        bk = i % 2
                        if i >= 2:
                            nc.tensor.wait_ge(vo.h, vt_ev[i - 2])
                        it = nc.tensor.transpose(
                            psP[:, bk * T:bk * T + 64].rearrange("p q -> p q") if False
                            else psP[0:64, bk * T:bk * T + 128],
                            vfm[:, d * T + cc * ACH:d * T + (cc + 1) * ACH],
                            c["id_f32"])
                        mm.inc(it, 1)
                        kb.v.wait_ge(mm.h, mm.n)
                        if i >= 2:
                            vsp[i - 2][0].wait(kb.v, vsp[i - 2][1])
                        kb.v.tensor_copy(vt[0:64, (i % 2) * 128:(i % 2) * 128 + 128],
                                         psP[0:64, bk * T:bk * T + 128])
                        vt_ev[i] = vo.n
                        kb.sc.wait_ge(vo.h, vt_ev[i])
                        ds_ = sps[i % 3]
                        vsp[i] = (ds_, ds_.issue(kb.sc.dma_start(
                            out=vtm_scr[ch, :, d * 128:(d + 1) * 128],
                            in_=vt[0:64, (i % 2) * 128:(i % 2) * 128 + 128])))
                for sp in sps:
                    sp.wait_all(kb.sc)
                kb.barrier()
        a1s.close()

        with ExitStack() as lo_:
            qfc = kb.sb(lo_, "a2_qfc", [128, AH * ACH], F32)
            kfc = kb.sb(lo_, "a2_kfc", [128, AH * ACH], F32)
            vtc = kb.sb(lo_, "a2_vtc", [64, AH * 128], F32)
            kft = kb.sb(lo_, "a2_kft", [64, 2 * 128], F32)
            Ssb = kb.sb(lo_, "a2_Ssb", [64, 2 * ACH], F32)
            otm = kb.sb(lo_, "a2_otm", [64, AH * 128], F32)
            osq = kb.sb(lo_, "a2_osq", [64, AH * 128], F32)
            rst = kb.sb(lo_, "a2_rst", [64, AH], F32)
            onm = kb.sb(lo_, "a2_onm", [64, 2 * 128], F32)
            ofm = kb.sb(lo_, "a2_ofm", [128, AH * T], F32)
            ps = [kb.ps(lo_, f"a2_ps{i}", [128, 512]) for i in range(8)]
            ohi = kb.sb(lo_, "a2_ohi", [128, AH * T], BF16)
            olo = kb.sb(lo_, "a2_olo", [128, AH * T], BF16)
            for ch in range(NAC):
                cg = ch % 8
                grp = ch // 8
                csl = slice(ch * ACH, (ch + 1) * ACH)
                ltoks = []
                ltoks.append(cld.issue(nc.sync.dma_start(
                    out=qfc[:].rearrange("p (h t) -> p h t", h=AH),
                    in_=qf_scr[:, :, csl].rearrange("h p t -> p h t"))))
                ltoks.append(cld.issue(nc.sync.dma_start(
                    out=kfc[:].rearrange("p (h t) -> p h t", h=AH),
                    in_=kf_scr[:, :, csl].rearrange("h p t -> p h t"))))
                ltoks.append(cld.issue(nc.sync.dma_start(
                    out=vtc[0:64, :], in_=vtm_scr[ch])))
                for tk in ltoks:
                    cld.wait(nc.tensor, tk)
                cld.wait(kb.v, ltoks[2])
                kv_ev = [None] * AH
                oc_ev = [None] * AH
                sm_ev = [None] * AH
                vy_mm = [None] * AH
                vkv_mm = [None] * AH

                def a2_front(kb, h):
                    hb = h % 2
                    qfh = qfc[:, h * ACH:(h + 1) * ACH]
                    kfh = kfc[:, h * ACH:(h + 1) * ACH]
                    psS = ps[0 + hb]
                    psT2 = ps[6 + hb]
                    if h >= 2:
                        nc.tensor.wait_ge(vo.h, sm_ev[h - 2])
                    im = nc.tensor.matmul(psS[0:64, 0:ACH], kfh, qfh, start=True, stop=True)
                    v_s = mm.inc(im, 1)
                    if h >= 2:
                        nc.tensor.wait_ge(vo.h, kv_ev[h - 2])
                    it = nc.tensor.transpose(psT2[0:64, 0:128], kfh, c["id_f32"])
                    vt_ = mm.inc(it, 1)
                    kb.v.wait_ge(mm.h, v_s)
                    if h >= 2:
                        kb.v.wait_ge(mm.h, vy_mm[h - 2])
                    kb.v.tensor_tensor(Ssb[0:64, hb * ACH:(hb + 1) * ACH], psS[0:64, 0:ACH],
                                       c["triu"][0:64, 0:64], ALU.mult)
                    kb.v.wait_ge(mm.h, vt_)
                    if h >= 2:
                        kb.v.wait_ge(mm.h, vkv_mm[h - 2])
                    kb.v.tensor_copy(kft[0:64, hb * 128:(hb + 1) * 128], psT2[0:64, 0:128])
                    sm_ev[h] = vo.n

                def a2_back(kb, h):
                    hb = h % 2
                    qfh = qfc[:, h * ACH:(h + 1) * ACH]
                    vth = vtc[0:64, h * 128:(h + 1) * 128]
                    psY = ps[2 + hb]
                    psKV = ps[4 + hb]
                    nc.tensor.wait_ge(vo.h, sm_ev[h])
                    if h >= 2:
                        nc.tensor.wait_ge(so.h, oc_ev[h - 2])
                    im = nc.tensor.matmul(psY[0:64, 0:128], qfh, KV[:, h * 128:(h + 1) * 128],
                                          start=True, stop=False)
                    im = nc.tensor.matmul(psY[0:64, 0:128], Ssb[0:64, hb * ACH:(hb + 1) * ACH],
                                          vth, start=False, stop=True)
                    vy_mm[h] = mm.inc(im, 1)
                    if h >= 2:
                        nc.tensor.wait_ge(vo.h, kv_ev[h - 2])
                    im = nc.tensor.matmul(psKV[:, 0:128], kft[0:64, hb * 128:(hb + 1) * 128],
                                          vth, start=True, stop=True)
                    vkv_mm[h] = mm.inc(im, 1)
                    kb.sc.wait_ge(mm.h, vy_mm[h])
                    kb.sc.activation(otm[0:64, h * 128:(h + 1) * 128], psY[0:64, 0:128],
                                     AF.Copy)
                    oc_ev[h] = so.n
                    kb.v.wait_ge(mm.h, vkv_mm[h])
                    kb.v.tensor_tensor(KV[:, h * 128:(h + 1) * 128],
                                       KV[:, h * 128:(h + 1) * 128],
                                       psKV[:, 0:128], ALU.add)
                    kv_ev[h] = vo.n

                for i in range(AH + 2):
                    if i >= 2:
                        a2_back(kb, i - 2)
                    if i < AH:
                        a2_front(kb, i)
                kb.sc.wait_ge(vo.h, vo.n)
                kb.sc.activation(osq[0:64, :], otm[0:64, :], AF.Square)
                kb.v.wait_ge(so.h, so.n)
                kb.v.tensor_reduce(rst[0:64, :],
                                   osq[0:64, :].rearrange("p (h d) -> p h d", h=AH),
                                   mybir.AxisListType.X, ALU.add)
                kb.v.tensor_scalar(rst[0:64, :], rst[0:64, :], 1.0 / 128, EPS_LN,
                                   ALU.mult, ALU.add)
                kb.sc.wait_ge(vo.h, vo.n)
                kb.sc.activation(rst[0:64, :], rst[0:64, :], AF.Sqrt)
                kb.v.wait_ge(so.h, so.n)
                kb.v.reciprocal(rst[0:64, :], rst[0:64, :])
                tr_ev = {}
                for h in range(AH):
                    hb = h % 2
                    kb.v.tensor_scalar(onm[0:64, hb * 128:(hb + 1) * 128],
                                       otm[0:64, h * 128:(h + 1) * 128],
                                       rst[0:64, h:h + 1], None, ALU.mult)
                    on_ev = vo.n
                    nc.tensor.wait_ge(vo.h, on_ev)
                    if h >= 2:
                        nc.tensor.wait_ge(vo.h, tr_ev[h - 2])
                    it = nc.tensor.transpose(ps[0 + hb][:, 0:64],
                                             onm[0:64, hb * 128:(hb + 1) * 128],
                                             c["id_f32"][0:64, 0:64])
                    v_t = mm.inc(it, 1)
                    kb.v.wait_ge(mm.h, v_t)
                    kb.v.tensor_copy(ofm[:, h * T + cg * ACH:h * T + (cg + 1) * ACH],
                                     ps[0 + hb][:, 0:64])
                    tr_ev[h] = vo.n
                kb.barrier()
                if cg == 7:
                    tok3 = slice(grp * T, (grp + 1) * T)
                    for d in range(AH):
                        dsl = slice(d * T, (d + 1) * T)
                        kb.v.tensor_copy(ohi[:, dsl], ofm[:, dsl])
                        kb.v.tensor_tensor(ofm[:, dsl], ofm[:, dsl], ohi[:, dsl], ALU.subtract)
                        kb.v.tensor_copy(olo[:, dsl], ofm[:, dsl])
                    cast_v = vo.n
                    with ExitStack() as l3:
                        wo = kb.sb(l3, "a3_wo", [128, 4 * 2048], BF16)
                        xd = kb.sb(l3, "a3_xd", [128, 2 * T], F32)
                        ot3 = kb.sb(l3, "a3_ot", [128, 2 * T], F32)
                        wo_ld, dmm, ev3, st3, xl3 = [], [], [], [], []
                        for d in range(DT):
                            buf = d % 2
                            whb = wo[:, buf * 2 * 2048:buf * 2 * 2048 + 2048]
                            wlb = wo[:, buf * 2 * 2048 + 2048:(buf + 1) * 2 * 2048]
                            if d >= 2:
                                nc.sync.wait_ge(mm.h, dmm[d - 2])
                            i1 = nc.sync.dma_start(out=whb, in_=wo_hi[d])
                            i2 = nc.sync.dma_start(out=wlb, in_=wo_lo[d])
                            wo_ld.append((wold.issue(i1), wold.issue(i2)))
                            wold.wait(nc.tensor, wo_ld[d][0])
                            wold.wait(nc.tensor, wo_ld[d][1])
                            if d == 0:
                                nc.tensor.wait_ge(vo.h, cast_v)
                            if d >= 2:
                                nc.tensor.wait_ge(vo.h, ev3[d - 2])
                            first = True
                            for wsrc, hsrc in ((whb, ohi), (whb, olo), (wlb, ohi)):
                                for kc in range(16):
                                    im = nc.tensor.matmul(
                                        ps[4 + buf][:, 0:T],
                                        wsrc[:, kc * 128:(kc + 1) * 128],
                                        hsrc[:, kc * T:(kc + 1) * T],
                                        start=first, stop=(wsrc is wlb and kc == 15))
                                    first = False
                            dmm.append(mm.inc(im, 1))
                            if d >= 2:
                                nc.sync.wait_ge(vo.h, ev3[d - 2])
                            xl3.append(xdld.issue(nc.sync.dma_start(
                                out=xd[:, buf * T:(buf + 1) * T], in_=s_in[d, :, tok3])))
                            kb.v.wait_ge(mm.h, dmm[d])
                            xdld.wait(kb.v, xl3[d])
                            if d >= 2:
                                outst.wait(kb.v, st3[d - 2])
                            kb.v.tensor_tensor(ot3[:, buf * T:(buf + 1) * T],
                                               ps[4 + buf][:, 0:T],
                                               xd[:, buf * T:(buf + 1) * T], ALU.add)
                            ev3.append(vo.n)
                            kb.sc.wait_ge(vo.h, ev3[d])
                            st3.append(outst.issue(kb.sc.dma_start(
                                out=s_out[d, :, tok3],
                                in_=ot3[:, buf * T:(buf + 1) * T])))
                        outst.wait_all(kb.sc)
                    kb.barrier()


def build_program(parts):
    nc = bass.Bass("TRN2", target_bir_lowering=False)
    stack = ExitStack()
    aps = {}

    def din(name, shape, dtype):
        aps[name] = nc.dram_tensor(name, list(shape), dtype, kind="ExternalInput").ap()
        return aps[name]

    def dout(name, shape, dtype):
        aps[name] = nc.dram_tensor(name, list(shape), dtype, kind="ExternalOutput").ap()
        return aps[name]

    def dint(name, shape, dtype):
        aps[name] = nc.dram_tensor(name, list(shape), dtype, kind="Internal").ap()
        return aps[name]

    cf_ap = din("c_f32_in", (128, CF_W), F32)
    cb_ap = din("c_bf_in", (128, CB_W), BF16)

    with stack:
        with nc.Block():
            kb = KB(nc, stack)
            need_lut = any(p.startswith("ghost") or p.startswith("ffn1") or p == "full"
                           for p in parts)
            lut_ap = din("lut_in", (1, 16), F32) if need_lut else None
            emit_consts(kb, cf_ap, cb_ap, lut_ap)

            for p in parts:
                if p == "quant":
                    wgu = din("wgu_t", (64, 128, 2 * 16 * 128), F32)
                    wpr = din("wproj_t", (16, 128, 64 * 128), F32)
                    qgu = dout("qgu", (64, 128, 2 * 16 * 128), BF16)
                    qpr = dout("qproj", (16, 128, 64 * 128), BF16)
                    emit_quant_phase(kb, wgu, qgu, 64, 2 * 16 * 128)
                    emit_quant_phase(kb, wpr, qpr, 16, 64 * 128)
                elif p == "full":
                    wgu = din("wgu_t", (64, 128, 2 * 16 * 128), F32)
                    wpr = din("wproj_t", (16, 128, 64 * 128), F32)
                    mpr = din("mproj_t", (81, 128, 2048), F32)
                    wor = din("wout_t", (16, 128, 4096), F32)
                    nwc = din("nw_t", (128, 32), F32)
                    cvt = din("conv_t", (128, 240), F32)
                    mvc = din("mvec", (128, 4), F32)
                    gidx = din("gidx_t", (GJ, 128, 16 * 128), I8)
                    uidx = din("uidx_t", (GJ, 128, 16 * 128), I8)
                    didx = din("didx_t", (16, 128, 5504), I8)
                    aw_in = {nm: din(nm + "_t", (16, 128, 2048), F32)
                             for nm in ("awq", "awk", "awv", "awo")}
                    nwa = din("anw_t", (128, 1), F32)
                    fm_in = din("afm_t", (1, 128, 512), F32)
                    ave = din("avec", (128, 4), F32)
                    sin = din("x_in", (DT, 128, NTOK), F32)
                    sout = dout("x_out", (DT, 128, NTOK), F32)
                    qgu = dint("qgu", (64, 128, 2 * 16 * 128), BF16)
                    qpr = dint("qproj", (16, 128, 64 * 128), BF16)
                    mph = dint("mp_hi", (81, 128, 2048), BF16)
                    mpl = dint("mp_lo", (81, 128, 2048), BF16)
                    woh = dint("wo_hi", (16, 128, 4096), BF16)
                    wol = dint("wo_lo", (16, 128, 4096), BF16)
                    gh = dint("gh", (GJ, 128, 16 * 128), BF16)
                    gl = dint("gl", (GJ, 128, 16 * 128), BF16)
                    uh = dint("uh", (GJ, 128, 16 * 128), BF16)
                    ul = dint("ul", (GJ, 128, 16 * 128), BF16)
                    dh = dint("dh", (16, 128, 5504), BF16)
                    dl = dint("dl", (16, 128, 5504), BF16)
                    aw = {}
                    for nm in ("awq", "awk", "awv", "awo"):
                        aw[nm + "_hi"] = dint(nm + "_hi", (16, 128, 2048), BF16)
                        aw[nm + "_lo"] = dint(nm + "_lo", (16, 128, 2048), BF16)
                    fmhi = dint("afm_hi", (1, 128, 512), BF16)
                    fmlo = dint("afm_lo", (1, 128, 512), BF16)
                    fhh = dint("fh_hi", (64, 128, T), BF16)
                    fhl = dint("fh_lo", (64, 128, T), BF16)
                    fh1h = dint("fh1_hi", (GJ, 128, T), BF16)
                    fh1l = dint("fh1_lo", (GJ, 128, T), BF16)
                    gsc = dint("m_gate_scr", (32, 128, NTOK), F32)
                    xsc = dint("m_x_scr", (32, 128, NTOK), F32)
                    bsc = dint("m_b_scr", (8, 128, NTOK), F32)
                    csc = dint("m_c_scr", (8, 128, NTOK), F32)
                    ghs = dint("m_gh_scr", (NCH, 128, 32 * MQ), BF16)
                    gls = dint("m_gl_scr", (NCH, 128, 32 * MQ), BF16)
                    qfs = dint("a_qf_scr", (AH, 128, NTOK), F32)
                    kfs = dint("a_kf_scr", (AH, 128, NTOK), F32)
                    vts = dint("a_vtm_scr", (NAC, 64, 2048), F32)
                    sA = dint("s_a", (DT, 128, NTOK), F32)
                    sB = dint("s_b", (DT, 128, NTOK), F32)
                    nw_sb = kb.sb(stack, "w_nw", [128, 32], F32)
                    nws = kb.sb(stack, "a_nw", [128, 1], F32)
                    tn1 = kb.ds("m_cld", 8).issue(nc.sync.dma_start(out=nw_sb[:], in_=nwc[:, :]))
                    tn2 = kb.ds("m_cld", 8).issue(nc.sync.dma_start(out=nws[:], in_=nwa[:, :]))
                    kb.ds("m_cld", 8).wait(kb.v, tn1)
                    kb.ds("m_cld", 8).wait(kb.v, tn2)
                    kb.barrier()
                    emit_quant_phase(kb, wgu, qgu, 64, 2 * 16 * 128)
                    emit_quant_phase(kb, wpr, qpr, 16, 64 * 128)
                    ssc_scope = ExitStack()
                    sbufs = make_split_bufs(kb, ssc_scope)
                    kb.bg_gen = split_gen(kb, [
                        (mpr, mph, mpl, 81, 2048, None, None),
                        (wor, woh, wol, 16, 4096, nw_sb, 128),
                        (aw_in["awq"], aw["awq_hi"], aw["awq_lo"], 16, 2048, None, None),
                        (aw_in["awk"], aw["awk_hi"], aw["awk_lo"], 16, 2048, None, None),
                        (aw_in["awv"], aw["awv_hi"], aw["awv_lo"], 16, 2048, None, None),
                        (aw_in["awo"], aw["awo_hi"], aw["awo_lo"], 16, 2048, nws[:, 0:1], None),
                        (fm_in, fmhi, fmlo, 1, 512, None, None),
                    ], sbufs)
                    emit_ffn0(kb, sin, sA, qgu, qpr, fhh, fhl)
                    kb.pump_all()
                    for dsn in ("sbg_sth", "sbg_stl"):
                        kb.ds(dsn).wait_all(nc.sync)
                    kb.barrier()
                    ssc_scope.close()
                    gsc_scope = ExitStack()
                    gbufs = make_ghost_bufs(kb, gsc_scope)
                    kb.bg_gen = ghost_gen(kb, [
                        (gidx, gh, gl, GJ, 16 * 128),
                        (uidx, uh, ul, GJ, 16 * 128),
                        (didx[:, :, 0:2752], dh[:, :, 0:2752], dl[:, :, 0:2752], 16, 2752),
                        (didx[:, :, 2752:5504], dh[:, :, 2752:5504], dl[:, :, 2752:5504],
                         16, 2752),
                    ], gbufs)
                    emit_mamba(kb, sA, sB, mph, mpl, woh, wol, cvt, mvc,
                               gsc, xsc, bsc, csc, ghs, gls)
                    emit_ffn0(kb, sB, sA, qgu, qpr, fhh, fhl)
                    kb.pump_all()
                    for dsn in ("gbg_sth", "gbg_stl"):
                        kb.ds(dsn).wait_all(nc.sync)
                    kb.barrier()
                    emit_ffn1(kb, sA, sB, gh, gl, uh, ul, dh, dl, fh1h, fh1l)
                    emit_attn(kb, sB, sA, aw["awq_hi"], aw["awq_lo"], aw["awk_hi"],
                              aw["awk_lo"], aw["awv_hi"], aw["awv_lo"], aw["awo_hi"],
                              aw["awo_lo"], fmhi, fmlo, ave, qfs, kfs, vts)
                    emit_ffn1(kb, sA, sout, gh, gl, uh, ul, dh, dl, fh1h, fh1l)
                    gsc_scope.close()
                elif p == "msplit":
                    mpr = din("mproj_t", (81, 128, 2048), F32)
                    mph = dout("mp_hi", (81, 128, 2048), BF16)
                    mpl = dout("mp_lo", (81, 128, 2048), BF16)
                    wor = din("wout_t", (16, 128, 4096), F32)
                    nwc = din("nw_t", (128, 32), F32)
                    woh = dout("wo_hi", (16, 128, 4096), BF16)
                    wol = dout("wo_lo", (16, 128, 4096), BF16)
                    nw_sb = kb.sb(stack, "w_nw", [128, 32], F32)
                    tnw = kb.ds("m_cld", 8).issue(nc.sync.dma_start(out=nw_sb[:], in_=nwc[:, :]))
                    kb.ds("m_cld", 8).wait(kb.v, tnw)
                    kb.barrier()
                    emit_split_phase(kb, mpr, mph, mpl, 81, 2048)
                    emit_split_phase(kb, wor, woh, wol, 16, 4096, scale_col=nw_sb, scale_w=128)
                elif p == "mamba":
                    sin = din("x_in", (DT, 128, NTOK), F32)
                    sout = dout("x_out", (DT, 128, NTOK), F32)
                    mph = din("mp_hi_in", (81, 128, 2048), BF16)
                    mpl = din("mp_lo_in", (81, 128, 2048), BF16)
                    woh = din("wo_hi_in", (16, 128, 4096), BF16)
                    wol = din("wo_lo_in", (16, 128, 4096), BF16)
                    cvt = din("conv_t", (128, 240), F32)
                    mvc = din("mvec", (128, 4), F32)
                    gsc = dint("m_gate_scr", (32, 128, NTOK), F32)
                    xsc = dint("m_x_scr", (32, 128, NTOK), F32)
                    bsc = dint("m_b_scr", (8, 128, NTOK), F32)
                    csc = dint("m_c_scr", (8, 128, NTOK), F32)
                    ghs = dint("m_gh_scr", (NCH, 128, 32 * MQ), BF16)
                    gls = dint("m_gl_scr", (NCH, 128, 32 * MQ), BF16)
                    emit_mamba(kb, sin, sout, mph, mpl, woh, wol, cvt, mvc,
                               gsc, xsc, bsc, csc, ghs, gls)
                elif p == "asplit":
                    names = ("awq", "awk", "awv", "awo")
                    for nm in names:
                        w_in = din(nm + "_t", (16, 128, 2048), F32)
                        whi = dout(nm + "_hi", (16, 128, 2048), BF16)
                        wlo = dout(nm + "_lo", (16, 128, 2048), BF16)
                        if nm == "awo":
                            nwa = din("anw_t", (128, 1), F32)
                            nws = kb.sb(stack, "a_nw", [128, 1], F32)
                            tn = kb.ds("m_cld", 8).issue(nc.sync.dma_start(out=nws[:], in_=nwa[:, :]))
                            kb.ds("m_cld", 8).wait(kb.v, tn)
                            kb.barrier()
                            emit_split_phase(kb, w_in, whi, wlo, 16, 2048, scale_col=nws[:, 0:1])
                        else:
                            emit_split_phase(kb, w_in, whi, wlo, 16, 2048)
                    fm_in = din("afm_t", (1, 128, 512), F32)
                    fmh = dout("afm_hi", (1, 128, 512), BF16)
                    fml = dout("afm_lo", (1, 128, 512), BF16)
                    emit_split_phase(kb, fm_in, fmh, fml, 1, 512)
                elif p == "attn":
                    sin = din("x_in", (DT, 128, NTOK), F32)
                    sout = dout("x_out", (DT, 128, NTOK), F32)
                    wqs = [din(f"{nm}_{sfx}_in", (16, 128, 2048), BF16)
                           for nm in ("awq", "awk", "awv", "awo") for sfx in ("hi", "lo")]
                    fmhi = din("afm_hi_in", (1, 128, 512), BF16)
                    fmlo = din("afm_lo_in", (1, 128, 512), BF16)
                    ave = din("avec", (128, 4), F32)
                    qfs = dint("a_qf_scr", (AH, 128, NTOK), F32)
                    kfs = dint("a_kf_scr", (AH, 128, NTOK), F32)
                    vts = dint("a_vtm_scr", (NAC, 64, 2048), F32)
                    emit_attn(kb, sin, sout, wqs[0], wqs[1], wqs[2], wqs[3],
                              wqs[4], wqs[5], wqs[6], wqs[7], fmhi, fmlo, ave,
                              qfs, kfs, vts)
                elif p == "ghostmat":
                    gidx = din("gidx_t", (GJ, 128, 16 * 128), I8)
                    uidx = din("uidx_t", (GJ, 128, 16 * 128), I8)
                    didx = din("didx_t", (16, 128, 5504), I8)
                    gh = dout("gh", (GJ, 128, 16 * 128), BF16)
                    gl = dout("gl", (GJ, 128, 16 * 128), BF16)
                    uh = dout("uh", (GJ, 128, 16 * 128), BF16)
                    ul = dout("ul", (GJ, 128, 16 * 128), BF16)
                    dh = dout("dh", (16, 128, 5504), BF16)
                    dl = dout("dl", (16, 128, 5504), BF16)
                    gsc_scope = ExitStack()
                    gbufs = make_ghost_bufs(kb, gsc_scope)
                    kb.gpsimd_bg = True
                    emit_ghost_phase(kb, gidx, gh, gl, GJ, 16 * 128, gbufs)
                    emit_ghost_phase(kb, uidx, uh, ul, GJ, 16 * 128, gbufs)
                    emit_ghost_phase(kb, didx[:, :, 0:2752], dh[:, :, 0:2752],
                                     dl[:, :, 0:2752], 16, 2752, gbufs)
                    emit_ghost_phase(kb, didx[:, :, 2752:5504], dh[:, :, 2752:5504],
                                     dl[:, :, 2752:5504], 16, 2752, gbufs)
                    kb.gpsimd_bg = False
                    for dsn in ("gbg_sth", "gbg_stl"):
                        kb.ds(dsn).wait_all(nc.sync)
                    kb.barrier()
                    gsc_scope.close()
                elif p == "ffn1":
                    sin = din("x_in", (DT, 128, NTOK), F32)
                    sout = dout("x_out", (DT, 128, NTOK), F32)
                    gh = din("gh_in", (GJ, 128, 16 * 128), BF16)
                    gl = din("gl_in", (GJ, 128, 16 * 128), BF16)
                    uh = din("uh_in", (GJ, 128, 16 * 128), BF16)
                    ul = din("ul_in", (GJ, 128, 16 * 128), BF16)
                    dh = din("dh_in", (16, 128, 5504), BF16)
                    dl = din("dl_in", (16, 128, 5504), BF16)
                    fhh = dint("fh1_hi", (GJ, 128, T), BF16)
                    fhl = dint("fh1_lo", (GJ, 128, T), BF16)
                    emit_ffn1(kb, sin, sout, gh, gl, uh, ul, dh, dl, fhh, fhl)
                elif p == "ffn0":
                    sin = din("x_in", (DT, 128, NTOK), F32)
                    sout = dout("x_out", (DT, 128, NTOK), F32)
                    qgu = din("qgu_in", (64, 128, 2 * 16 * 128), BF16)
                    qpr = din("qproj_in", (16, 128, 64 * 128), BF16)
                    fhh = dint("fh_hi", (64, 128, T), BF16)
                    fhl = dint("fh_lo", (64, 128, T), BF16)
                    emit_ffn0(kb, sin, sout, qgu, qpr, fhh, fhl)
                else:
                    raise ValueError(p)
    return nc, aps


def host_prep(inputs):
    z = inputs
    h = {}
    wguT = np.ascontiguousarray(z['L0_ffn_w_gate_up'].T)
    h['wgu_t'] = np.concatenate([pret(wguT[:, :8192]), pret(wguT[:, 8192:])], axis=2)
    h['wproj_t'] = pret(np.ascontiguousarray(z['L0_ffn_w_proj'].T))
    Wdt_p = np.zeros((2048, 128), np.float32)
    Wdt_p[:, :64] = z['L0_attn_Wdt']
    Wcat = np.concatenate([z['L0_attn_Wz'], z['L0_attn_Wx'], z['L0_attn_WB'],
                           z['L0_attn_WC'], Wdt_p], axis=1)
    h['mproj_t'] = pret(np.ascontiguousarray(Wcat))
    h['wout_t'] = pret(np.ascontiguousarray(z['L0_attn_Wout']))
    h['nw_t'] = np.ascontiguousarray(z['L0_attn_norm_w'].reshape(32, 128).T)
    conv_t = np.zeros((128, 240), np.float32)
    cw = z['L0_attn_conv_w'].reshape(48, 128, 4)
    cbv = z['L0_attn_conv_b'].reshape(48, 128)
    for ct in range(48):
        conv_t[:, ct * 4:(ct + 1) * 4] = cw[ct]
        conv_t[:, 192 + ct] = cbv[ct]
    h['conv_t'] = conv_t
    mvec = np.zeros((128, 4), np.float32)
    mvec[:64, 0] = z['L0_attn_dt_bias']
    mvec[:64, 1] = z['L0_attn_A_log']
    mvec[:64, 2] = z['L0_attn_D']
    h['mvec'] = mvec
    h['gidx_t'] = pret(np.ascontiguousarray(z['L1_ffn_gate_idx'].T)).astype(np.int8)
    h['uidx_t'] = pret(np.ascontiguousarray(z['L1_ffn_up_idx'].T)).astype(np.int8)
    h['didx_t'] = pret(np.ascontiguousarray(z['L1_ffn_down_idx'].T)).astype(np.int8)
    h['lut_in'] = np.ascontiguousarray(z['L1_ffn_lut'].reshape(1, 16))
    h['awq_t'] = pret(np.ascontiguousarray(z['L1_attn_wq'].T))
    h['awk_t'] = pret(np.ascontiguousarray(z['L1_attn_wk'].T))
    h['awv_t'] = pret(np.ascontiguousarray(z['L1_attn_wv'].T))
    h['awo_t'] = pret(np.ascontiguousarray(z['L1_attn_wo'].T))
    h['anw_t'] = np.ascontiguousarray(z['L1_attn_norm_w'].reshape(128, 1))
    h['afm_t'] = np.concatenate([np.ascontiguousarray(z['L1_attn_fq1_w'].T),
                                 np.ascontiguousarray(z['L1_attn_fq2_w'].T),
                                 np.ascontiguousarray(z['L1_attn_fk1_w'].T),
                                 np.ascontiguousarray(z['L1_attn_fk2_w'].T)], axis=1)[None]
    avec = np.zeros((128, 4), np.float32)
    avec[:, 0] = z['L1_attn_fq1_b']
    avec[:, 1] = z['L1_attn_fq2_b']
    avec[:, 2] = z['L1_attn_fk1_b']
    avec[:, 3] = z['L1_attn_fk2_b']
    h['avec'] = avec
    cf, cb = make_consts()
    h['c_f32_in'] = cf
    h['c_bf_in'] = cb
    return h


def kernel(**inputs):
    h = host_prep(inputs)
    x = inputs['L0_attn_hidden_states'].astype(np.float32, copy=False)
    nc, aps = build_program(['full'])
    in_maps = []
    for core in range(8):
        m = dict(h)
        m['x_in'] = tile_stream(np.ascontiguousarray(x[core // 2]))
        in_maps.append(m)
    res = run_bass_kernel_spmd(nc, in_maps, core_ids=list(range(8)))
    out = np.stack([untile_stream(np.asarray(res.results[2 * s]['x_out']).reshape(DT, 128, NTOK))
                    for s in range(4)], axis=0)
    return out.astype(np.float32, copy=False)
